# version 01
# baseline (speedup 1.0000x reference)
import numpy as np
from contextlib import ExitStack
import concourse.bass as bass
import concourse.mybir as mybir
from concourse.bass_utils import run_bass_kernel_spmd
from concourse.ap import AP

F32 = mybir.dt.float32
BF16 = mybir.dt.bfloat16
I32 = mybir.dt.int32
AF = mybir.ActivationFunctionType
ALU = mybir.AluOpType
AX = mybir.AxisListType

TEN, VEC, SCA, GPS, SYN = "tensor", "vector", "scalar", "gpsimd", "sync"
ENGINES = [TEN, VEC, SCA, GPS, SYN]

NSEQ = 2
T = 2048
D = 1024
DC = D // 128
HF = 4096
HC = HF // 128
NE = 8
TB = 512
NTB = T // TB


class Sem:
    def __init__(self, h):
        self.h = h
        self.n = 0

    def bump(self, k):
        self.n += k
        return self.n


class Prog:
    def __init__(self, nc, stack):
        self.nc = nc
        self.stack = stack
        self.ops = {e: [] for e in ENGINES}

    def sem(self, name):
        return Sem(self.stack.enter_context(self.nc.semaphore(name)))

    def sbuf(self, name, shape, dt):
        return self.stack.enter_context(self.nc.sbuf_tensor(name, shape, dt))

    def psum(self, name, shape, dt):
        return self.stack.enter_context(self.nc.psum_tensor(name, shape, dt))

    def do(self, eng, fn, wait=(), inc=None, inc_k=1):
        if inc is not None:
            inc.bump(inc_k)
        self.ops[eng].append((tuple(wait), fn, (inc.h if inc else None), inc_k))

    def dma(self, eng, out, in_, wait=(), inc=None, **kw):
        assert inc is not None
        self.do(eng, lambda e: e.dma_start(out=out, in_=in_, **kw), wait=wait,
                inc=inc, inc_k=16)

    def build(self):
        with self.nc.Block() as block:
            for eng_name in ENGINES:
                ops = self.ops[eng_name]
                if not ops:
                    continue
                dec = getattr(block, eng_name)

                def section(e, _ops=ops):
                    for waits, fn, inc_h, inc_k in _ops:
                        for (s, v) in waits:
                            e.wait_ge(s.h, v)
                        inst = fn(e)
                        if inc_h is not None:
                            inst.then_inc(inc_h, inc_k)

                dec(section)


def _mk_nc():
    return bass.Bass("TRN2", target_bir_lowering=False, debug=False)


class Common:
    def __init__(self, p, sel8=False, identf=False):
        self.p = p
        s = p.sem("c_init")
        self.sem = s
        self.ident = p.sbuf("c_ident", [128, 128], BF16)
        p.do(GPS, lambda e: e.memset(self.ident[:], 0.0), inc=s)
        p.do(GPS, lambda e: e.affine_select(
            out=self.ident[:], in_=self.ident[:], compare_op=ALU.not_equal,
            fill=1.0, base=0, pattern=[[-1, 128]], channel_multiplier=1),
            wait=[(s, s.n)], inc=s)
        if identf:
            self.identf = p.sbuf("c_identf", [128, 128], F32)
            p.do(GPS, lambda e: e.memset(self.identf[:], 0.0), wait=[(s, s.n)], inc=s)
            p.do(GPS, lambda e: e.affine_select(
                out=self.identf[:], in_=self.identf[:], compare_op=ALU.not_equal,
                fill=1.0, base=0, pattern=[[-1, 128]], channel_multiplier=1),
                wait=[(s, s.n)], inc=s)
        self.ones_b = p.sbuf("c_onesb", [128, 1], BF16)
        self.ones_f = p.sbuf("c_onesf", [128, 1], F32)
        self.ones_row = p.sbuf("c_onesrow", [1, 128], BF16)
        self.ones_rowf = p.sbuf("c_onesrowf", [1, 128], F32)
        self.eps = p.sbuf("c_eps", [1, 1], F32)
        p.do(GPS, lambda e: e.memset(self.ones_b[:], 1.0), wait=[(s, s.n)], inc=s)
        p.do(GPS, lambda e: e.memset(self.ones_f[:], 1.0), wait=[(s, s.n)], inc=s)
        p.do(GPS, lambda e: e.memset(self.ones_row[:], 1.0), wait=[(s, s.n)], inc=s)
        p.do(GPS, lambda e: e.memset(self.ones_rowf[:], 1.0), wait=[(s, s.n)], inc=s)
        p.do(GPS, lambda e: e.memset(self.eps[:], 1e-5), wait=[(s, s.n)], inc=s)
        if sel8:
            self.sel8 = p.sbuf("c_sel8", [8, 8 * 128], BF16)
            p.do(GPS, lambda e: e.memset(self.sel8[:], 0.0), wait=[(s, s.n)], inc=s)
            for ei in range(8):
                p.do(GPS, lambda e, ei=ei: e.affine_select(
                    out=self.sel8[:, ei * 128:(ei + 1) * 128],
                    in_=self.sel8[:, ei * 128:(ei + 1) * 128],
                    compare_op=ALU.not_equal, fill=1.0,
                    base=-ei, pattern=[[0, 128]], channel_multiplier=1),
                    wait=[(s, s.n)], inc=s)
        self.done = s.n


class LnOut:
    def __init__(self, s_ln, s_st):
        self.s_ln = s_ln
        self.s_st = s_st


def emit_ln(p, cm, pfx, x_dram, base_elem, lnb, ps4, xin_sb, xsq, rows, tmp,
            init_waits=(), x_eng=SYN):
    ps_a, ps_b, ps_c, ps_d = ps4
    s_ld2 = [p.sem(pfx + "_ld0"), p.sem(pfx + "_ld1")]
    s_mm = p.sem(pfx + "_mm")
    s_sq = p.sem(pfx + "_sq")
    s_st = p.sem(pfx + "_st")
    s_bc = p.sem(pfx + "_bc")
    s_ln = p.sem(pfx + "_ln")
    sqmm = {}

    def row(i):
        return rows[0:1, (i - 1) * TB:i * TB]

    for tb in range(NTB):
        buf = tb % 2
        xt = xin_sb[:, buf * DC * TB:(buf + 1) * DC * TB]
        s_ld = s_ld2[tb % 2]
        w_ld = list(init_waits) if tb == 0 else ([(s_ln, s_ln.n)] if tb >= 2 else [])
        for j in range(DC):
            src = AP(x_dram, base_elem + (j * 128) * T + tb * TB, [[T, 128], [1, TB]])
            p.dma(x_eng, xt[:, j * TB:(j + 1) * TB], src,
                  wait=(w_ld if j == 0 else []), inc=s_ld)
        ld_done = s_ld.n
        for j in range(DC):
            w = []
            if j == 0:
                w = [(s_ld, ld_done)] + (list(init_waits) if tb == 0 else [])
                if tb > 0:
                    w.append((s_st, s_st.n))
            p.do(TEN, lambda e, j=j, xt=xt: e.matmul(
                ps_a[0:1, :], cm.ones_f[:], xt[:, j * TB:(j + 1) * TB],
                start=(j == 0), stop=(j == DC - 1)), wait=w,
                inc=(s_mm if j == DC - 1 else None))
        for j in range(DC):
            sq = xsq[:, (j % 2) * TB:(j % 2 + 1) * TB]
            w = [(s_ld, ld_done)]
            if j >= 2:
                w.append((s_mm, sqmm[(tb, j - 2)]))
            elif tb > 0:
                w.append((s_mm, sqmm[(tb - 1, DC - 2 + j)]))
            p.do(SCA, lambda e, j=j, xt=xt, sq=sq: e.activation(
                out=sq, in_=xt[:, j * TB:(j + 1) * TB], func=AF.Square),
                wait=w, inc=s_sq)
            w2 = [(s_sq, s_sq.n)]
            if tb > 0 and j == 0:
                w2.append((s_st, s_st.n))
            p.do(TEN, lambda e, j=j, sq=sq: e.matmul(
                ps_b[0:1, :], cm.ones_b[:], sq,
                start=(j == 0), stop=(j == DC - 1)), wait=w2, inc=s_mm)
            sqmm[(tb, j)] = s_mm.n
        mm_done = s_mm.n
        p.do(VEC, lambda e: e.tensor_scalar_mul(row(1), ps_a[0:1, :], 1.0 / D),
             wait=[(s_mm, mm_done)], inc=s_st)
        p.do(VEC, lambda e: e.tensor_mul(row(2), row(1), row(1)),
             wait=[(s_st, s_st.n)], inc=s_st)
        p.do(VEC, lambda e: e.scalar_tensor_tensor(
            out=row(2), in0=ps_b[0:1, :], scalar=1.0 / D, in1=row(2),
            op0=ALU.mult, op1=ALU.subtract), wait=[(s_st, s_st.n)], inc=s_st)
        p.do(SCA, lambda e: e.activation(out=row(2), in_=row(2), func=AF.Sqrt,
                                         bias=cm.eps[:]), wait=[(s_st, s_st.n)], inc=s_st)
        p.do(VEC, lambda e: e.reciprocal(row(2), row(2)),
             wait=[(s_st, s_st.n)], inc=s_st)
        p.do(VEC, lambda e: e.scalar_tensor_tensor(
            out=row(1), in0=row(1), scalar=-1.0, in1=row(2),
            op0=ALU.mult, op1=ALU.mult), wait=[(s_st, s_st.n)], inc=s_st)
        st_done = s_st.n
        w_bc = [(s_st, st_done)]
        if tb > 0:
            w_bc.append((s_ln, s_ln.n))
        elif init_waits:
            w_bc += list(init_waits)
        p.do(TEN, lambda e: e.matmul(ps_c[:, :], cm.ones_rowf[:], row(2),
                                     start=True, stop=True), wait=w_bc, inc=s_bc)
        p.do(TEN, lambda e: e.matmul(ps_d[:, :], cm.ones_rowf[:], row(1),
                                     start=True, stop=True), inc=s_bc)
        bc_done = s_bc.n
        for j in range(DC):
            p.do(VEC, lambda e, j=j, xt=xt: e.tensor_mul(
                tmp[:], xt[:, j * TB:(j + 1) * TB], ps_c[:, :]),
                wait=[(s_bc, bc_done), (s_st, s_st.n)], inc=s_st)
            p.do(VEC, lambda e, j=j, tb=tb: e.tensor_add(
                lnb[:, j * T + tb * TB: j * T + (tb + 1) * TB], tmp[:], ps_d[:, :]),
                wait=[(s_st, s_st.n)], inc=(s_ln if j == DC - 1 else s_st))
    return LnOut(s_ln, s_st)


def build_phase_d():
    nc = _mk_nc()
    x3 = nc.dram_tensor("x3", [NSEQ * D, T], F32, kind="ExternalInput")
    wg = nc.dram_tensor("wg", [D, NE], F32, kind="ExternalInput")
    bg = nc.dram_tensor("bg", [NE], F32, kind="ExternalInput")
    w1 = nc.dram_tensor("w1", [NE * D, HF], F32, kind="ExternalInput")
    b1 = nc.dram_tensor("b1", [NE * 128, HC], F32, kind="ExternalInput")
    w2 = nc.dram_tensor("w2", [NE * HF, D], F32, kind="ExternalInput")
    b2 = nc.dram_tensor("b2", [NE, D], F32, kind="ExternalInput")
    x4 = nc.dram_tensor("x4", [NSEQ * D, T], F32, kind="ExternalOutput")
    with ExitStack() as stack:
        p = Prog(nc, stack)
        emit_l1ffn(p, x3, wg, bg, w1, b1, w2, b2, x4)
        p.build()
    return nc


def emit_l1ffn(p, x3, wg, bg, w1, b1, w2, b2, x4):
    cm = Common(p, sel8=True, identf=True)
    s_ld = p.sem("d_ld")
    s_w1 = p.sem("d_w1s")
    s_w2 = p.sem("d_w2s")
    s_b1 = p.sem("d_b1s")
    s_x3 = p.sem("d_x3s")
    s_mm = p.sem("d_mm")
    s_evh = p.sem("d_evh")
    s_evv = p.sem("d_evv")
    s_g = p.sem("d_g")
    s_st = p.sem("d_st")

    lnb = p.sbuf("d_ln", [128, DC * T], BF16)
    w1sb = p.sbuf("d_w1", [128, DC * HF], BF16)
    w2sb = p.sbuf("d_w2", [128, HC * D], BF16)
    hsb = p.sbuf("d_h", [128, HC * TB], BF16)
    xin = hsb[:].bitcast(F32)
    arena = p.sbuf("d_arena", [128, 2 * TB], F32)
    rows = arena[0:1, 0:2 * TB]
    wgsb = p.sbuf("d_wg", [128, DC * NE], BF16)
    bgsb = p.sbuf("d_bg", [NE, 1], F32)
    b1sb = p.sbuf("d_b1", [128, HC], F32)
    b2sb = p.sbuf("d_b2", [NE, D], BF16)
    gtok = p.sbuf("d_gtok", [128, 4 * NE], F32)
    gsc = p.sbuf("d_gsc", [128, 4 * NE], F32)
    gkeep = p.sbuf("d_gkeep", [128, 4 * NE], F32)
    gred = p.sbuf("d_gred", [128, 4], F32)
    gbf = p.sbuf("d_gbf", [128, 4 * NE], BF16)
    garena = p.sbuf("d_garena", [128, T], BF16)
    gwT = garena[0:NE, 0:T]
    xsq = garena[:, 0:2 * TB]
    lntmp = garena[:, 2 * TB:4 * TB].bitcast(F32)
    gb_sb = arena[:, 0:TB]
    tmp2 = arena[:, TB:2 * TB]
    lg_sb = arena[0:NE, 0:TB]

    ps_a = p.psum("d_psa", [128, TB], F32)
    ps_b = p.psum("d_psb", [128, TB], F32)
    ps_c = p.psum("d_psc", [128, TB], F32)
    ps_d = p.psum("d_psd", [128, TB], F32)
    ps_m = [p.psum("d_pm0", [128, TB], F32), p.psum("d_pm1", [128, TB], F32)]
    ps_t = p.psum("d_pt", [128, 1024], BF16)
    ps_tf = p.psum("d_ptf", [128, TB], F32)

    for j in range(DC):
        p.dma(GPS, wgsb[:, j * NE:(j + 1) * NE],
              AP(wg, j * 128 * NE, [[NE, 128], [1, NE]]),
              wait=([(cm.sem, cm.done)] if j == 0 else []), inc=s_ld)
    p.dma(GPS, bgsb[:], AP(bg, 0, [[1, NE], [1, 1]]), inc=s_ld)
    p.dma(GPS, b2sb[:], b2.ap(), inc=s_ld)
    static_ld = s_ld.n

    def load_w1(ei, wait):
        for k in range(DC):
            p.dma(GPS, w1sb[:, k * HF:(k + 1) * HF],
                  AP(w1, (ei * D + k * 128) * HF, [[HF, 128], [1, HF]]),
                  wait=(wait if k == 0 else []), inc=s_w1)

    def load_w2(ei, wait):
        for k in range(HC):
            p.dma(GPS, w2sb[:, k * D:(k + 1) * D],
                  AP(w2, (ei * HF + k * 128) * D, [[D, 128], [1, D]]),
                  wait=(wait if k == 0 else []), inc=s_w2)

    g3 = gtok[:].rearrange("p (q e) -> p q e", e=NE)
    m3 = gsc[:].rearrange("p (q e) -> p q e", e=NE)
    k3 = gkeep[:].rearrange("p (q e) -> p q e", e=NE)
    redb = gred[:].unsqueeze(2).broadcast_to([128, 4, NE])

    load_w1(0, [(s_ld, static_ld)])
    load_w2(0, [])
    w1_ready = s_w1.n
    w2_ready = s_w2.n
    prev_ln = None
    for seq in range(NSEQ):
        iw = [(cm.sem, cm.done)]
        if seq > 0:
            iw = [(s_evh, s_evh.n), (s_evv, s_evv.n), (s_mm, s_mm.n),
                  (s_g, s_g.n), (s_st, s_st.n),
                  (prev_ln.s_ln, NTB), (prev_ln.s_st, prev_ln.s_st.n)]
        ln = emit_ln(p, cm, f"dln{seq}", x3, seq * D * T, lnb,
                     (ps_a, ps_b, ps_c, ps_d), xin, xsq, rows, lntmp,
                     init_waits=iw)
        prev_ln = ln
        ln_done = [(ln.s_ln, ln.s_ln.n), (ln.s_st, ln.s_st.n)]

        for tb in range(NTB):
            for j in range(DC):
                w = []
                if j == 0:
                    w = list(ln_done) + [(s_ld, static_ld)]
                    if tb > 0 or seq > 0:
                        w.append((s_g, s_g.n))
                p.do(TEN, lambda e, j=j, tb=tb: e.matmul(
                    ps_a[0:NE, :], wgsb[:, j * NE:(j + 1) * NE],
                    lnb[:, j * T + tb * TB: j * T + (tb + 1) * TB],
                    start=(j == 0), stop=(j == DC - 1)), wait=w,
                    inc=(s_mm if j == DC - 1 else None))
            lg_mm = s_mm.n
            p.do(VEC, lambda e: e.tensor_scalar_add(lg_sb[:], ps_a[0:NE, :], bgsb[:]),
                 wait=[(s_mm, lg_mm)], inc=s_g)
            lg_done = s_g.n
            for q in range(4):
                p.do(TEN, lambda e, q=q: e.matmul(
                    ps_tf[:, q * NE:(q + 1) * NE], lg_sb[:, q * 128:(q + 1) * 128],
                    cm.identf[0:NE, 0:NE], is_transpose=True, start=(q == 0), stop=(q == 3)),
                    wait=([(s_g, lg_done)] if q == 0 else []),
                    inc=(s_mm if q == 3 else None))
            tp_done = s_mm.n
            for q in range(4):
                gq = gtok[:, q * NE:(q + 1) * NE]
                kq = gkeep[:, q * NE:(q + 1) * NE]
                mq = gsc[:, q * NE:(q + 1) * NE]
                c1 = gred[:, 0:1]
                c2 = gred[:, 1:2]
                pq = ps_tf[:, q * NE:(q + 1) * NE]
                p.do(VEC, lambda e, pq=pq, c1=c1: e.reduce_max(
                    c1, pq, axis=AX.X, negate=True),
                    wait=[(s_mm, tp_done), (s_g, s_g.n)], inc=s_g)
                p.do(SCA, lambda e, pq=pq, gq=gq, c1=c1: e.activation(
                    out=gq, in_=pq, func=AF.Exp, bias=c1),
                    wait=[(s_g, s_g.n)], inc=s_g)
                p.do(VEC, lambda e, gq=gq, c1=c1: e.reduce_max(c1, gq, axis=AX.X),
                     wait=[(s_g, s_g.n)], inc=s_g)
                p.do(VEC, lambda e, gq=gq, kq=kq, c1=c1: e.tensor_single_scalar(
                    out=kq, in_=gq, scalar=c1, op=ALU.is_ge),
                    wait=[(s_g, s_g.n)], inc=s_g)
                p.do(VEC, lambda e, kq=kq, mq=mq: e.tensor_scalar(
                    out=mq, in0=kq, scalar1=-1.0, scalar2=1.0,
                    op0=ALU.mult, op1=ALU.add), wait=[(s_g, s_g.n)], inc=s_g)
                p.do(VEC, lambda e, mq=mq, gq=gq: e.tensor_mul(mq, mq, gq),
                     wait=[(s_g, s_g.n)], inc=s_g)
                p.do(VEC, lambda e, mq=mq, c2=c2: e.reduce_max(c2, mq, axis=AX.X),
                     wait=[(s_g, s_g.n)], inc=s_g)
                p.do(VEC, lambda e, mq=mq, c2=c2: e.tensor_single_scalar(
                    out=mq, in_=mq, scalar=c2, op=ALU.is_ge),
                    wait=[(s_g, s_g.n)], inc=s_g)
                p.do(VEC, lambda e, kq=kq, mq=mq: e.tensor_add(kq, kq, mq),
                     wait=[(s_g, s_g.n)], inc=s_g)
                p.do(VEC, lambda e, gq=gq, kq=kq: e.tensor_mul(gq, gq, kq),
                     wait=[(s_g, s_g.n)], inc=s_g)
                p.do(VEC, lambda e, gq=gq, c1=c1: e.reduce_sum(c1, gq, axis=AX.X),
                     wait=[(s_g, s_g.n)], inc=s_g)
                p.do(VEC, lambda e, c1=c1: e.reciprocal(c1, c1),
                     wait=[(s_g, s_g.n)], inc=s_g)
                p.do(VEC, lambda e, gq=gq, c1=c1: e.tensor_scalar_mul(gq, gq, c1),
                     wait=[(s_g, s_g.n)], inc=s_g)
            p.do(VEC, lambda e: e.tensor_copy(out=gbf[:], in_=gtok[:]),
                 wait=[(s_g, s_g.n)], inc=s_g)
            gcast = s_g.n
            for q in range(4):
                p.do(TEN, lambda e, q=q: e.matmul(
                    ps_t[0:NE, q * 128:(q + 1) * 128], gbf[:, q * NE:(q + 1) * NE],
                    cm.ident[:], is_transpose=True, start=(q == 0), stop=(q == 3)),
                    wait=([(s_g, gcast)] if q == 0 else []),
                    inc=(s_mm if q == 3 else None))
            p.do(VEC, lambda e, tb=tb: e.tensor_copy(
                out=gwT[:, tb * TB:(tb + 1) * TB], in_=ps_t[0:NE, 0:512]),
                wait=[(s_mm, s_mm.n)], inc=s_g)
        gates_done = s_g.n

        for tb in range(NTB):
            for m in range(DC):
                buf = gb_sb if m % 2 == 0 else tmp2
                p.dma(SYN, buf,
                      AP(x3, (seq * D + m * 128) * T + tb * TB, [[T, 128], [1, TB]]),
                      wait=[(s_st, s_st.n), (s_g, gates_done)], inc=s_x3)
                x3_ld = s_x3.n
                p.do(TEN, lambda e, m=m, tb=tb: e.matmul(
                    ps_d[:, :], b2sb[:, m * 128:(m + 1) * 128],
                    gwT[:, tb * TB:(tb + 1) * TB], start=True, stop=True),
                    wait=[(s_g, gates_done), (s_evv, s_evv.n)], inc=s_mm)
                p.do(VEC, lambda e, buf=buf: e.tensor_add(buf, buf, ps_d[:, :]),
                     wait=[(s_mm, s_mm.n), (s_x3, x3_ld), (s_evv, s_evv.n)],
                     inc=s_evv)
                p.dma(GPS, AP(x4, (seq * D + m * 128) * T + tb * TB,
                              [[T, 128], [1, TB]]), buf,
                      wait=[(s_evv, s_evv.n)], inc=s_st)
        init_done = s_st.n

        evh = {0: 0, 1: 0}
        evv = {0: 0, 1: 0}
        for ei in range(NE):
            p.dma(GPS, b1sb[:], AP(b1, ei * 128 * HC, [[HC, 128], [1, HC]]),
                  wait=[(s_evh, s_evh.n)], inc=s_b1)
            b1_ready = s_b1.n
            w1_next = None
            for tb in range(NTB):
                for m in range(HC):
                    bank = m % 2
                    ps = ps_m[bank]
                    for k in range(DC):
                        w = []
                        if k == 0:
                            w.append((s_evh, evh[bank]))
                            w.append((s_evv, evv[bank]))
                            if m == 0 and tb == 0:
                                w += [(s_w1, w1_ready), (s_g, gates_done)]
                        p.do(TEN, lambda e, m=m, k=k, tb=tb, ps=ps: e.matmul(
                            ps[:, :],
                            w1sb[:, k * HF + m * 128: k * HF + (m + 1) * 128],
                            lnb[:, k * T + tb * TB: k * T + (tb + 1) * TB],
                            start=(k == 0), stop=(k == DC - 1)), wait=w,
                            inc=(s_mm if k == DC - 1 else None))
                    p.do(SCA, lambda e, m=m, ps=ps: e.activation(
                        out=hsb[:, m * TB:(m + 1) * TB], in_=ps[:, :],
                        func=AF.Gelu, bias=b1sb[:, m:m + 1]),
                        wait=[(s_mm, s_mm.n), (s_b1, b1_ready)], inc=s_evh)
                    evh[bank] = s_evh.n
                h_done = s_evh.n
                if tb == NTB - 1 and ei < NE - 1:
                    load_w1(ei + 1, [(s_mm, s_mm.n)])
                    w1_next = s_w1.n
                p.do(TEN, lambda e, ei=ei, tb=tb: e.matmul(
                    ps_c[:, :], cm.sel8[0:NE, ei * 128:(ei + 1) * 128],
                    gwT[:, tb * TB:(tb + 1) * TB], start=True, stop=True),
                    wait=[(s_evh, s_evh.n)], inc=s_mm)
                p.do(SCA, lambda e: e.copy(out=gb_sb[:], in_=ps_c[:, :]),
                     wait=[(s_mm, s_mm.n)], inc=s_evh)
                gb_done = s_evh.n
                for m in range(DC):
                    bank = m % 2
                    ps = ps_m[bank]
                    for k in range(HC):
                        w = []
                        if k == 0:
                            w.append((s_evh, evh[bank]))
                            w.append((s_evv, evv[bank]))
                            if m == 0:
                                w.append((s_evh, h_done))
                                if tb == 0:
                                    w.append((s_w2, w2_ready))
                        p.do(TEN, lambda e, m=m, k=k, tb=tb, ps=ps: e.matmul(
                            ps[:, :],
                            w2sb[:, k * D + m * 128: k * D + (m + 1) * 128],
                            hsb[:, k * TB:(k + 1) * TB],
                            start=(k == 0), stop=(k == HC - 1)), wait=w,
                            inc=(s_mm if k == HC - 1 else None))
                    t2 = tmp2
                    p.do(VEC, lambda e, ps=ps, t2=t2: e.tensor_mul(
                        t2, ps[:, :], gb_sb[:]),
                        wait=[(s_mm, s_mm.n), (s_evh, gb_done), (s_st, s_st.n),
                              (s_evv, s_evv.n)],
                        inc=s_evv)
                    evv[bank] = s_evv.n
                    wr = s_evv.n
                    dst = AP(x4, (seq * D + m * 128) * T + tb * TB, [[T, 128], [1, TB]])
                    p.dma(GPS, dst, t2, wait=[(s_evv, wr), (s_st, s_st.n)],
                          inc=s_st, accum_op=ALU.add)
            if ei < NE - 1:
                load_w2(ei + 1, [(s_mm, s_mm.n)])
                w1_ready = w1_next
                w2_ready = s_w2.n
        if seq < NSEQ - 1:
            load_w1(0, [(s_mm, s_mm.n)])
            w1_ready = s_w1.n
            load_w2(0, [])
            w2_ready = s_w2.n
    p.do(SYN, lambda e: e.nop(), wait=[(s_st, s_st.n)])




GELU_C = 0.7978845608028654
GELU_A = 0.044715


def build_phase_b():
    nc = _mk_nc()
    x1 = nc.dram_tensor("x1", [NSEQ * D, T], F32, kind="ExternalInput")
    alpha = nc.dram_tensor("alpha", [1], F32, kind="ExternalInput")
    wgt = nc.dram_tensor("wgt", [D, NE], F32, kind="ExternalInput")
    bgt = nc.dram_tensor("bgt", [NE], F32, kind="ExternalInput")
    w1m = nc.dram_tensor("w1m", [D, HF], F32, kind="ExternalInput")
    b1m = nc.dram_tensor("b1m", [128, HC], F32, kind="ExternalInput")
    w2m = nc.dram_tensor("w2m", [HF, D], F32, kind="ExternalInput")
    b2m = nc.dram_tensor("b2m", [128, DC], F32, kind="ExternalInput")
    w1e = nc.dram_tensor("w1e", [NE * D, HF], F32, kind="ExternalInput")
    b1e = nc.dram_tensor("b1e", [NE * 128, HC], F32, kind="ExternalInput")
    w2e = nc.dram_tensor("w2e", [NE * HF, D], F32, kind="ExternalInput")
    b2e = nc.dram_tensor("b2e", [NE * 128, DC], F32, kind="ExternalInput")
    x2 = nc.dram_tensor("x2", [NSEQ * D, T], F32, kind="ExternalOutput")
    with ExitStack() as stack:
        p = Prog(nc, stack)
        emit_l0ffn(p, x1, alpha, wgt, bgt, w1m, b1m, w2m, b2m,
                   w1e, b1e, w2e, b2e, x2)
        p.build()
    return nc


def emit_l0ffn(p, x1, alpha, wgt, bgt, w1m, b1m, w2m, b2m,
               w1e, b1e, w2e, b2e, x2):
    cm = Common(p, sel8=False, identf=True)
    s_ld = p.sem("b_ld")
    s_w1 = p.sem("b_w1s")
    s_w2 = p.sem("b_w2s")
    s_x1 = p.sem("b_x1s")
    s_mm = p.sem("b_mm")
    s_gl = [p.sem("b_gl0"), p.sem("b_gl1")]
    s_g = p.sem("b_g")
    s_st = p.sem("b_st")
    s_ix = p.sem("b_ix")

    lnb = p.sbuf("b_ln", [128, DC * T], BF16)
    w1sb = p.sbuf("b_w1", [128, DC * HF], BF16)
    w2sb = p.sbuf("b_w2", [128, HC * D], BF16)
    hsb = p.sbuf("b_h", [128, HC * TB], BF16)
    xin = hsb[:].bitcast(F32)
    garena = p.sbuf("b_garena", [128, T], BF16)
    xsq = garena[:, 0:2 * TB]
    lntmp = garena[:, 2 * TB:4 * TB].bitcast(F32)
    _gs = garena[:, 0:2 * TB].bitcast(F32)
    _gw = garena[:, 2 * TB:4 * TB].bitcast(F32)
    _gt = p.sbuf("b_gt", [128, TB], F32)
    gs = [_gs, _gs]
    gw = [_gw, _gw]
    gt = [_gt, _gt]
    arena = p.sbuf("b_arena", [128, 2 * TB], F32)
    rows = arena[0:1, 0:2 * TB]
    xacc = arena[:, 0:TB]
    tmp2 = arena[:, TB:2 * TB]
    wgsb = p.sbuf("b_wg", [128, DC * NE], F32)
    bgsb = p.sbuf("b_bg", [NE, 1], F32)
    b1msb = p.sbuf("b_b1m", [128, HC], F32)
    b1esb = p.sbuf("b_b1e", [128, HC], F32)
    mu = p.sbuf("b_mu", [128, DC], F32)
    grow = p.sbuf("b_grow", [1, 32], F32)
    lgcol = p.sbuf("b_lgcol", [NE, 1], F32)
    cols = p.sbuf("b_cols", [128, 8], F32)
    btot = p.sbuf("b_btot", [128, DC], F32)
    bxc = p.sbuf("b_bxc", [128, 2 * DC], F32)
    idxs = [p.sbuf("b_idxa", [128, DC + HC + 1], I32),
            p.sbuf("b_idxb", [128, DC + HC + 1], I32)]
    iotaw = p.sbuf("b_iotaw", [128, DC + HC + 1], F32)
    iota8 = p.sbuf("b_iota8", [1, NE], F32)
    idxf = p.sbuf("b_idxf", [128, HC], F32)
    ii1 = p.sbuf("b_ii1", [128, HC], I32)

    ps_a = p.psum("b_psa", [128, TB], F32)
    ps_b = p.psum("b_psb", [128, TB], F32)
    ps_c = p.psum("b_psc", [128, TB], F32)
    ps_d = p.psum("b_psd", [128, TB], F32)
    ps_m = [p.psum("b_pm0", [128, TB], F32), p.psum("b_pm1", [128, TB], F32)]
    ps_tf = p.psum("b_ptf", [128, TB], F32)

    s0 = cm.sem
    p.do(GPS, lambda e: e.iota(ii1[:, 0:DC], pattern=[[128, DC]], base=0,
                               channel_multiplier=1), wait=[(s0, s0.n)], inc=s0)
    p.do(VEC, lambda e: e.tensor_copy(out=iotaw[:, 0:DC], in_=ii1[:, 0:DC]),
         wait=[(s0, s0.n)], inc=s0)
    p.do(GPS, lambda e: e.iota(ii1[:, 0:HC], pattern=[[128, HC]], base=0,
                               channel_multiplier=1), wait=[(s0, s0.n)], inc=s0)
    p.do(VEC, lambda e: e.tensor_copy(out=iotaw[:, DC:DC + HC], in_=ii1[:, 0:HC]),
         wait=[(s0, s0.n)], inc=s0)
    p.do(GPS, lambda e: e.iota(ii1[:, 0:1], pattern=[[0, 1]], base=0,
                               channel_multiplier=1), wait=[(s0, s0.n)], inc=s0)
    p.do(VEC, lambda e: e.tensor_copy(out=iotaw[:, DC + HC:DC + HC + 1],
                                      in_=ii1[:, 0:1]), wait=[(s0, s0.n)], inc=s0)
    p.do(GPS, lambda e: e.iota(ii1[0:1, 0:NE], pattern=[[1, NE]], base=0,
                               channel_multiplier=0), wait=[(s0, s0.n)], inc=s0)
    p.do(VEC, lambda e: e.tensor_copy(out=iota8[:], in_=ii1[0:1, 0:NE]),
         wait=[(s0, s0.n)], inc=s0)
    cdone = s0.n

    for j in range(DC):
        p.dma(GPS, wgsb[:, j * NE:(j + 1) * NE],
              AP(wgt, j * 128 * NE, [[NE, 128], [1, NE]]),
              wait=([(s0, cdone)] if j == 0 else []), inc=s_ld)
    p.dma(GPS, bgsb[:], AP(bgt, 0, [[1, NE], [1, 1]]), inc=s_ld)
    p.dma(GPS, bxc[:, 0:DC], b2m.ap(), inc=s_ld)
    p.dma(GPS, cols[:, 0:1], AP(alpha, 0, [[0, 128], [1, 1]]), inc=s_ld)
    p.dma(GPS, b1msb[:], b1m.ap(), inc=s_ld)
    static_ld = s_ld.n
    p.do(VEC, lambda e: e.tensor_scalar(out=cols[:, 0:1], in0=cols[:, 0:1],
                                        scalar1=0.1, scalar2=1.0, op0=ALU.max,
                                        op1=ALU.min), wait=[(s_ld, static_ld)], inc=s_g)
    p.do(VEC, lambda e: e.tensor_scalar(out=cols[:, 1:2], in0=cols[:, 0:1],
                                        scalar1=-1.0, scalar2=1.0, op0=ALU.mult,
                                        op1=ALU.add), wait=[(s_g, s_g.n)], inc=s_g)

    def load_w1m(wait):
        for k in range(DC):
            p.dma(GPS, w1sb[:, k * HF:(k + 1) * HF],
                  AP(w1m, (k * 128) * HF, [[HF, 128], [1, HF]]),
                  wait=(wait if k == 0 else []), inc=s_w1)

    def load_w2m(wait):
        for k in range(HC):
            p.dma(GPS, w2sb[:, k * D:(k + 1) * D],
                  AP(w2m, (k * 128) * D, [[D, 128], [1, D]]),
                  wait=(wait if k == 0 else []), inc=s_w2)

    def gather_w1e(ix, wait):
        for k in range(DC):
            p.do(GPS, lambda e, k=k, ix=ix: e.indirect_dma_start(
                out=w1sb[:, k * HF:(k + 1) * HF], out_offset=None, in_=w1e.ap(),
                in_offset=bass.IndirectOffsetOnAxis(idxs[ix][:, k:k + 1], 0)),
                wait=(wait if k == 0 else []), inc=s_w1, inc_k=16)

    def gather_w2e(ix, wait):
        for k in range(HC):
            p.do(GPS, lambda e, k=k, ix=ix: e.indirect_dma_start(
                out=w2sb[:, k * D:(k + 1) * D], out_offset=None, in_=w2e.ap(),
                in_offset=bass.IndirectOffsetOnAxis(idxs[ix][:, DC + k:DC + k + 1], 0)),
                wait=(wait if k == 0 else []), inc=s_w2, inc_k=16)

    def emit_idx(i_colidx, ix):
        ic = cols[:, i_colidx:i_colidx + 1]
        sc = cols[:, 6:7]
        for (stride, lo, n) in [(float(D), 0, DC), (float(HF), DC, HC),
                                (128.0, DC + HC, 1)]:
            p.do(VEC, lambda e, stride=stride: e.tensor_scalar_mul(sc, ic, stride),
                 wait=[(s_g, s_g.n)], inc=s_g)
            p.do(VEC, lambda e, lo=lo, n=n: e.tensor_single_scalar(
                out=idxf[:, 0:n], in_=iotaw[:, lo:lo + n], scalar=sc, op=ALU.add),
                wait=[(s_g, s_g.n)], inc=s_g)
            p.do(VEC, lambda e, ix=ix, lo=lo, n=n: e.tensor_copy(
                out=idxs[ix][:, lo:lo + n], in_=idxf[:, 0:n]),
                wait=[(s_g, s_g.n)], inc=s_g)
        return s_g.n

    def ffn_pass(b1col_t, scale_col, first_wait, w1rdy, w2rdy, seq, after_h_last):
        evb = {0: 0, 1: 0}
        for tb in range(NTB):
            for m in range(HC):
                bank = m % 2
                ps = ps_m[bank]
                sg = s_gl[bank]
                sq, gww, gtt = gs[bank], gw[bank], gt[bank]
                bc = b1col_t[:, m:m + 1]
                for k in range(DC):
                    w = []
                    if k == 0:
                        w = [(sg, evb[bank])]
                        if m == 0 and tb == 0:
                            w = w + first_wait + [(s_w1, w1rdy)]
                    p.do(TEN, lambda e, m=m, k=k, tb=tb, ps=ps: e.matmul(
                        ps[:, :], w1sb[:, k * HF + m * 128: k * HF + (m + 1) * 128],
                        lnb[:, k * T + tb * TB: k * T + (tb + 1) * TB],
                        start=(k == 0), stop=(k == DC - 1)), wait=w,
                        inc=(s_mm if k == DC - 1 else None))
                bw = lambda: [(s_gl[0], s_gl[0].n), (s_gl[1], s_gl[1].n)]
                p.do(SCA, lambda e, ps=ps, sq=sq, bc=bc: e.activation(
                    out=sq[:, 0:TB], in_=ps[:, :], func=AF.Square, bias=bc),
                    wait=[(s_mm, s_mm.n)] + bw(), inc=sg)
                p.do(GPS, lambda e, sq=sq, gww=gww: e.tensor_scalar(
                    out=gww[:], in0=sq[:, 0:TB], scalar1=2 * GELU_C * GELU_A,
                    scalar2=2 * GELU_C, op0=ALU.mult, op1=ALU.add),
                    wait=bw(), inc=sg)
                p.do(VEC, lambda e, ps=ps, gww=gww, gtt=gtt, bc=bc:
                     e.scalar_tensor_tensor(
                         out=gtt[:], in0=ps[:, :], scalar=bc, in1=gww[:],
                         op0=ALU.add, op1=ALU.mult), wait=bw(), inc=sg)
                p.do(SCA, lambda e, gtt=gtt: e.activation(
                    out=gtt[:], in_=gtt[:], func=AF.Sigmoid),
                    wait=bw(), inc=sg)
                p.do(VEC, lambda e, ps=ps, gtt=gtt, m=m, bc=bc:
                     e.scalar_tensor_tensor(
                         out=hsb[:, m * TB:(m + 1) * TB], in0=ps[:, :], scalar=bc,
                         in1=gtt[:], op0=ALU.add, op1=ALU.mult),
                     wait=bw(), inc=sg)
                evb[bank] = sg.n
            h_done = [(s_gl[0], s_gl[0].n), (s_gl[1], s_gl[1].n)]
            if tb == NTB - 1 and after_h_last is not None:
                after_h_last()
            for m in range(DC):
                bank = m % 2
                ps = ps_m[bank]
                sg = s_gl[bank]
                for k in range(HC):
                    w = []
                    if k == 0:
                        w = [(sg, evb[bank])]
                        if m == 0:
                            w = w + h_done
                            if tb == 0:
                                w.append((s_w2, w2rdy))
                    p.do(TEN, lambda e, m=m, k=k, tb=tb, ps=ps: e.matmul(
                        ps[:, :], w2sb[:, k * D + m * 128: k * D + (m + 1) * 128],
                        hsb[:, k * TB:(k + 1) * TB],
                        start=(k == 0), stop=(k == HC - 1)), wait=w,
                        inc=(s_mm if k == DC - 1 and False else
                             (s_mm if k == HC - 1 else None)))
                p.do(VEC, lambda e, ps=ps: e.tensor_single_scalar(
                    out=tmp2, in_=ps[:, :], scalar=scale_col, op=ALU.mult),
                    wait=[(s_mm, s_mm.n), (s_st, s_st.n),
                          (s_gl[0], s_gl[0].n), (s_gl[1], s_gl[1].n)], inc=sg)
                evb[bank] = sg.n
                dst = AP(x2, (seq * D + m * 128) * T + tb * TB, [[T, 128], [1, TB]])
                p.dma(GPS, dst, tmp2, wait=[(sg, sg.n), (s_st, s_st.n)],
                      inc=s_st, accum_op=ALU.add)

    load_w1m([(s_ld, static_ld)])
    load_w2m([])
    w1_ready, w2_ready = s_w1.n, s_w2.n
    prev_ln = None
    for seq in range(NSEQ):
        iw = [(cm.sem, cm.done)]
        if seq > 0:
            iw = [(s_gl[0], s_gl[0].n), (s_gl[1], s_gl[1].n), (s_mm, s_mm.n),
                  (s_g, s_g.n), (s_st, s_st.n),
                  (prev_ln.s_ln, NTB), (prev_ln.s_st, prev_ln.s_st.n)]
        ln = emit_ln(p, cm, f"bln{seq}", x1, seq * D * T, lnb,
                     (ps_a, ps_b, ps_c, ps_d), xin, xsq, rows, lntmp,
                     init_waits=iw)
        prev_ln = ln

        for j in range(DC):
            p.do(VEC, lambda e, j=j: e.reduce_sum(
                mu[:, j:j + 1], lnb[:, j * T:(j + 1) * T], axis=AX.X),
                wait=[(ln.s_ln, NTB), (s_g, s_g.n)], inc=s_g)
        mu_done = s_g.n
        for j in range(DC):
            p.do(TEN, lambda e, j=j: e.matmul(
                ps_tf[0:NE, 0:1], wgsb[:, j * NE:(j + 1) * NE], mu[:, j:j + 1],
                start=(j == 0), stop=(j == DC - 1)),
                wait=([(s_g, mu_done), (s_ld, static_ld)] if j == 0 else []),
                inc=(s_mm if j == DC - 1 else None))
        p.do(VEC, lambda e: e.tensor_scalar(
            out=lgcol[:], in0=ps_tf[0:NE, 0:1], scalar1=1.0 / T,
            scalar2=bgsb[:], op0=ALU.mult, op1=ALU.add),
            wait=[(s_mm, s_mm.n), (s_g, s_g.n)], inc=s_g)
        p.do(TEN, lambda e: e.matmul(
            ps_d[0:1, 0:NE], lgcol[:], cm.identf[0:NE, 0:NE],
            is_transpose=True, start=True, stop=True),
            wait=[(s_g, s_g.n)], inc=s_mm)
        lg = grow[0:1, 0:NE]
        kq = grow[0:1, 16:16 + NE]
        mq = grow[0:1, 24:24 + NE]
        c1 = grow[0:1, 8:9]
        c2 = grow[0:1, 9:10]
        p.do(VEC, lambda e: e.tensor_copy(out=lg, in_=ps_d[0:1, 0:NE]),
             wait=[(s_mm, s_mm.n), (s_g, s_g.n)], inc=s_g)
        p.do(VEC, lambda e: e.reduce_max(c1, lg, axis=AX.X, negate=True),
             wait=[(s_g, s_g.n)], inc=s_g)
        p.do(SCA, lambda e: e.activation(out=lg, in_=lg, func=AF.Exp, bias=c1),
             wait=[(s_g, s_g.n)], inc=s_g)
        p.do(VEC, lambda e: e.reduce_max(c1, lg, axis=AX.X), wait=[(s_g, s_g.n)],
             inc=s_g)
        p.do(VEC, lambda e: e.tensor_single_scalar(out=kq, in_=lg, scalar=c1,
                                                   op=ALU.is_ge),
             wait=[(s_g, s_g.n)], inc=s_g)
        p.do(VEC, lambda e: e.tensor_scalar(out=mq, in0=kq, scalar1=-1.0,
                                            scalar2=1.0, op0=ALU.mult, op1=ALU.add),
             wait=[(s_g, s_g.n)], inc=s_g)
        p.do(VEC, lambda e: e.tensor_mul(mq, mq, lg), wait=[(s_g, s_g.n)], inc=s_g)
        p.do(VEC, lambda e: e.reduce_max(c2, mq, axis=AX.X), wait=[(s_g, s_g.n)],
             inc=s_g)
        p.do(VEC, lambda e: e.tensor_single_scalar(out=mq, in_=mq, scalar=c2,
                                                   op=ALU.is_ge),
             wait=[(s_g, s_g.n)], inc=s_g)
        p.do(VEC, lambda e: e.tensor_add(grow[0:1, 10:11], c1, c2),
             wait=[(s_g, s_g.n)], inc=s_g)
        p.do(VEC, lambda e: e.reciprocal(grow[0:1, 10:11], grow[0:1, 10:11]),
             wait=[(s_g, s_g.n)], inc=s_g)
        p.do(VEC, lambda e: e.tensor_mul(grow[0:1, 11:12], c1, grow[0:1, 10:11]),
             wait=[(s_g, s_g.n)], inc=s_g)
        p.do(VEC, lambda e: e.tensor_mul(grow[0:1, 12:13], c2, grow[0:1, 10:11]),
             wait=[(s_g, s_g.n)], inc=s_g)
        p.do(VEC, lambda e: e.tensor_mul(kq, kq, iota8[0:1, :]),
             wait=[(s_g, s_g.n)], inc=s_g)
        p.do(VEC, lambda e: e.reduce_sum(grow[0:1, 13:14], kq, axis=AX.X),
             wait=[(s_g, s_g.n)], inc=s_g)
        p.do(VEC, lambda e: e.tensor_mul(mq, mq, iota8[0:1, :]),
             wait=[(s_g, s_g.n)], inc=s_g)
        p.do(VEC, lambda e: e.reduce_sum(grow[0:1, 14:15], mq, axis=AX.X),
             wait=[(s_g, s_g.n)], inc=s_g)
        top2_done = s_g.n
        for n, (src, dstc) in enumerate([(11, 2), (12, 3), (13, 4), (14, 5)]):
            p.do(TEN, lambda e, src=src, dstc=dstc, n=n: e.matmul(
                ps_tf[:, dstc:dstc + 1], cm.ones_rowf[:], grow[0:1, src:src + 1],
                start=(n == 0), stop=(n == 3)),
                wait=([(s_g, top2_done)] if n == 0 else []),
                inc=(s_mm if n == 3 else None))
        p.do(VEC, lambda e: e.tensor_copy(out=cols[:, 2:6], in_=ps_tf[:, 2:6]),
             wait=[(s_mm, s_mm.n), (s_g, s_g.n)], inc=s_g)
        p.do(VEC, lambda e: e.tensor_mul(cols[:, 2:3], cols[:, 2:3], cols[:, 1:2]),
             wait=[(s_g, s_g.n)], inc=s_g)
        p.do(VEC, lambda e: e.tensor_mul(cols[:, 3:4], cols[:, 3:4], cols[:, 1:2]),
             wait=[(s_g, s_g.n)], inc=s_g)
        cols_done = s_g.n

        idx1_done = emit_idx(4, 0)
        idx2_done = emit_idx(5, 1)

        p.do(GPS, lambda e: e.indirect_dma_start(
            out=bxc[:, DC:2 * DC], out_offset=None, in_=b2e.ap(),
            in_offset=bass.IndirectOffsetOnAxis(
                idxs[0][:, DC + HC:DC + HC + 1], 0)),
            wait=[(s_g, idx2_done)], inc=s_ix, inc_k=16)
        p.do(VEC, lambda e: e.tensor_single_scalar(
            out=btot[:], in_=bxc[:, 0:DC], scalar=cols[:, 0:1], op=ALU.mult),
            wait=[(s_ld, static_ld), (s_g, cols_done)], inc=s_g)
        p.do(VEC, lambda e: e.tensor_single_scalar(
            out=bxc[:, DC:2 * DC], in_=bxc[:, DC:2 * DC], scalar=cols[:, 2:3],
            op=ALU.mult), wait=[(s_ix, s_ix.n), (s_g, s_g.n)], inc=s_g)
        p.do(VEC, lambda e: e.tensor_add(btot[:], btot[:], bxc[:, DC:2 * DC]),
             wait=[(s_g, s_g.n)], inc=s_g)
        b2e1_used = s_g.n
        p.do(GPS, lambda e: e.indirect_dma_start(
            out=bxc[:, DC:2 * DC], out_offset=None, in_=b2e.ap(),
            in_offset=bass.IndirectOffsetOnAxis(
                idxs[1][:, DC + HC:DC + HC + 1], 0)),
            wait=[(s_g, b2e1_used)], inc=s_ix, inc_k=16)
        p.do(VEC, lambda e: e.tensor_single_scalar(
            out=bxc[:, DC:2 * DC], in_=bxc[:, DC:2 * DC], scalar=cols[:, 3:4],
            op=ALU.mult), wait=[(s_ix, s_ix.n), (s_g, s_g.n)], inc=s_g)
        p.do(VEC, lambda e: e.tensor_add(btot[:], btot[:], bxc[:, DC:2 * DC]),
             wait=[(s_g, s_g.n)], inc=s_g)
        btot_done = s_g.n

        for tb in range(NTB):
            for m in range(DC):
                p.dma(SYN, xacc,
                      AP(x1, (seq * D + m * 128) * T + tb * TB, [[T, 128], [1, TB]]),
                      wait=[(s_st, s_st.n), (s_g, btot_done)], inc=s_x1)
                p.do(VEC, lambda e, m=m: e.tensor_single_scalar(
                    out=xacc, in_=xacc, scalar=btot[:, m:m + 1], op=ALU.add),
                    wait=[(s_x1, s_x1.n), (s_g, s_g.n)], inc=s_g)
                p.dma(GPS, AP(x2, (seq * D + m * 128) * T + tb * TB,
                              [[T, 128], [1, TB]]), xacc,
                      wait=[(s_g, s_g.n)], inc=s_st)

        def after_main_h():
            gather_w1e(0, [(s_mm, s_mm.n), (s_g, idx1_done)])
        ffn_pass(b1msb, cols[:, 0:1], [(ln.s_ln, NTB), (s_g, btot_done)],
                 w1_ready, w2_ready, seq, after_main_h)
        w1_ready = s_w1.n
        gather_w2e(0, [(s_mm, s_mm.n)])
        w2_ready = s_w2.n
        p.do(GPS, lambda e: e.indirect_dma_start(
            out=b1esb[:], out_offset=None, in_=b1e.ap(),
            in_offset=bass.IndirectOffsetOnAxis(
                idxs[0][:, DC + HC:DC + HC + 1], 0)),
            wait=[(s_gl[0], s_gl[0].n), (s_gl[1], s_gl[1].n)], inc=s_ix, inc_k=16)
        b1e_rd = s_ix.n

        def after_e1_h():
            gather_w1e(1, [(s_mm, s_mm.n), (s_g, idx2_done)])
        ffn_pass(b1esb, cols[:, 2:3], [(s_ix, b1e_rd)],
                 w1_ready, w2_ready, seq, after_e1_h)
        w1_ready = s_w1.n
        gather_w2e(1, [(s_mm, s_mm.n)])
        w2_ready = s_w2.n
        p.do(GPS, lambda e: e.indirect_dma_start(
            out=b1esb[:], out_offset=None, in_=b1e.ap(),
            in_offset=bass.IndirectOffsetOnAxis(
                idxs[1][:, DC + HC:DC + HC + 1], 0)),
            wait=[(s_gl[0], s_gl[0].n), (s_gl[1], s_gl[1].n)], inc=s_ix, inc_k=16)
        b1e_rd = s_ix.n

        def after_e2_h():
            if seq < NSEQ - 1:
                load_w1m([(s_mm, s_mm.n)])
        ffn_pass(b1esb, cols[:, 3:4], [(s_ix, b1e_rd)],
                 w1_ready, w2_ready, seq, after_e2_h)
        w1_ready = s_w1.n
        if seq < NSEQ - 1:
            load_w2m([(s_mm, s_mm.n)])
            w2_ready = s_w2.n
    p.do(SYN, lambda e: e.nop(), wait=[(s_st, s_st.n)])




def build_phase_c():
    nc = _mk_nc()
    x2 = nc.dram_tensor("x2", [NSEQ * D, T], F32, kind="ExternalInput")
    wqt = nc.dram_tensor("wqt", [D, D], F32, kind="ExternalInput")
    wkt = nc.dram_tensor("wkt", [D, D], F32, kind="ExternalInput")
    wvt = nc.dram_tensor("wvt", [D, D], F32, kind="ExternalInput")
    wot = nc.dram_tensor("wot", [D, D], F32, kind="ExternalInput")
    bq = nc.dram_tensor("bq", [128, DC], F32, kind="ExternalInput")
    bk = nc.dram_tensor("bk", [128, DC], F32, kind="ExternalInput")
    bv = nc.dram_tensor("bv", [128, DC], F32, kind="ExternalInput")
    bo = nc.dram_tensor("bo", [128, DC], F32, kind="ExternalInput")
    srcb = nc.dram_tensor("srcb", [3], I32, kind="ExternalInput")
    tgtb = nc.dram_tensor("tgtb", [3], I32, kind="ExternalInput")
    x3 = nc.dram_tensor("x3", [NSEQ * D, T], F32, kind="ExternalOutput")
    mxd = nc.dram_tensor("mxd", [16 * 128, T], BF16,
                         kind="ExternalOutput" if DEBUG_C else "Internal")
    dbg = None
    if DEBUG_C:
        dbg = {n: nc.dram_tensor("dbg_" + n, [128, DC * T], BF16,
                                 kind="ExternalOutput") for n in
               ["qv", "kv", "vv", "yfm"]}
        dbg["sc"] = nc.dram_tensor("dbg_sc", [128, T], F32, kind="ExternalOutput")
        dbg["att"] = nc.dram_tensor("dbg_att", [128, T], BF16, kind="ExternalOutput")
        dbg["attT"] = nc.dram_tensor("dbg_attT", [128, T], BF16, kind="ExternalOutput")
        dbg["vT"] = nc.dram_tensor("dbg_vT", [128, 1024], BF16, kind="ExternalOutput")
        dbg["mx"] = nc.dram_tensor("dbg_mx", [128, 2 * T], BF16, kind="ExternalOutput")
    with ExitStack() as stack:
        p = Prog(nc, stack)
        emit_l1attn(p, x2, wqt, wkt, wvt, wot, bq, bk, bv, bo, srcb, tgtb,
                    x3, mxd, dbg)
        p.build()
    return nc


DEBUG_C = False
DBG_HI = (0, 0)


def emit_l1attn(p, x2, wqt, wkt, wvt, wot, bq, bk, bv, bo, srcb, tgtb, x3, mxd,
                dbg=None):
    cm = Common(p, sel8=False, identf=True)
    s_ld = p.sem("c_ld")
    s_wb = [p.sem("c_ws0"), p.sem("c_ws1")]
    s_x2 = p.sem("c_x2s")
    s_mxp = [p.sem("c_mxs0"), p.sem("c_mxs1")]
    s_mm = p.sem("c_mm")
    s_ev = p.sem("c_ev")
    s_vv = p.sem("c_vv")
    s_g = p.sem("c_g")
    s_st = p.sem("c_st")

    lnb = p.sbuf("c_ln", [128, DC * T], BF16)
    y_fm = lnb
    qv = p.sbuf("c_qv", [128, DC * T], BF16)
    xin = qv[:].bitcast(F32)
    kv = p.sbuf("c_kv", [128, DC * T], BF16)
    vv = p.sbuf("c_vv", [128, DC * T], BF16)
    wful = [p.sbuf("c_w0", [128, DC * 1024], BF16),
            p.sbuf("c_w1", [128, DC * 1024], BF16)]
    mx_sb = p.sbuf("c_mx", [128, 2 * T], BF16)
    garena = p.sbuf("c_garena", [128, T], BF16)
    xsq = garena[:, 0:2 * TB]
    lntmp = garena[:, 2 * TB:4 * TB].bitcast(F32)
    arena = p.sbuf("c_arena", [128, 2 * TB], F32)
    rows = arena[0:1, 0:2 * TB]
    xacc = arena[:, 0:TB]
    sc_sb = p.sbuf("c_sc", [128, T], F32)
    ri = sc_sb[0:32, 0:1024].bitcast(BF16)
    emat = p.sbuf("c_emat", [32, T], BF16)
    att = p.sbuf("c_att", [128, T], BF16)
    attT = p.sbuf("c_attT", [128, T], BF16)
    vT = p.sbuf("c_vT", [128, 16 * 64], BF16)
    stage = p.sbuf("c_stage", [128, 2 * TB], BF16)
    bcols = p.sbuf("c_bcols", [128, 4 * DC], F32)
    zcol = p.sbuf("c_zcol", [128, 2], F32)
    swap64 = p.sbuf("c_swap", [128, 128], BF16)
    bmsb = p.sbuf("c_bm", [32, 32], F32)
    bmb = p.sbuf("c_bmb", [32, 32], BF16)
    srow = p.sbuf("c_srow", [1, 8], F32)
    iota32 = p.sbuf("c_iota32", [32, 1], F32)
    eqs = p.sbuf("c_eqs", [32, 2], F32)
    eqrow = p.sbuf("c_eqrow", [1, 64], F32)
    asb = p.sbuf("c_asb", [128, 32], BF16)
    atsb = p.sbuf("c_atsb", [32, 128], BF16)
    ii = p.sbuf("c_ii", [32, 1], I32)

    ps_a = p.psum("c_psa", [128, TB], F32)
    ps_b = p.psum("c_psb", [128, TB], F32)
    ps_c = p.psum("c_psc", [128, TB], F32)
    ps_d = p.psum("c_psd", [128, TB], F32)
    ps_t = [p.psum("c_pt0", [128, 1024], BF16), p.psum("c_pt1", [128, 1024], BF16)]
    ps_y = p.psum("c_py", [128, TB], F32)
    ps_sw = p.psum("c_psw", [128, TB], F32)

    s0 = cm.sem
    p.do(GPS, lambda e: e.iota(ii[:], pattern=[[0, 1]], base=0,
                               channel_multiplier=1), wait=[(s0, s0.n)], inc=s0)
    p.do(VEC, lambda e: e.tensor_copy(out=iota32[:], in_=ii[:]),
         wait=[(s0, s0.n)], inc=s0)
    p.do(GPS, lambda e: e.memset(swap64[:], 0.0), wait=[(s0, s0.n)], inc=s0)
    p.do(GPS, lambda e: e.affine_select(
        out=swap64[:], in_=swap64[:], compare_op=ALU.not_equal, fill=1.0,
        base=64, pattern=[[-1, 128]], channel_multiplier=1),
        wait=[(s0, s0.n)], inc=s0)
    p.do(GPS, lambda e: e.affine_select(
        out=swap64[:], in_=swap64[:], compare_op=ALU.not_equal, fill=1.0,
        base=-64, pattern=[[-1, 128]], channel_multiplier=1),
        wait=[(s0, s0.n)], inc=s0)
    p.do(GPS, lambda e: e.memset(sc_sb[0:32, :], 0.0), wait=[(s0, s0.n)], inc=s0)
    for i in range(16):
        p.do(GPS, lambda e, i=i: e.affine_select(
            out=ri[:, i * 128:i * 128 + 64], in_=ri[:, i * 128:i * 128 + 64],
            compare_op=ALU.not_equal, fill=1.0, base=-2 * i,
            pattern=[[0, 64]], channel_multiplier=1), wait=[(s0, s0.n)], inc=s0)
        p.do(GPS, lambda e, i=i: e.affine_select(
            out=ri[:, i * 128 + 64:i * 128 + 128],
            in_=ri[:, i * 128 + 64:i * 128 + 128],
            compare_op=ALU.not_equal, fill=1.0, base=-(2 * i + 1),
            pattern=[[0, 64]], channel_multiplier=1), wait=[(s0, s0.n)], inc=s0)
    p.do(GPS, lambda e: e.memset(emat[:], 1.0), wait=[(s0, s0.n)], inc=s0)
    p.do(GPS, lambda e: e.affine_select(
        out=emat[:], in_=emat[:], compare_op=ALU.is_ge, fill=0.0,
        base=0, pattern=[[1, T]], channel_multiplier=-64),
        wait=[(s0, s0.n)], inc=s0)
    p.do(GPS, lambda e: e.affine_select(
        out=emat[:], in_=emat[:], compare_op=ALU.is_ge, fill=0.0,
        base=63, pattern=[[-1, T]], channel_multiplier=64),
        wait=[(s0, s0.n)], inc=s0)
    p.do(GPS, lambda e: e.memset(bmsb[:], 1.0), wait=[(s0, s0.n)], inc=s0)
    p.do(GPS, lambda e: e.affine_select(
        out=bmsb[:], in_=bmsb[:], compare_op=ALU.is_ge, fill=0.0, base=1,
        pattern=[[-1, 32]], channel_multiplier=1), wait=[(s0, s0.n)], inc=s0)
    p.do(GPS, lambda e: e.affine_select(
        out=bmsb[:], in_=bmsb[:], compare_op=ALU.is_ge, fill=0.0, base=1,
        pattern=[[1, 32]], channel_multiplier=-1), wait=[(s0, s0.n)], inc=s0)
    p.do(GPS, lambda e: e.affine_select(
        out=bmsb[:], in_=bmsb[:], compare_op=ALU.not_equal, fill=1.0, base=0,
        pattern=[[1, 32]], channel_multiplier=0), wait=[(s0, s0.n)], inc=s0)
    p.do(GPS, lambda e: e.affine_select(
        out=bmsb[:], in_=bmsb[:], compare_op=ALU.not_equal, fill=1.0, base=-31,
        pattern=[[1, 32]], channel_multiplier=0), wait=[(s0, s0.n)], inc=s0)
    cdone = s0.n
    p.do(GPS, lambda e: e.memset(srow[:].bitcast(I32), 0), wait=[(s0, cdone)],
         inc=s0)
    p.dma(GPS, srow[0:1, 0:3].bitcast(I32), AP(srcb, 0, [[3, 1], [1, 3]]),
          wait=[(s0, s0.n)], inc=s_ld)
    p.dma(GPS, srow[0:1, 4:7].bitcast(I32), AP(tgtb, 0, [[3, 1], [1, 3]]), inc=s_ld)
    sT_ld = s_ld.n
    p.do(VEC, lambda e: e.tensor_copy(out=srow[:], in_=srow[:].bitcast(I32)),
         wait=[(s_ld, sT_ld), (s0, cdone)], inc=s_g)
    for k in range(3):
        for n_, col in enumerate([k, 4 + k]):
            p.do(TEN, lambda e, col=col: e.matmul(
                ps_d[0:32, 0:1], cm.ones_rowf[0:1, 0:32], srow[0:1, col:col + 1],
                start=True, stop=True), wait=[(s_g, s_g.n)], inc=s_mm)
            p.do(VEC, lambda e, n_=n_: e.tensor_tensor(
                out=eqs[:, n_:n_ + 1], in0=iota32[:], in1=ps_d[0:32, 0:1],
                op=ALU.is_equal), wait=[(s_mm, s_mm.n), (s_g, s_g.n)], inc=s_g)
        for cc, doff in [(0, 0), (1, 32)]:
            p.do(TEN, lambda e, cc=cc: e.matmul(
                ps_d[0:1, 0:32], eqs[:, cc:cc + 1], cm.identf[0:32, 0:32],
                is_transpose=True, start=True, stop=True),
                wait=[(s_g, s_g.n)], inc=s_mm)
            p.do(VEC, lambda e, doff=doff: e.tensor_copy(
                out=eqrow[0:1, doff:doff + 32], in_=ps_d[0:1, 0:32]),
                wait=[(s_mm, s_mm.n), (s_g, s_g.n)], inc=s_g)
        p.do(TEN, lambda e, k=k: e.matmul(
            ps_c[0:32, 0:32], eqrow[0:1, 0:32], eqrow[0:1, 32:64],
            start=(k == 0), stop=(k == 2)), wait=[(s_g, s_g.n)],
            inc=(s_mm if k == 2 else None))
    p.do(VEC, lambda e: e.tensor_add(bmsb[:], bmsb[:], ps_c[0:32, 0:32]),
         wait=[(s_mm, s_mm.n), (s_g, s_g.n)], inc=s_g)
    p.do(VEC, lambda e: e.tensor_scalar(out=bmsb[:], in0=bmsb[:], scalar1=0.99,
                                        scalar2=None, op0=ALU.is_ge),
         wait=[(s_g, s_g.n)], inc=s_g)
    p.do(VEC, lambda e: e.tensor_scalar(out=bmb[:], in0=bmsb[:], scalar1=1e9,
                                        scalar2=-1e9, op0=ALU.mult, op1=ALU.add),
         wait=[(s_g, s_g.n)], inc=s_g)
    for i in range(16):
        p.do(TEN, lambda e, i=i: e.matmul(
            ps_d[:, 0:32], ri[:, i * 128:(i + 1) * 128], bmb[:],
            start=True, stop=True), wait=[(s_g, s_g.n)], inc=s_mm)
        p.do(VEC, lambda e: e.tensor_copy(out=asb[:], in_=ps_d[:, 0:32]),
             wait=[(s_mm, s_mm.n), (s_g, s_g.n), (s_st, s_st.n)], inc=s_g)
        p.do(TEN, lambda e: e.matmul(
            ps_t[0][0:32, 0:128], asb[:], cm.ident[:], is_transpose=True,
            start=True, stop=True), wait=[(s_g, s_g.n)], inc=s_mm)
        p.do(VEC, lambda e: e.tensor_copy(out=atsb[:], in_=ps_t[0][0:32, 0:128]),
             wait=[(s_mm, s_mm.n), (s_g, s_g.n)], inc=s_g)
        for q in range(4):
            ps = ps_a if q % 2 == 0 else ps_b
            p.do(TEN, lambda e, q=q, ps=ps: e.matmul(
                ps[:, :], atsb[:], emat[:, q * TB:(q + 1) * TB],
                start=True, stop=True), wait=[(s_g, s_g.n)], inc=s_mm)
            p.do(VEC, lambda e, q=q, ps=ps: e.tensor_copy(
                out=att[:, q * TB:(q + 1) * TB], in_=ps[:, :]),
                wait=[(s_mm, s_mm.n), (s_g, s_g.n)], inc=s_g)
        p.dma(GPS, AP(mxd, i * 128 * T, [[T, 128], [1, T]]), att[:, 0:T],
              wait=[(s_g, s_g.n)], inc=s_st)
    mx_built = s_st.n

    def load_w(buf, dram, wait):
        for k in range(DC):
            p.dma(GPS, wful[buf][:, k * 1024:(k + 1) * 1024],
                  AP(dram, (k * 128) * D, [[D, 128], [1, D]]),
                  wait=(wait if k == 0 else []), inc=s_wb[buf])

    prev_ln = None
    proj_ev = {0: 0, 1: 0}
    for seq in range(NSEQ):
        iw = [(cm.sem, cm.done), (s_st, s_st.n), (s_g, s_g.n)]
        if seq > 0:
            iw += [(s_ev, s_ev.n), (s_vv, s_vv.n), (s_mm, s_mm.n),
                   (prev_ln.s_ln, NTB), (prev_ln.s_st, prev_ln.s_st.n)]
        ln = emit_ln(p, cm, f"cln{seq}", x2, seq * D * T, lnb,
                     (ps_a, ps_b, ps_c, ps_d), xin, xsq, rows, lntmp,
                     init_waits=iw)
        prev_ln = ln

        if seq == 0:
            p.dma(GPS, bcols[:, 0:DC], bq.ap(), inc=s_ld)
            p.dma(GPS, bcols[:, DC:2 * DC], bk.ap(), inc=s_ld)
            p.dma(GPS, bcols[:, 2 * DC:3 * DC], bv.ap(), inc=s_ld)
            p.dma(GPS, bcols[:, 3 * DC:4 * DC], bo.ap(), inc=s_ld)
        bcols_ld = s_ld.n
        load_w(0, wqt, [(s_mm, s_mm.n), (s_vv, s_vv.n), (s_st, mx_built)])
        w_ready = s_wb[0].n

        evq = {0: 0, 1: 0}
        _next_w = [wkt, wvt, wot]
        for wi, (dst, bofs, scale) in enumerate([(qv, 0, 1.0), (kv, DC, 0.125),
                                                 (vv, 2 * DC, 1.0)]):
            wbuf = wful[wi % 2]
            load_w((wi + 1) % 2, _next_w[wi], [(s_mm, s_mm.n)])
            w_next = s_wb[(wi + 1) % 2].n
            for tb in range(NTB):
                for m in range(DC):
                    bank = m % 2
                    ps = [ps_a, ps_b][bank]
                    for k in range(DC):
                        w = []
                        if k == 0:
                            w = [(s_vv, evq[bank]), (s_ev, s_ev.n)]
                            if m == 0 and tb == 0:
                                w += [(ln.s_ln, NTB),
                                      (s_wb[wi % 2], w_ready),
                                      (s_ld, bcols_ld)]
                        p.do(TEN, lambda e, m=m, k=k, tb=tb, ps=ps, wbuf=wbuf:
                             e.matmul(
                                 ps[:, :],
                                 wbuf[:, k * 1024 + m * 128: k * 1024 + (m + 1) * 128],
                                 lnb[:, k * T + tb * TB: k * T + (tb + 1) * TB],
                                 start=(k == 0), stop=(k == DC - 1)), wait=w,
                             inc=(s_mm if k == DC - 1 else None))
                    mmn = s_mm.n
                    stg = stage[:, bank * TB:(bank + 1) * TB]
                    p.do(SCA, lambda e, ps=ps, stg=stg, bofs=bofs, m=m:
                         e.activation(out=stg, in_=ps[:, :], func=AF.Identity,
                                      bias=bcols[:, bofs + m:bofs + m + 1]),
                         wait=[(s_mm, mmn), (s_ev, s_ev.n), (s_vv, evq[bank])],
                         inc=s_ev)
                    stg_done = s_ev.n
                    p.do(TEN, lambda e, stg=stg: e.matmul(
                        ps_sw[:, :], swap64[:], stg, start=True, stop=True),
                        wait=[(s_ev, stg_done), (s_vv, s_vv.n)], inc=s_mm)
                    swn = s_mm.n
                    for half in range(2):
                        for par in range(2):
                            b = 2 * m + half
                            shifted = (half != par)

                            def mkop(e, stg=stg, half=half, par=par, b=b, tb=tb,
                                     dst=dst, shifted=shifted, scale=scale):
                                if shifted:
                                    src = ps_sw[64 * par:64 * par + 64, :]
                                else:
                                    src = stg[64 * par:64 * par + 64, :]
                                src = src.rearrange("p (a l) -> p a l",
                                                    l=128)[:, par::2, :]
                                out = dst[64 * par:64 * par + 64, :].rearrange(
                                    "p (c a l s) -> p c a l s", c=4, a=2,
                                    l=128)[:, tb, :, :, b]
                                if scale == 1.0:
                                    return e.tensor_copy(out=out, in_=src)
                                return e.tensor_scalar_mul(out, src, scale)
                            p.do(VEC, mkop,
                                 wait=[(s_ev, stg_done), (s_mm, swn),
                                       (s_vv, s_vv.n)], inc=s_vv)
                    evq[bank] = s_vv.n
            w_ready = w_next
        qkv_done = [(s_vv, s_vv.n), (s_mm, s_mm.n)]

        mxld = {0: 0, 1: 0}
        for h in range(16):
            hb = 64 * (h % 2)
            hcol = (h // 2) * 2048
            for c in range(16):
                bank2 = (c // 4) % 2
                p.do(TEN, lambda e, c=c, hb=hb, hcol=hcol, bank2=bank2: e.matmul(
                    ps_t[bank2][0:128, (c % 4) * 64:(c % 4) * 64 + 64],
                    vv[hb:hb + 64, hcol + c * 128:hcol + (c + 1) * 128],
                    cm.ident[hb:hb + 64, hb:hb + 64], is_transpose=True,
                    start=(c % 4 == 0), stop=(c % 4 == 3)),
                    wait=(qkv_done + [(s_ev, s_ev.n)] if c == 0 else
                          ([(s_ev, s_ev.n)] if c % 4 == 0 else [])),
                    inc=(s_mm if c % 4 == 3 else None))
                if c % 4 == 3:
                    grp = c // 4
                    p.do(SCA, lambda e, grp=grp, bank2=bank2: e.copy(
                        out=vT[:, grp * 256:(grp + 1) * 256],
                        in_=ps_t[bank2][0:128, 0:256]),
                        wait=[(s_mm, s_mm.n), (s_ev, s_ev.n)], inc=s_ev)
            vt_done = s_ev.n
            for i in range(16):
                mxbuf = mx_sb[:, (i % 2) * T:(i % 2 + 1) * T]
                s_mx = s_mxp[i % 2]
                p.dma(SYN, mxbuf, AP(mxd, i * 128 * T, [[T, 128], [1, T]]),
                      wait=[(s_vv, mxld[i % 2])] +
                           ([(s_mm, s_mm.n)] if (h == 0 and i <= 1) else []),
                      inc=s_mx)
                mx_rd = s_mx.n
                madd_cnt = {}
                for q in range(4):
                    ps = [ps_a, ps_b][q % 2]
                    wq = []
                    if q <= 1:
                        wq = [(s_vv, s_vv.n), (s_ev, s_ev.n)]
                    else:
                        wq = [(s_vv, madd_cnt[q - 2])]
                    p.do(TEN, lambda e, q=q, ps=ps, hb=hb, hcol=hcol, i=i:
                         e.matmul(
                             ps[:, :],
                             qv[hb:hb + 64, hcol + i * 128:hcol + (i + 1) * 128],
                             kv[hb:hb + 64, hcol + q * TB:hcol + (q + 1) * TB],
                             start=True, stop=True),
                         wait=wq, inc=s_mm)
                    mmq = s_mm.n
                    p.do(VEC, lambda e, q=q, ps=ps, mxbuf=mxbuf: e.tensor_add(
                        sc_sb[:, q * TB:(q + 1) * TB], ps[:, :],
                        mxbuf[:, q * TB:(q + 1) * TB]),
                        wait=[(s_mm, mmq), (s_mx, mx_rd), (s_vv, s_vv.n)] +
                             ([(s_ev, s_ev.n)] if q == 0 else []), inc=s_vv)
                    madd_cnt[q] = s_vv.n
                sc_done = [(s_vv, s_vv.n), (s_ev, s_ev.n)]
                p.do(VEC, lambda e: e.reduce_max(zcol[:, 0:1], sc_sb[:],
                                                 axis=AX.X, negate=True),
                     wait=sc_done, inc=s_vv)
                p.do(SCA, lambda e: e.activation(out=att[:], in_=sc_sb[:],
                                                 func=AF.Exp, bias=zcol[:, 0:1]),
                     wait=[(s_vv, s_vv.n), (s_ev, s_ev.n), (s_mm, s_mm.n)],
                     inc=s_ev)
                p.do(VEC, lambda e: e.reduce_sum(zcol[:, 1:2], att[:], axis=AX.X),
                     wait=[(s_ev, s_ev.n), (s_vv, s_vv.n)], inc=s_vv)
                p.do(VEC, lambda e: e.reciprocal(zcol[:, 1:2], zcol[:, 1:2]),
                     wait=[(s_vv, s_vv.n)], inc=s_vv)
                p.do(VEC, lambda e: e.tensor_single_scalar(
                    out=att[:], in_=att[:], scalar=zcol[:, 1:2], op=ALU.mult),
                    wait=[(s_vv, s_vv.n)], inc=s_vv)
                att_done = s_vv.n
                if dbg is not None and seq == 0 and (h, i) == DBG_HI:
                    p.dma(GPS, dbg["sc"].ap(), sc_sb[:],
                          wait=[(s_vv, att_done)], inc=s_st)
                    p.dma(GPS, dbg["att"].ap(), att[:],
                          wait=[(s_vv, att_done)], inc=s_st)
                    p.dma(GPS, dbg["mx"].ap()[0:128, 0:2 * T].rearrange(
                        "p c -> p c") if False else dbg["mx"].ap(), mx_sb[:],
                        wait=[(s_vv, att_done)], inc=s_st)
                for c in range(16):
                    bk2 = c // 8
                    p.do(TEN, lambda e, c=c, bk2=bk2: e.matmul(
                        ps_t[bk2][:, (c % 8) * 128:(c % 8) * 128 + 128],
                        att[:, c * 128:(c + 1) * 128], cm.ident[:],
                        is_transpose=True, start=(c % 8 == 0), stop=(c % 8 == 7)),
                        wait=([(s_vv, att_done)] if c == 0 else []) +
                             ([(s_ev, s_ev.n), (s_vv, s_vv.n)] if c % 8 == 0 else []),
                        inc=(s_mm if c % 8 == 7 else None))
                    if c % 8 == 7:
                        p.do(SCA, lambda e, bk2=bk2: e.copy(
                            out=attT[:, bk2 * 1024:(bk2 + 1) * 1024],
                            in_=ps_t[bk2][:, :]),
                            wait=[(s_mm, s_mm.n), (s_ev, s_ev.n), (s_vv, s_vv.n)],
                            inc=s_ev)
                attT_done = s_ev.n
                for c in range(16):
                    p.do(TEN, lambda e, c=c, hb=hb: e.matmul(
                        ps_y[hb:hb + 64, 0:128], vT[:, c * 64:(c + 1) * 64],
                        attT[:, c * 128:(c + 1) * 128],
                        start=(c == 0), stop=(c == 15)),
                        wait=([(s_ev, attT_done), (s_vv, s_vv.n)] if c == 0 else []),
                        inc=(s_mm if c == 15 else None))
                p.do(SCA, lambda e, i=i, hb=hb, hcol=hcol: e.copy(
                    out=y_fm[hb:hb + 64, hcol + i * 128:hcol + (i + 1) * 128],
                    in_=ps_y[hb:hb + 64, 0:128]),
                    wait=[(s_mm, s_mm.n), (s_ev, s_ev.n)], inc=s_ev)
                mxld[i % 2] = s_vv.n
        ystrips_done = s_ev.n
        if dbg is not None and seq == 0:
            for nname, buf in [("qv", qv), ("kv", kv), ("vv", vv), ("yfm", y_fm),
                               ("attT", attT), ("vT", vT)]:
                p.dma(GPS, dbg[nname].ap(), buf[:],
                      wait=[(s_ev, ystrips_done), (s_vv, s_vv.n)], inc=s_st)

        wbuf = wful[1]
        for tb in range(NTB):
            for m in range(DC):
                bank = m % 2
                ps = [ps_a, ps_b][bank]
                for k in range(DC):
                    w = []
                    if k == 0:
                        w = [(s_vv, proj_ev[bank])]
                        if m == 0 and tb == 0:
                            w += [(s_ev, ystrips_done), (s_wb[1], s_wb[1].n)]
                    p.do(TEN, lambda e, m=m, k=k, tb=tb, ps=ps, wbuf=wbuf:
                         e.matmul(
                             ps[:, :],
                             wbuf[:, k * 1024 + m * 128: k * 1024 + (m + 1) * 128],
                             y_fm[:, k * T + tb * TB: k * T + (tb + 1) * TB],
                             start=(k == 0), stop=(k == DC - 1)), wait=w,
                         inc=(s_mm if k == DC - 1 else None))
                mmn = s_mm.n
                p.dma(SYN, xacc,
                      AP(x2, (seq * D + m * 128) * T + tb * TB, [[T, 128], [1, TB]]),
                      wait=[(s_vv, s_vv.n), (s_st, s_st.n)], inc=s_x2)
                p.do(VEC, lambda e, ps=ps, m=m: e.scalar_tensor_tensor(
                    out=xacc, in0=ps[:, :],
                    scalar=bcols[:, 3 * DC + m:3 * DC + m + 1],
                    in1=xacc, op0=ALU.add, op1=ALU.add),
                    wait=[(s_mm, mmn), (s_x2, s_x2.n), (s_vv, s_vv.n)], inc=s_vv)
                proj_ev[bank] = s_vv.n
                p.dma(GPS, AP(x3, (seq * D + m * 128) * T + tb * TB,
                              [[T, 128], [1, TB]]), xacc,
                      wait=[(s_vv, s_vv.n)], inc=s_st)
    p.do(SYN, lambda e: e.nop(), wait=[(s_st, s_st.n)])




EXT = 1536
PAD = 1536


def build_phase_a():
    nc = _mk_nc()
    xin = nc.dram_tensor("xin", [NSEQ * D, T], F32, kind="ExternalInput")
    watt = nc.dram_tensor("watt", [D, 3 * D], F32, kind="ExternalInput")
    wproj = nc.dram_tensor("wproj", [D, D], F32, kind="ExternalInput")
    ekt = nc.dram_tensor("ekt", [64, EXT], F32, kind="ExternalInput")
    evr = nc.dram_tensor("evr", [EXT, 64], F32, kind="ExternalInput")
    x1 = nc.dram_tensor("x1", [NSEQ * D, T], F32, kind="ExternalOutput")
    pd = [nc.dram_tensor(f"pd{b}", [128 * EXT + 2 * EXT], BF16, kind="Internal")
          for b in range(2)]
    ad = [nc.dram_tensor(f"ad{b}", [PAD + 128 * EXT + EXT], BF16, kind="Internal")
          for b in range(2)]
    with ExitStack() as stack:
        p = Prog(nc, stack)
        emit_l0attn(p, xin, watt, wproj, ekt, evr, x1, pd, ad)
        p.build()
    return nc


def emit_l0attn(p, xin_d, watt_d, wproj_d, ekt_d, evr_d, x1_d, pd, ad):
    cm = Common(p, sel8=False, identf=True)
    s_ld = p.sem("a_ld")
    s_wp = [p.sem("a_wp0"), p.sem("a_wp1")]
    s_ps = [p.sem("a_ps0"), p.sem("a_ps1")]
    s_bd = [p.sem("a_bd0"), p.sem("a_bd1")]
    s_as = [p.sem("a_as0"), p.sem("a_as1")]
    s_ar = [p.sem("a_ar0"), p.sem("a_ar1")]
    s_x = p.sem("a_xs")
    s_mm = p.sem("a_mm")
    s_ev = p.sem("a_ev")
    s_vv = p.sem("a_vv")
    s_st = p.sem("a_st")

    lnb = p.sbuf("a_ln", [128, DC * T], BF16)
    sc_n = lnb[:, 0:4096].bitcast(F32)
    att = lnb[:, 4096:5632]
    attT = lnb[:, 5632:7168]
    attR = lnb[:, 7168:8704]
    attRT = lnb[:, 8704:10240]
    sc_f = lnb[:, 10240:11264].bitcast(F32)
    attf = lnb[:, 11264:11776]
    attfT = lnb[:, 11776:12288]
    vT = lnb[:, 12288:13312]
    p_sb = lnb[:, 13312:14848]
    band = lnb[:, 14848:16384]
    wasb = p.sbuf("a_watt", [128, DC * 3 * D], BF16)
    yv = wasb[:, 0:DC * T]
    qv = p.sbuf("a_qv", [128, DC * T], BF16)
    kv = p.sbuf("a_kv", [128, DC * T], BF16)
    vv = p.sbuf("a_vv", [128, DC * T], BF16)
    xls = qv[:].bitcast(F32)
    garena = p.sbuf("a_garena", [128, T], BF16)
    xsq = garena[:, 0:2 * TB]
    lntmp = garena[:, 2 * TB:4 * TB].bitcast(F32)
    arena = p.sbuf("a_arena", [128, 2 * TB], F32)
    rows = arena[0:1, 0:2 * TB]
    xacc = arena[:, 0:TB]
    ekb = p.sbuf("a_ekb", [128, EXT], BF16)
    evb = p.sbuf("a_evb", [128, 12 * 64], BF16)
    masks = p.sbuf("a_masks", [128, 4 * TB + TB], BF16)
    zc = p.sbuf("a_zc", [128, 6], F32)
    fmr = p.sbuf("a_fmr", [1, 128], BF16)

    ps_a = p.psum("a_psa", [128, TB], F32)
    ps_b = p.psum("a_psb", [128, TB], F32)
    ps_c = p.psum("a_psc", [128, TB], F32)
    ps_d = p.psum("a_psd", [128, TB], F32)
    ps_t = [p.psum("a_pt0", [128, 1024], BF16), p.psum("a_pt1", [128, 1024], BF16)]
    ps_y = p.psum("a_py", [128, TB], F32)
    ps_tf = p.psum("a_ptf", [128, TB], F32)

    s0 = cm.sem
    p.do(GPS, lambda e: e.memset(masks[:], 0.0), wait=[(s0, s0.n)], inc=s0)
    for v in range(4):
        p.do(GPS, lambda e, v=v: e.affine_select(
            out=masks[:, v * TB:(v + 1) * TB], in_=masks[:, v * TB:(v + 1) * TB],
            compare_op=ALU.is_ge, fill=-1e9, base=128 * v,
            pattern=[[-1, TB]], channel_multiplier=1), wait=[(s0, s0.n)], inc=s0)
    cdone = s0.n
    p.dma(GPS, ekb[0:64, :], ekt_d.ap(), wait=[(s0, cdone)], inc=s_ld)
    p.dma(GPS, ekb[64:128, :], ekt_d.ap(), inc=s_ld)
    for c in range(12):
        p.dma(GPS, evb[:, c * 64:(c + 1) * 64],
              AP(evr_d, c * 128 * 64, [[64, 128], [1, 64]]), inc=s_ld)
    p.do(GPS, lambda e: e.memset(garena[:], 0.0), wait=[(s0, cdone)], inc=s0)
    zdone = s0.n
    for b in range(2):
        p.dma(SYN, AP(pd[b], 128 * EXT, [[EXT, 2], [1, EXT]]), garena[:, 0:EXT * 2]
              if False else garena[0:2, 0:EXT], wait=[(s0, zdone)], inc=s_ps[b])
        p.dma(SYN, AP(ad[b], 0, [[EXT, 1], [1, PAD]]), garena[0:1, 0:PAD],
              wait=[(s0, zdone)], inc=s_as[b])
        p.dma(SYN, AP(ad[b], PAD + 128 * EXT, [[EXT, 1], [1, EXT]]),
              garena[0:1, 0:EXT], wait=[(s0, zdone)], inc=s_as[b])
    z_ps = [s_ps[0].n, s_ps[1].n]
    z_as = [s_as[0].n, s_as[1].n]

    def load_watt(wait):
        for k in range(DC):
            p.dma(GPS, wasb[:, k * 3 * D:(k + 1) * 3 * D],
                  AP(watt_d, (k * 128) * 3 * D, [[3 * D, 128], [1, 3 * D]]),
                  wait=(wait if k == 0 else []), inc=s_wp[0])

    load_watt([(s_ld, s_ld.n)])
    watt_ready = s_wp[0].n
    prev_ln = None
    prev_attn = []
    for seq in range(NSEQ):
        iw = [(cm.sem, cm.done), (s_ld, s_ld.n),
              (s_ps[0], z_ps[0]), (s_ps[1], z_ps[1]),
              (s_as[0], z_as[0]), (s_as[1], z_as[1])]
        if seq > 0:
            iw = [(s_ev, s_ev.n), (s_vv, s_vv.n), (s_mm, s_mm.n), (s_st, s_st.n),
                  (s_ps[0], s_ps[0].n), (s_ps[1], s_ps[1].n),
                  (s_bd[0], s_bd[0].n), (s_bd[1], s_bd[1].n),
                  (s_as[0], s_as[0].n), (s_as[1], s_as[1].n),
                  (s_ar[0], s_ar[0].n), (s_ar[1], s_ar[1].n),
                  (prev_ln.s_ln, NTB), (prev_ln.s_st, prev_ln.s_st.n)]
            load_watt([(s_mm, s_mm.n), (s_x, s_x.n)])
            watt_ready = s_wp[0].n
        ln = emit_ln(p, cm, f"aln{seq}", xin_d, seq * D * T, lnb,
                     (ps_a, ps_b, ps_c, ps_d), xls, xsq, rows, lntmp,
                     init_waits=iw)
        prev_ln = ln

        evq = {0: 0, 1: 0}
        for tb in range(NTB):
            for m in range(3 * DC):
                bank = m % 2
                ps = [ps_a, ps_b][bank]
                for k in range(DC):
                    w = []
                    if k == 0:
                        w = [(s_ev, evq[bank])]
                        if m == 0 and tb == 0:
                            w += [(ln.s_ln, NTB), (s_wp[0], watt_ready)]
                    p.do(TEN, lambda e, m=m, k=k, tb=tb, ps=ps: e.matmul(
                        ps[:, :],
                        wasb[:, k * 3 * D + m * 128: k * 3 * D + (m + 1) * 128],
                        lnb[:, k * T + tb * TB: k * T + (tb + 1) * TB],
                        start=(k == 0), stop=(k == DC - 1)), wait=w,
                        inc=(s_mm if k == DC - 1 else None))
                dst = [qv, kv, vv][m // DC]
                mc = m % DC
                scl = 0.125 if (m // DC) == 1 else None
                p.do(SCA, lambda e, ps=ps, dst=dst, mc=mc, tb=tb, scl=scl:
                     (e.mul(out=dst[:, mc * T + tb * TB: mc * T + (tb + 1) * TB],
                            in_=ps[:, :], mul=0.125) if scl else
                      e.copy(out=dst[:, mc * T + tb * TB: mc * T + (tb + 1) * TB],
                             in_=ps[:, :])),
                     wait=[(s_mm, s_mm.n), (s_ev, s_ev.n)], inc=s_ev)
                evq[bank] = s_ev.n
        qkv_done = [(s_ev, s_ev.n), (s_mm, s_mm.n)]
        for kc in range(DC):
            p.dma(GPS, wasb[:, DC * T + kc * 1024:DC * T + (kc + 1) * 1024],
                  AP(wproj_d, (kc * 128) * D, [[D, 128], [1, D]]),
                  wait=([(s_mm, s_mm.n)] if kc == 0 else []), inc=s_wp[1])
        wp_pre = s_wp[1].n

        for h in range(16):
            hb = 64 * (h % 2)
            hcol = (h // 2) * 2048
            for c in range(16):
                bank2 = (c // 4) % 2
                p.do(TEN, lambda e, c=c, hb=hb, hcol=hcol, bank2=bank2: e.matmul(
                    ps_t[bank2][0:128, (c % 4) * 64:(c % 4) * 64 + 64],
                    vv[hb:hb + 64, hcol + c * 128:hcol + (c + 1) * 128],
                    cm.ident[hb:hb + 64, hb:hb + 64], is_transpose=True,
                    start=(c % 4 == 0), stop=(c % 4 == 3)),
                    wait=(qkv_done + [(s_vv, s_vv.n)] if c == 0 else
                          ([(s_vv, s_vv.n)] if c % 4 == 0 else [])),
                    inc=(s_mm if c % 4 == 3 else None))
                if c % 4 == 3:
                    grp = c // 4
                    p.do(VEC, lambda e, grp=grp, bank2=bank2: e.tensor_copy(
                        out=vT[:, grp * 256:(grp + 1) * 256],
                        in_=ps_t[bank2][0:128, 0:256]),
                        wait=[(s_mm, s_mm.n), (s_vv, s_vv.n)], inc=s_vv)
            vt_done = s_vv.n
            for i in range(16):
                t0 = i * 128
                B0 = 1 if t0 >= 1536 else 0
                smin = B0 * 512
                live = (t0 - smin) // 512 + 1
                has_far = (B0 == 1)
                par = i % 2
                q_sl = qv[hb:hb + 64, hcol + i * 128:hcol + (i + 1) * 128]
                for b3 in range(3):
                    ps = [ps_a, ps_b][b3 % 2]
                    p.do(TEN, lambda e, b3=b3, ps=ps, q_sl=q_sl, hb=hb: e.matmul(
                        ps[:, :], q_sl, ekb[hb:hb + 64, b3 * TB:(b3 + 1) * TB],
                        start=True, stop=True),
                        wait=[(s_vv, s_vv.n), (s_ev, s_ev.n)] if b3 <= 1 else
                             [(s_ev, s_ev.n)],
                        inc=s_mm)
                    p.do(SCA, lambda e, b3=b3, ps=ps: e.copy(
                        out=p_sb[:, b3 * TB:(b3 + 1) * TB], in_=ps[:, :]),
                        wait=[(s_mm, s_mm.n), (s_ev, s_ev.n)], inc=s_ev)
                p_done = s_ev.n
                p.dma(SYN, AP(pd[par], 0, [[EXT, 128], [1, EXT]]), p_sb[:],
                      wait=[(s_ev, p_done), (s_bd[par], s_bd[par].n)], inc=s_ps[par])
                p.dma(GPS, band[:],
                      AP(pd[par], EXT - 1 - t0 + smin, [[EXT - 1, 128], [1, EXT]]),
                      wait=[(s_ps[par], s_ps[par].n)], inc=s_bd[par])
                band_rd = s_bd[par].n
                for b3 in range(3):
                    ps = [ps_a, ps_b][b3 % 2]
                    s_abs = smin + b3 * 512
                    p.do(TEN, lambda e, b3=b3, ps=ps, q_sl=q_sl, hb=hb,
                         hcol=hcol, s_abs=s_abs: e.matmul(
                             ps[:, :], q_sl,
                             kv[hb:hb + 64, hcol + s_abs:hcol + s_abs + TB],
                             start=True, stop=True),
                         wait=[(s_vv, s_vv.n), (s_ev, s_ev.n)] if b3 <= 1 else
                              [(s_ev, s_ev.n)], inc=s_mm)
                    mmb = s_mm.n
                    if b3 == live - 1:
                        mv = (t0 - smin - b3 * 512) // 128
                        p.do(VEC, lambda e, b3=b3, ps=ps, mv=mv: e.tensor_add(
                            sc_n[:, b3 * TB:(b3 + 1) * TB], ps[:, :],
                            band[:, b3 * TB:(b3 + 1) * TB]),
                            wait=[(s_mm, mmb), (s_bd[par], band_rd),
                                  (s_vv, s_vv.n)], inc=s_vv)
                        p.do(VEC, lambda e, b3=b3, mv=mv: e.tensor_add(
                            sc_n[:, b3 * TB:(b3 + 1) * TB],
                            sc_n[:, b3 * TB:(b3 + 1) * TB],
                            masks[:, mv * TB:(mv + 1) * TB]),
                            wait=[(s_vv, s_vv.n)], inc=s_vv)
                    elif b3 < live:
                        p.do(VEC, lambda e, b3=b3, ps=ps: e.tensor_add(
                            sc_n[:, b3 * TB:(b3 + 1) * TB], ps[:, :],
                            band[:, b3 * TB:(b3 + 1) * TB]),
                            wait=[(s_mm, mmb), (s_bd[par], band_rd),
                                  (s_vv, s_vv.n)], inc=s_vv)
                    else:
                        p.do(VEC, lambda e, b3=b3, ps=ps: e.tensor_scalar(
                            out=sc_n[:, b3 * TB:(b3 + 1) * TB], in0=ps[:, :],
                            scalar1=0.0, scalar2=-1e9, op0=ALU.mult, op1=ALU.add),
                            wait=[(s_mm, mmb), (s_vv, s_vv.n)], inc=s_vv)
                if has_far:
                    p.do(TEN, lambda e, q_sl=q_sl, hb=hb, hcol=hcol: e.matmul(
                        ps_c[:, :], q_sl, kv[hb:hb + 64, hcol:hcol + TB],
                        start=True, stop=True), wait=[(s_vv, s_vv.n)], inc=s_mm)
                    p.do(VEC, lambda e: e.tensor_copy(out=zc[:, 4:5],
                                                      in_=p_sb[:, 512:513]),
                         wait=[(s_ev, p_done), (s_vv, s_vv.n)], inc=s_vv)
                    p.do(VEC, lambda e: e.tensor_single_scalar(
                        out=sc_f[:, 0:TB], in_=ps_c[:, :],
                        scalar=zc[:, 4:5], op=ALU.add),
                        wait=[(s_mm, s_mm.n), (s_vv, s_vv.n)],
                        inc=s_vv)
                p.do(VEC, lambda e: e.reduce_max(zc[:, 0:1], sc_n[:, 0:1536],
                                                 axis=AX.X, negate=True),
                     wait=[(s_vv, s_vv.n)], inc=s_vv)
                if has_far:
                    p.do(VEC, lambda e: e.reduce_max(zc[:, 1:2], sc_f[:, 0:TB],
                                                     axis=AX.X, negate=True),
                         wait=[(s_vv, s_vv.n)], inc=s_vv)
                    p.do(VEC, lambda e: e.tensor_tensor(
                        out=zc[:, 0:1], in0=zc[:, 0:1], in1=zc[:, 1:2],
                        op=ALU.min), wait=[(s_vv, s_vv.n)], inc=s_vv)
                nm_done = s_vv.n
                p.do(SCA, lambda e: e.activation(out=att[:], in_=sc_n[:, 0:1536],
                                                 func=AF.Exp, bias=zc[:, 0:1]),
                     wait=[(s_vv, nm_done), (s_ev, s_ev.n), (s_mm, s_mm.n),
                           (s_as[par], s_as[par].n)], inc=s_ev)
                if has_far:
                    p.do(SCA, lambda e: e.activation(out=attf[:], in_=sc_f[:, 0:TB],
                                                     func=AF.Exp, bias=zc[:, 0:1]),
                         wait=[(s_ev, s_ev.n)], inc=s_ev)
                exp_done = s_ev.n
                p.do(VEC, lambda e: e.reduce_sum(zc[:, 2:3], att[:], axis=AX.X),
                     wait=[(s_ev, exp_done), (s_vv, s_vv.n)], inc=s_vv)
                if has_far:
                    p.do(VEC, lambda e: e.reduce_sum(zc[:, 3:4], attf[:], axis=AX.X),
                         wait=[(s_vv, s_vv.n)], inc=s_vv)
                    p.do(VEC, lambda e: e.tensor_add(zc[:, 2:3], zc[:, 2:3],
                                                     zc[:, 3:4]),
                         wait=[(s_vv, s_vv.n)], inc=s_vv)
                p.do(VEC, lambda e: e.reciprocal(zc[:, 2:3], zc[:, 2:3]),
                     wait=[(s_vv, s_vv.n)], inc=s_vv)
                p.do(VEC, lambda e: e.tensor_single_scalar(
                    out=att[:], in_=att[:], scalar=zc[:, 2:3], op=ALU.mult),
                    wait=[(s_vv, s_vv.n)], inc=s_vv)
                if has_far:
                    p.do(VEC, lambda e: e.tensor_single_scalar(
                        out=attf[:], in_=attf[:], scalar=zc[:, 2:3], op=ALU.mult),
                        wait=[(s_vv, s_vv.n)], inc=s_vv)
                att_done = s_vv.n
                p.dma(SYN, AP(ad[par], PAD, [[EXT, 128], [1, EXT]]), att[:],
                      wait=[(s_vv, att_done), (s_ar[par], s_ar[par].n)],
                      inc=s_as[par])
                p.dma(GPS, attR[:],
                      AP(ad[par], PAD + 1 + t0 - smin - 1536,
                         [[EXT + 1, 128], [1, EXT]]),
                      wait=[(s_as[par], s_as[par].n)], inc=s_ar[par])
                ar_rd = s_ar[par].n
                for c in range(12):
                    bk2 = c // 8
                    p.do(TEN, lambda e, c=c, bk2=bk2: e.matmul(
                        ps_t[bk2][:, (c % 8) * 128:(c % 8) * 128 + 128],
                        att[:, c * 128:(c + 1) * 128], cm.ident[:],
                        is_transpose=True, start=(c % 8 == 0),
                        stop=(c % 8 == 7 or c == 11)),
                        wait=([(s_vv, att_done)] if c == 0 else []) +
                             ([(s_vv, s_vv.n)] if c % 8 == 0 else []),
                        inc=(s_mm if (c % 8 == 7 or c == 11) else None))
                    if c % 8 == 7 or c == 11:
                        n_c = 1024 if c == 7 else 512
                        p.do(VEC, lambda e, bk2=bk2, c=c, n_c=n_c: e.tensor_copy(
                            out=attT[:, bk2 * 1024:bk2 * 1024 + n_c],
                            in_=ps_t[bk2][:, 0:n_c]),
                            wait=[(s_mm, s_mm.n), (s_vv, s_vv.n)], inc=s_vv)
                attT_done = s_vv.n
                if has_far:
                    for c in range(4):
                        p.do(TEN, lambda e, c=c: e.matmul(
                            ps_t[0][:, c * 128:(c + 1) * 128],
                            attf[:, c * 128:(c + 1) * 128], cm.ident[:],
                            is_transpose=True, start=(c == 0), stop=(c == 3)),
                            wait=[(s_vv, attT_done)] if c == 0 else [],
                            inc=(s_mm if c == 3 else None))
                    p.do(VEC, lambda e: e.tensor_copy(out=attfT[:],
                                                      in_=ps_t[0][:, 0:512]),
                         wait=[(s_mm, s_mm.n), (s_vv, s_vv.n)], inc=s_vv)
                    p.do(VEC, lambda e: e.reduce_sum(zc[:, 3:4], attf[:],
                                                     axis=AX.X),
                         wait=[(s_vv, s_vv.n)], inc=s_vv)
                    p.do(TEN, lambda e: e.matmul(
                        ps_tf[0:1, 0:128], zc[:, 3:4], cm.identf[:],
                        is_transpose=True, start=True, stop=True),
                        wait=[(s_vv, s_vv.n)], inc=s_mm)
                    p.do(VEC, lambda e: e.tensor_copy(out=fmr[:],
                                                      in_=ps_tf[0:1, 0:128]),
                         wait=[(s_mm, s_mm.n), (s_vv, s_vv.n)], inc=s_vv)
                aux_done = s_vv.n
                for c in range(12):
                    bk2 = c // 8
                    p.do(TEN, lambda e, c=c, bk2=bk2: e.matmul(
                        ps_t[bk2][:, (c % 8) * 128:(c % 8) * 128 + 128],
                        attR[:, c * 128:(c + 1) * 128], cm.ident[:],
                        is_transpose=True, start=(c % 8 == 0),
                        stop=(c % 8 == 7 or c == 11)),
                        wait=([(s_ar[par], ar_rd), (s_vv, aux_done)] if c == 0
                              else []) +
                             ([(s_vv, s_vv.n)] if c % 8 == 0 and c > 0 else []),
                        inc=(s_mm if (c % 8 == 7 or c == 11) else None))
                    if c % 8 == 7 or c == 11:
                        n_c = 1024 if c == 7 else 512
                        p.do(VEC, lambda e, bk2=bk2, n_c=n_c: e.tensor_copy(
                            out=attRT[:, bk2 * 1024:bk2 * 1024 + n_c],
                            in_=ps_t[bk2][:, 0:n_c]),
                            wait=[(s_mm, s_mm.n), (s_vv, s_vv.n)], inc=s_vv)
                attRT_done = s_vv.n
                nmm = []
                for c in range(12):
                    cab = B0 * 4 + c // 1 if False else B0 * 4 + c
                    nmm.append(("near", c, cab))
                if has_far:
                    for c in range(4):
                        nmm.append(("far", c, c))
                for c in range(12):
                    nmm.append(("rel", c, c))
                if has_far:
                    nmm.append(("fout", 0, 0))
                nt = len(nmm)
                for idx, (kind, c, cab) in enumerate(nmm):
                    def mk(e, kind=kind, c=c, cab=cab, idx=idx, hb=hb, nt=nt):
                        st = (idx == 0)
                        sp = (idx == nt - 1)
                        if kind == "near":
                            return e.matmul(ps_y[hb:hb + 64, 0:128],
                                            vT[:, cab * 64:(cab + 1) * 64],
                                            attT[:, c * 128:(c + 1) * 128],
                                            start=st, stop=sp)
                        if kind == "far":
                            return e.matmul(ps_y[hb:hb + 64, 0:128],
                                            vT[:, cab * 64:(cab + 1) * 64],
                                            attfT[:, c * 128:(c + 1) * 128],
                                            start=st, stop=sp)
                        if kind == "rel":
                            return e.matmul(ps_y[hb:hb + 64, 0:128],
                                            evb[:, c * 64:(c + 1) * 64],
                                            attRT[:, c * 128:(c + 1) * 128],
                                            start=st, stop=sp)
                        return e.matmul(ps_y[hb:hb + 64, 0:128],
                                        evb[0:1, 256:320], fmr[:],
                                        start=st, stop=sp)
                    p.do(TEN, mk,
                         wait=([(s_vv, attRT_done), (s_ev, s_ev.n)]
                               if idx == 0 else []),
                         inc=(s_mm if idx == nt - 1 else None))
                p.do(SCA, lambda e, i=i, hb=hb, hcol=hcol: e.copy(
                    out=yv[hb:hb + 64, hcol + i * 128:hcol + (i + 1) * 128],
                    in_=ps_y[hb:hb + 64, 0:128]),
                    wait=[(s_mm, s_mm.n), (s_ev, s_ev.n)], inc=s_ev)
        y_done = s_ev.n

        pj_ev = {0: 0, 1: 0}
        for tb in range(NTB):
            for m in range(DC):
                bank = m % 2
                ps = [ps_a, ps_b][bank]
                for k in range(DC):
                    w = []
                    if k == 0:
                        w = [(s_vv, pj_ev[bank])]
                        if m == 0 and tb == 0:
                            w += [(s_ev, y_done), (s_wp[1], s_wp[1].n)]
                    p.do(TEN, lambda e, m=m, k=k, tb=tb, ps=ps: e.matmul(
                        ps[:, :],
                        wasb[:, DC * T + k * 1024 + m * 128:
                             DC * T + k * 1024 + (m + 1) * 128],
                        yv[:, k * T + tb * TB: k * T + (tb + 1) * TB],
                        start=(k == 0), stop=(k == DC - 1)), wait=w,
                        inc=(s_mm if k == DC - 1 else None))
                mmn = s_mm.n
                p.dma(SYN, xacc,
                      AP(xin_d, (seq * D + m * 128) * T + tb * TB,
                         [[T, 128], [1, TB]]),
                      wait=[(s_vv, s_vv.n), (s_st, s_st.n)], inc=s_x)
                p.do(VEC, lambda e, ps=ps: e.tensor_add(xacc, ps[:, :], xacc),
                     wait=[(s_mm, mmn), (s_x, s_x.n), (s_vv, s_vv.n)], inc=s_vv)
                pj_ev[bank] = s_vv.n
                p.dma(GPS, AP(x1_d, (seq * D + m * 128) * T + tb * TB,
                              [[T, 128], [1, TB]]), xacc,
                      wait=[(s_vv, s_vv.n), (s_st, s_st.n)], inc=s_st)
    p.do(SYN, lambda e: e.nop(), wait=[(s_st, s_st.n)])


def _feat_major(x):
    return np.ascontiguousarray(x.transpose(0, 2, 1)).reshape(-1, x.shape[1])


def _b1_cols(b):
    ne, hf = b.shape
    return np.ascontiguousarray(
        b.reshape(ne, hf // 128, 128).transpose(0, 2, 1)).reshape(ne * 128, hf // 128)


def _bias_cols(b):
    return np.ascontiguousarray(b.reshape(-1, 128).T)


def kernel(L0_attn_x, L0_attn_w_attn, L0_attn_w_proj, L0_attn_embk, L0_attn_embv,
           L0_ffn_alpha, L0_ffn_w_gate, L0_ffn_b_gate, L0_ffn_W1m, L0_ffn_b1m,
           L0_ffn_W2m, L0_ffn_b2m, L0_ffn_W1e, L0_ffn_b1e, L0_ffn_W2e, L0_ffn_b2e,
           L1_attn_Wq, L1_attn_bq, L1_attn_Wk, L1_attn_bk, L1_attn_Wv, L1_attn_bv,
           L1_attn_Wo, L1_attn_bo, L1_attn_src_blocks, L1_attn_tgt_blocks,
           L1_ffn_Wg, L1_ffn_bg, L1_ffn_W1, L1_ffn_b1, L1_ffn_W2, L1_ffn_b2):
    ncore = 8
    x = np.asarray(L0_attn_x, dtype=np.float32)

    rho = np.arange(EXT)
    rr = np.minimum(EXT - 1 - rho, 1023)
    ekt = np.ascontiguousarray(np.asarray(L0_attn_embk, np.float32)[rr].T)
    evr = np.ascontiguousarray(np.asarray(L0_attn_embv, np.float32)[rr])
    watt = np.ascontiguousarray(np.asarray(L0_attn_w_attn, np.float32))
    wproj = np.ascontiguousarray(np.asarray(L0_attn_w_proj, np.float32))
    alpha = np.asarray(L0_ffn_alpha, np.float32).reshape(1)
    wgt = np.ascontiguousarray(np.asarray(L0_ffn_w_gate, np.float32))
    bgt = np.ascontiguousarray(np.asarray(L0_ffn_b_gate, np.float32))
    w1m = np.ascontiguousarray(np.asarray(L0_ffn_W1m, np.float32))
    b1m = _bias_cols(np.asarray(L0_ffn_b1m, np.float32))
    w2m = np.ascontiguousarray(np.asarray(L0_ffn_W2m, np.float32))
    b2m = _bias_cols(np.asarray(L0_ffn_b2m, np.float32))
    w1e = np.ascontiguousarray(np.asarray(L0_ffn_W1e, np.float32).reshape(NE * D, HF))
    b1e = _b1_cols(np.asarray(L0_ffn_b1e, np.float32))
    w2e = np.ascontiguousarray(np.asarray(L0_ffn_W2e, np.float32).reshape(NE * HF, D))
    b2e = np.ascontiguousarray(
        np.asarray(L0_ffn_b2e, np.float32).reshape(NE, DC, 128)
        .transpose(0, 2, 1)).reshape(NE * 128, DC)
    wqt = np.ascontiguousarray(np.asarray(L1_attn_Wq, np.float32).T)
    wkt = np.ascontiguousarray(np.asarray(L1_attn_Wk, np.float32).T)
    wvt = np.ascontiguousarray(np.asarray(L1_attn_Wv, np.float32).T)
    wot = np.ascontiguousarray(np.asarray(L1_attn_Wo, np.float32).T)
    bq = _bias_cols(np.asarray(L1_attn_bq, np.float32))
    bk = _bias_cols(np.asarray(L1_attn_bk, np.float32))
    bv = _bias_cols(np.asarray(L1_attn_bv, np.float32))
    bo = _bias_cols(np.asarray(L1_attn_bo, np.float32))
    srcb = np.asarray(L1_attn_src_blocks, np.int32)
    tgtb = np.asarray(L1_attn_tgt_blocks, np.int32)
    wg = np.ascontiguousarray(np.asarray(L1_ffn_Wg, np.float32))
    bg = np.ascontiguousarray(np.asarray(L1_ffn_bg, np.float32))
    w1 = np.ascontiguousarray(np.asarray(L1_ffn_W1, np.float32).reshape(NE * D, HF))
    b1 = _b1_cols(np.asarray(L1_ffn_b1, np.float32))
    w2 = np.ascontiguousarray(np.asarray(L1_ffn_W2, np.float32).reshape(NE * HF, D))
    b2 = np.ascontiguousarray(np.asarray(L1_ffn_b2, np.float32))

    xs = [_feat_major(x[2 * c:2 * c + 2]) for c in range(ncore)]
    cores = list(range(ncore))

    nc_a = build_phase_a()
    res = run_bass_kernel_spmd(nc_a, [
        {"xin": xs[c], "watt": watt, "wproj": wproj, "ekt": ekt, "evr": evr}
        for c in cores], core_ids=cores)
    x1s = [res.results[c]["x1"] for c in cores]

    nc_b = build_phase_b()
    res = run_bass_kernel_spmd(nc_b, [
        {"x1": x1s[c], "alpha": alpha, "wgt": wgt, "bgt": bgt,
         "w1m": w1m, "b1m": b1m, "w2m": w2m, "b2m": b2m,
         "w1e": w1e, "b1e": b1e, "w2e": w2e, "b2e": b2e}
        for c in cores], core_ids=cores)
    x2s = [res.results[c]["x2"] for c in cores]

    nc_c = build_phase_c()
    res = run_bass_kernel_spmd(nc_c, [
        {"x2": x2s[c], "wqt": wqt, "wkt": wkt, "wvt": wvt, "wot": wot,
         "bq": bq, "bk": bk, "bv": bv, "bo": bo, "srcb": srcb, "tgtb": tgtb}
        for c in cores], core_ids=cores)
    x3s = [res.results[c]["x3"] for c in cores]

    nc_d = build_phase_d()
    res = run_bass_kernel_spmd(nc_d, [
        {"x3": x3s[c], "wg": wg, "bg": bg, "w1": w1, "b1": b1, "w2": w2, "b2": b2}
        for c in cores], core_ids=cores)
    x4s = [res.results[c]["x4"] for c in cores]

    out = np.concatenate(
        [x4s[c].reshape(NSEQ, D, T).transpose(0, 2, 1) for c in range(ncore)],
        axis=0)
    return np.ascontiguousarray(out).astype(np.float32)


# version 02
# speedup vs baseline: 1.1661x; 1.1661x over previous
import numpy as np
from contextlib import ExitStack
import concourse.bass as bass
import concourse.mybir as mybir
from concourse.bass_utils import run_bass_kernel_spmd
from concourse.ap import AP

F32 = mybir.dt.float32
BF16 = mybir.dt.bfloat16
I32 = mybir.dt.int32
AF = mybir.ActivationFunctionType
ALU = mybir.AluOpType
AX = mybir.AxisListType

TEN, VEC, SCA, GPS, SYN = "tensor", "vector", "scalar", "gpsimd", "sync"
ENGINES = [TEN, VEC, SCA, GPS, SYN]

NSEQ = 2
T = 2048
D = 1024
DC = D // 128
HF = 4096
HC = HF // 128
NE = 8
TB = 512
NTB = T // TB


class Sem:
    def __init__(self, h):
        self.h = h
        self.n = 0

    def bump(self, k):
        self.n += k
        return self.n


class Prog:
    def __init__(self, nc, stack, sem_stack=None):
        self.nc = nc
        self.stack = stack
        self.sem_stack = sem_stack if sem_stack is not None else stack
        self.ops = {e: [] for e in ENGINES}

    def sem(self, name):
        return Sem(self.sem_stack.enter_context(self.nc.semaphore(name)))

    def sbuf(self, name, shape, dt):
        return self.stack.enter_context(self.nc.sbuf_tensor(name, shape, dt))

    def psum(self, name, shape, dt):
        return self.stack.enter_context(self.nc.psum_tensor(name, shape, dt))

    def do(self, eng, fn, wait=(), inc=None, inc_k=1):
        if inc is not None:
            inc.bump(inc_k)
        self.ops[eng].append((tuple(wait), fn, (inc.h if inc else None), inc_k))

    def dma(self, eng, out, in_, wait=(), inc=None, **kw):
        assert inc is not None
        self.do(eng, lambda e: e.dma_start(out=out, in_=in_, **kw), wait=wait,
                inc=inc, inc_k=16)

    def flush(self, block):
        for eng_name in ENGINES:
            ops = self.ops[eng_name]
            if not ops:
                continue
            dec = getattr(block, eng_name)

            def section(e, _ops=ops):
                for waits, fn, inc_h, inc_k in _ops:
                    for (s, v) in waits:
                        e.wait_ge(s.h, v)
                    inst = fn(e)
                    if inc_h is not None:
                        inst.then_inc(inc_h, inc_k)

            dec(section)
        self.ops = {e: [] for e in ENGINES}

    def build(self):
        with self.nc.Block() as block:
            self.flush(block)


def _mk_nc():
    return bass.Bass("TRN2", target_bir_lowering=False, debug=False)


class Common:
    def __init__(self, p, sel8=False, identf=False, pfx="c", entry=()):
        self.p = p
        s = p.sem(pfx + "_init")
        self.sem = s
        self.ident = p.sbuf(pfx + "_ident", [128, 128], BF16)
        p.do(GPS, lambda e: e.memset(self.ident[:], 0.0), wait=list(entry), inc=s)
        p.do(GPS, lambda e: e.affine_select(
            out=self.ident[:], in_=self.ident[:], compare_op=ALU.not_equal,
            fill=1.0, base=0, pattern=[[-1, 128]], channel_multiplier=1),
            wait=[(s, s.n)], inc=s)
        if identf:
            self.identf = p.sbuf(pfx + "_identf", [128, 128], F32)
            p.do(GPS, lambda e: e.memset(self.identf[:], 0.0), wait=[(s, s.n)], inc=s)
            p.do(GPS, lambda e: e.affine_select(
                out=self.identf[:], in_=self.identf[:], compare_op=ALU.not_equal,
                fill=1.0, base=0, pattern=[[-1, 128]], channel_multiplier=1),
                wait=[(s, s.n)], inc=s)
        self.ones_b = p.sbuf(pfx + "_onesb", [128, 1], BF16)
        self.ones_f = p.sbuf(pfx + "_onesf", [128, 1], F32)
        self.ones_row = p.sbuf(pfx + "_onesrow", [1, 128], BF16)
        self.ones_rowf = p.sbuf(pfx + "_onesrowf", [1, 128], F32)
        self.eps = p.sbuf(pfx + "_eps", [1, 1], F32)
        p.do(GPS, lambda e: e.memset(self.ones_b[:], 1.0), wait=[(s, s.n)], inc=s)
        p.do(GPS, lambda e: e.memset(self.ones_f[:], 1.0), wait=[(s, s.n)], inc=s)
        p.do(GPS, lambda e: e.memset(self.ones_row[:], 1.0), wait=[(s, s.n)], inc=s)
        p.do(GPS, lambda e: e.memset(self.ones_rowf[:], 1.0), wait=[(s, s.n)], inc=s)
        p.do(GPS, lambda e: e.memset(self.eps[:], 1e-5), wait=[(s, s.n)], inc=s)
        if sel8:
            self.sel8 = p.sbuf(pfx + "_sel8", [8, 8 * 128], BF16)
            p.do(GPS, lambda e: e.memset(self.sel8[:], 1.0), wait=[(s, s.n)], inc=s)
            p.do(GPS, lambda e: e.affine_select(
                out=self.sel8[:], in_=self.sel8[:], compare_op=ALU.is_ge,
                fill=0.0, base=0, pattern=[[1, 1024]], channel_multiplier=-128),
                wait=[(s, s.n)], inc=s)
            p.do(GPS, lambda e: e.affine_select(
                out=self.sel8[:], in_=self.sel8[:], compare_op=ALU.is_ge,
                fill=0.0, base=127, pattern=[[-1, 1024]], channel_multiplier=128),
                wait=[(s, s.n)], inc=s)
        self.done = s.n


class LnOut:
    def __init__(self, s_ln, s_st):
        self.s_ln = s_ln
        self.s_st = s_st


def make_ln_sems(p, pfx):
    return tuple(p.sem(pfx + s) for s in
                 ["_ld0", "_ld1", "_mm", "_sq", "_st", "_bc", "_ln"])


def emit_ln(p, cm, pfx, x_dram, base_elem, lnb, ps4, xin_sb, xsq, rows, tmp,
            init_waits=(), x_eng=SYN, sems=None):
    ps_a, ps_b, ps_c, ps_d = ps4
    if sems is None:
        sems = make_ln_sems(p, pfx)
    s_ld2 = [sems[0], sems[1]]
    s_mm, s_sq, s_st, s_bc, s_ln = sems[2], sems[3], sems[4], sems[5], sems[6]
    sqmm = {}

    def row(i):
        return rows[0:1, (i - 1) * TB:i * TB]

    for tb in range(NTB):
        buf = tb % 2
        xt = xin_sb[:, buf * DC * TB:(buf + 1) * DC * TB]
        s_ld = s_ld2[tb % 2]
        w_ld = list(init_waits) if tb == 0 else ([(s_ln, s_ln.n)] if tb >= 2 else [])
        for j in range(DC):
            src = AP(x_dram, base_elem + (j * 128) * T + tb * TB, [[T, 128], [1, TB]])
            p.dma(x_eng, xt[:, j * TB:(j + 1) * TB], src,
                  wait=(w_ld if j == 0 else []), inc=s_ld)
        ld_done = s_ld.n
        for j in range(DC):
            w = []
            if j == 0:
                w = [(s_ld, ld_done)] + (list(init_waits) if tb == 0 else [])
                if tb > 0:
                    w.append((s_st, s_st.n))
            p.do(TEN, lambda e, j=j, xt=xt: e.matmul(
                ps_a[0:1, :], cm.ones_f[:], xt[:, j * TB:(j + 1) * TB],
                start=(j == 0), stop=(j == DC - 1)), wait=w,
                inc=(s_mm if j == DC - 1 else None))
        for j in range(DC):
            sq = xsq[:, (j % 2) * TB:(j % 2 + 1) * TB]
            w = [(s_ld, ld_done)]
            if j >= 2:
                w.append((s_mm, sqmm[(tb, j - 2)]))
            elif tb > 0:
                w.append((s_mm, sqmm[(tb - 1, DC - 2 + j)]))
            p.do(SCA, lambda e, j=j, xt=xt, sq=sq: e.activation(
                out=sq, in_=xt[:, j * TB:(j + 1) * TB], func=AF.Square),
                wait=w, inc=s_sq)
            w2 = [(s_sq, s_sq.n)]
            if tb > 0 and j == 0:
                w2.append((s_st, s_st.n))
            p.do(TEN, lambda e, j=j, sq=sq: e.matmul(
                ps_b[0:1, :], cm.ones_b[:], sq,
                start=(j == 0), stop=(j == DC - 1)), wait=w2, inc=s_mm)
            sqmm[(tb, j)] = s_mm.n
        mm_done = s_mm.n
        p.do(VEC, lambda e: e.tensor_scalar_mul(row(1), ps_a[0:1, :], 1.0 / D),
             wait=[(s_mm, mm_done)], inc=s_st)
        p.do(VEC, lambda e: e.tensor_mul(row(2), row(1), row(1)),
             wait=[(s_st, s_st.n)], inc=s_st)
        p.do(VEC, lambda e: e.scalar_tensor_tensor(
            out=row(2), in0=ps_b[0:1, :], scalar=1.0 / D, in1=row(2),
            op0=ALU.mult, op1=ALU.subtract), wait=[(s_st, s_st.n)], inc=s_st)
        p.do(SCA, lambda e: e.activation(out=row(2), in_=row(2), func=AF.Sqrt,
                                         bias=cm.eps[:]), wait=[(s_st, s_st.n)], inc=s_st)
        p.do(VEC, lambda e: e.reciprocal(row(2), row(2)),
             wait=[(s_st, s_st.n)], inc=s_st)
        p.do(VEC, lambda e: e.scalar_tensor_tensor(
            out=row(1), in0=row(1), scalar=-1.0, in1=row(2),
            op0=ALU.mult, op1=ALU.mult), wait=[(s_st, s_st.n)], inc=s_st)
        st_done = s_st.n
        w_bc = [(s_st, st_done)]
        if tb > 0:
            w_bc.append((s_ln, s_ln.n))
        elif init_waits:
            w_bc += list(init_waits)
        p.do(TEN, lambda e: e.matmul(ps_c[:, :], cm.ones_rowf[:], row(2),
                                     start=True, stop=True), wait=w_bc, inc=s_bc)
        p.do(TEN, lambda e: e.matmul(ps_d[:, :], cm.ones_rowf[:], row(1),
                                     start=True, stop=True), inc=s_bc)
        bc_done = s_bc.n
        for j in range(DC):
            p.do(VEC, lambda e, j=j, xt=xt: e.tensor_mul(
                tmp[:], xt[:, j * TB:(j + 1) * TB], ps_c[:, :]),
                wait=[(s_bc, bc_done), (s_st, s_st.n)], inc=s_st)
            p.do(VEC, lambda e, j=j, tb=tb: e.tensor_add(
                lnb[:, j * T + tb * TB: j * T + (tb + 1) * TB], tmp[:], ps_d[:, :]),
                wait=[(s_st, s_st.n)], inc=(s_ln if j == DC - 1 else s_st))
    return LnOut(s_ln, s_st)


def build_phase_d():
    nc = _mk_nc()
    x3 = nc.dram_tensor("x3", [NSEQ * D, T], F32, kind="ExternalInput")
    wg = nc.dram_tensor("wg", [D, NE], F32, kind="ExternalInput")
    bg = nc.dram_tensor("bg", [NE], F32, kind="ExternalInput")
    w1 = nc.dram_tensor("w1", [NE * D, HF], F32, kind="ExternalInput")
    b1 = nc.dram_tensor("b1", [NE * 128, HC], F32, kind="ExternalInput")
    w2 = nc.dram_tensor("w2", [NE * HF, D], F32, kind="ExternalInput")
    b2 = nc.dram_tensor("b2", [NE, D], F32, kind="ExternalInput")
    x4 = nc.dram_tensor("x4", [NSEQ * D, T], F32, kind="ExternalOutput")
    with ExitStack() as stack:
        p = Prog(nc, stack)
        emit_l1ffn(p, x3, wg, bg, w1, b1, w2, b2, x4)
        p.build()
    return nc


def emit_l1ffn(p, x3, wg, bg, w1, b1, w2, b2, x4, entry=()):
    cm = Common(p, sel8=True, identf=True, pfx="cd", entry=entry)
    lnsems = make_ln_sems(p, "dln")
    s_ld = p.sem("d_ld")
    s_w1 = p.sem("d_w1s")
    s_w2 = p.sem("d_w2s")
    s_b1 = p.sem("d_b1s")
    s_x3 = p.sem("d_x3s")
    s_mm = p.sem("d_mm")
    s_evh = p.sem("d_evh")
    s_evv = p.sem("d_evv")
    s_g = p.sem("d_g")
    s_st = p.sem("d_st")

    lnb = p.sbuf("d_ln", [128, DC * T], BF16)
    w1sb = p.sbuf("d_w1", [128, DC * HF], BF16)
    w2sb = p.sbuf("d_w2", [128, HC * D], BF16)
    hsb = p.sbuf("d_h", [128, HC * TB], BF16)
    xin = hsb[:].bitcast(F32)
    arena = p.sbuf("d_arena", [128, 2 * TB], F32)
    rows = arena[0:1, 0:2 * TB]
    wgsb = p.sbuf("d_wg", [128, DC * NE], BF16)
    bgsb = p.sbuf("d_bg", [NE, 1], F32)
    b1sb = p.sbuf("d_b1", [128, HC], F32)
    b2sb = p.sbuf("d_b2", [NE, D], BF16)
    gtok = p.sbuf("d_gtok", [128, 4 * NE], F32)
    gsc = p.sbuf("d_gsc", [128, 4 * NE], F32)
    gkeep = p.sbuf("d_gkeep", [128, 4 * NE], F32)
    gred = p.sbuf("d_gred", [128, 4], F32)
    gbf = p.sbuf("d_gbf", [128, 4 * NE], BF16)
    garena = p.sbuf("d_garena", [128, T], BF16)
    gwT = garena[0:NE, 0:T]
    xsq = garena[:, 0:2 * TB]
    lntmp = garena[:, 2 * TB:4 * TB].bitcast(F32)
    gb_sb = arena[:, 0:TB]
    tmp2 = arena[:, TB:2 * TB]
    lg_sb = arena[0:NE, 0:TB]

    ps_a = p.psum("d_psa", [128, TB], F32)
    ps_b = p.psum("d_psb", [128, TB], F32)
    ps_c = p.psum("d_psc", [128, TB], F32)
    ps_d = p.psum("d_psd", [128, TB], F32)
    ps_m = [p.psum("d_pm0", [128, TB], F32), p.psum("d_pm1", [128, TB], F32)]
    ps_t = p.psum("d_pt", [128, 1024], BF16)
    ps_tf = p.psum("d_ptf", [128, TB], F32)

    for j in range(DC):
        p.dma(GPS, wgsb[:, j * NE:(j + 1) * NE],
              AP(wg, j * 128 * NE, [[NE, 128], [1, NE]]),
              wait=([(cm.sem, cm.done)] if j == 0 else []), inc=s_ld)
    p.dma(GPS, bgsb[:], AP(bg, 0, [[1, NE], [1, 1]]), inc=s_ld)
    p.dma(GPS, b2sb[:], b2.ap(), inc=s_ld)
    static_ld = s_ld.n

    def load_w1(ei, wait):
        for k in range(DC):
            p.dma(GPS, w1sb[:, k * HF:(k + 1) * HF],
                  AP(w1, (ei * D + k * 128) * HF, [[HF, 128], [1, HF]]),
                  wait=(wait if k == 0 else []), inc=s_w1)

    def load_w2(ei, wait):
        for k in range(HC):
            p.dma(GPS, w2sb[:, k * D:(k + 1) * D],
                  AP(w2, (ei * HF + k * 128) * D, [[D, 128], [1, D]]),
                  wait=(wait if k == 0 else []), inc=s_w2)

    g3 = gtok[:].rearrange("p (q e) -> p q e", e=NE)
    m3 = gsc[:].rearrange("p (q e) -> p q e", e=NE)
    k3 = gkeep[:].rearrange("p (q e) -> p q e", e=NE)
    redb = gred[:].unsqueeze(2).broadcast_to([128, 4, NE])

    load_w1(0, [(s_ld, static_ld)])
    load_w2(0, [])
    w1_ready = s_w1.n
    w2_ready = s_w2.n
    prev_ln = None
    for seq in range(NSEQ):
        iw = [(cm.sem, cm.done)]
        if seq > 0:
            iw = [(s_evh, s_evh.n), (s_evv, s_evv.n), (s_mm, s_mm.n),
                  (s_g, s_g.n), (s_st, s_st.n),
                  (prev_ln.s_ln, prev_ln.s_ln.n), (prev_ln.s_st, prev_ln.s_st.n)]
        ln = emit_ln(p, cm, f"dln{seq}", x3, seq * D * T, lnb,
                     (ps_a, ps_b, ps_c, ps_d), xin, xsq, rows, lntmp,
                     init_waits=iw, sems=lnsems)
        prev_ln = ln
        ln_done = [(ln.s_ln, ln.s_ln.n), (ln.s_st, ln.s_st.n)]

        for tb in range(NTB):
            for j in range(DC):
                w = []
                if j == 0:
                    w = list(ln_done) + [(s_ld, static_ld)]
                    if tb > 0 or seq > 0:
                        w.append((s_g, s_g.n))
                p.do(TEN, lambda e, j=j, tb=tb: e.matmul(
                    ps_a[0:NE, :], wgsb[:, j * NE:(j + 1) * NE],
                    lnb[:, j * T + tb * TB: j * T + (tb + 1) * TB],
                    start=(j == 0), stop=(j == DC - 1)), wait=w,
                    inc=(s_mm if j == DC - 1 else None))
            lg_mm = s_mm.n
            p.do(VEC, lambda e: e.tensor_scalar_add(lg_sb[:], ps_a[0:NE, :], bgsb[:]),
                 wait=[(s_mm, lg_mm)], inc=s_g)
            lg_done = s_g.n
            for q in range(4):
                p.do(TEN, lambda e, q=q: e.matmul(
                    ps_tf[:, q * NE:(q + 1) * NE], lg_sb[:, q * 128:(q + 1) * 128],
                    cm.identf[0:NE, 0:NE], is_transpose=True, start=(q == 0), stop=(q == 3)),
                    wait=([(s_g, lg_done)] if q == 0 else []),
                    inc=(s_mm if q == 3 else None))
            tp_done = s_mm.n
            for q in range(4):
                gq = gtok[:, q * NE:(q + 1) * NE]
                kq = gkeep[:, q * NE:(q + 1) * NE]
                mq = gsc[:, q * NE:(q + 1) * NE]
                c1 = gred[:, 0:1]
                c2 = gred[:, 1:2]
                pq = ps_tf[:, q * NE:(q + 1) * NE]
                p.do(VEC, lambda e, pq=pq, c1=c1: e.reduce_max(
                    c1, pq, axis=AX.X, negate=True),
                    wait=[(s_mm, tp_done), (s_g, s_g.n)], inc=s_g)
                p.do(SCA, lambda e, pq=pq, gq=gq, c1=c1: e.activation(
                    out=gq, in_=pq, func=AF.Exp, bias=c1),
                    wait=[(s_g, s_g.n)], inc=s_g)
                p.do(VEC, lambda e, gq=gq, c1=c1: e.reduce_max(c1, gq, axis=AX.X),
                     wait=[(s_g, s_g.n)], inc=s_g)
                p.do(VEC, lambda e, gq=gq, kq=kq, c1=c1: e.tensor_single_scalar(
                    out=kq, in_=gq, scalar=c1, op=ALU.is_ge),
                    wait=[(s_g, s_g.n)], inc=s_g)
                p.do(VEC, lambda e, kq=kq, mq=mq: e.tensor_scalar(
                    out=mq, in0=kq, scalar1=-1.0, scalar2=1.0,
                    op0=ALU.mult, op1=ALU.add), wait=[(s_g, s_g.n)], inc=s_g)
                p.do(VEC, lambda e, mq=mq, gq=gq: e.tensor_mul(mq, mq, gq),
                     wait=[(s_g, s_g.n)], inc=s_g)
                p.do(VEC, lambda e, mq=mq, c2=c2: e.reduce_max(c2, mq, axis=AX.X),
                     wait=[(s_g, s_g.n)], inc=s_g)
                p.do(VEC, lambda e, mq=mq, c2=c2: e.tensor_single_scalar(
                    out=mq, in_=mq, scalar=c2, op=ALU.is_ge),
                    wait=[(s_g, s_g.n)], inc=s_g)
                p.do(VEC, lambda e, kq=kq, mq=mq: e.tensor_add(kq, kq, mq),
                     wait=[(s_g, s_g.n)], inc=s_g)
                p.do(VEC, lambda e, gq=gq, kq=kq: e.tensor_mul(gq, gq, kq),
                     wait=[(s_g, s_g.n)], inc=s_g)
                p.do(VEC, lambda e, gq=gq, c1=c1: e.reduce_sum(c1, gq, axis=AX.X),
                     wait=[(s_g, s_g.n)], inc=s_g)
                p.do(VEC, lambda e, c1=c1: e.reciprocal(c1, c1),
                     wait=[(s_g, s_g.n)], inc=s_g)
                p.do(VEC, lambda e, gq=gq, c1=c1: e.tensor_scalar_mul(gq, gq, c1),
                     wait=[(s_g, s_g.n)], inc=s_g)
            p.do(VEC, lambda e: e.tensor_copy(out=gbf[:], in_=gtok[:]),
                 wait=[(s_g, s_g.n)], inc=s_g)
            gcast = s_g.n
            for q in range(4):
                p.do(TEN, lambda e, q=q: e.matmul(
                    ps_t[0:NE, q * 128:(q + 1) * 128], gbf[:, q * NE:(q + 1) * NE],
                    cm.ident[:], is_transpose=True, start=(q == 0), stop=(q == 3)),
                    wait=([(s_g, gcast)] if q == 0 else []),
                    inc=(s_mm if q == 3 else None))
            p.do(VEC, lambda e, tb=tb: e.tensor_copy(
                out=gwT[:, tb * TB:(tb + 1) * TB], in_=ps_t[0:NE, 0:512]),
                wait=[(s_mm, s_mm.n)], inc=s_g)
        gates_done = s_g.n

        for tb in range(NTB):
            for m in range(DC):
                buf = gb_sb if m % 2 == 0 else tmp2
                p.dma(SYN, buf,
                      AP(x3, (seq * D + m * 128) * T + tb * TB, [[T, 128], [1, TB]]),
                      wait=[(s_st, s_st.n), (s_g, gates_done)], inc=s_x3)
                x3_ld = s_x3.n
                p.do(TEN, lambda e, m=m, tb=tb: e.matmul(
                    ps_d[:, :], b2sb[:, m * 128:(m + 1) * 128],
                    gwT[:, tb * TB:(tb + 1) * TB], start=True, stop=True),
                    wait=[(s_g, gates_done), (s_evv, s_evv.n)], inc=s_mm)
                p.do(VEC, lambda e, buf=buf: e.tensor_add(buf, buf, ps_d[:, :]),
                     wait=[(s_mm, s_mm.n), (s_x3, x3_ld), (s_evv, s_evv.n)],
                     inc=s_evv)
                p.dma(GPS, AP(x4, (seq * D + m * 128) * T + tb * TB,
                              [[T, 128], [1, TB]]), buf,
                      wait=[(s_evv, s_evv.n)], inc=s_st)
        init_done = s_st.n

        evh = {0: 0, 1: 0}
        evv = {0: 0, 1: 0}
        for ei in range(NE):
            p.dma(GPS, b1sb[:], AP(b1, ei * 128 * HC, [[HC, 128], [1, HC]]),
                  wait=[(s_evh, s_evh.n)], inc=s_b1)
            b1_ready = s_b1.n
            w1_next = None
            for tb in range(NTB):
                for m in range(HC):
                    bank = m % 2
                    ps = ps_m[bank]
                    for k in range(DC):
                        w = []
                        if k == 0:
                            w.append((s_evh, evh[bank]))
                            w.append((s_evv, evv[bank]))
                            if m == 0 and tb == 0:
                                w += [(s_w1, w1_ready), (s_g, gates_done)]
                        p.do(TEN, lambda e, m=m, k=k, tb=tb, ps=ps: e.matmul(
                            ps[:, :],
                            w1sb[:, k * HF + m * 128: k * HF + (m + 1) * 128],
                            lnb[:, k * T + tb * TB: k * T + (tb + 1) * TB],
                            start=(k == 0), stop=(k == DC - 1)), wait=w,
                            inc=(s_mm if k == DC - 1 else None))
                    p.do(SCA, lambda e, m=m, ps=ps: e.activation(
                        out=hsb[:, m * TB:(m + 1) * TB], in_=ps[:, :],
                        func=AF.Gelu, bias=b1sb[:, m:m + 1]),
                        wait=[(s_mm, s_mm.n), (s_b1, b1_ready)], inc=s_evh)
                    evh[bank] = s_evh.n
                h_done = s_evh.n
                if tb == NTB - 1 and ei < NE - 1:
                    load_w1(ei + 1, [(s_mm, s_mm.n)])
                    w1_next = s_w1.n
                p.do(TEN, lambda e, ei=ei, tb=tb: e.matmul(
                    ps_c[:, :], cm.sel8[0:NE, ei * 128:(ei + 1) * 128],
                    gwT[:, tb * TB:(tb + 1) * TB], start=True, stop=True),
                    wait=[(s_evh, s_evh.n)], inc=s_mm)
                p.do(SCA, lambda e: e.copy(out=gb_sb[:], in_=ps_c[:, :]),
                     wait=[(s_mm, s_mm.n)], inc=s_evh)
                gb_done = s_evh.n
                for m in range(DC):
                    bank = m % 2
                    ps = ps_m[bank]
                    for k in range(HC):
                        w = []
                        if k == 0:
                            w.append((s_evh, evh[bank]))
                            w.append((s_evv, evv[bank]))
                            if m == 0:
                                w.append((s_evh, h_done))
                                if tb == 0:
                                    w.append((s_w2, w2_ready))
                        p.do(TEN, lambda e, m=m, k=k, tb=tb, ps=ps: e.matmul(
                            ps[:, :],
                            w2sb[:, k * D + m * 128: k * D + (m + 1) * 128],
                            hsb[:, k * TB:(k + 1) * TB],
                            start=(k == 0), stop=(k == HC - 1)), wait=w,
                            inc=(s_mm if k == HC - 1 else None))
                    t2 = tmp2
                    p.do(VEC, lambda e, ps=ps, t2=t2: e.tensor_mul(
                        t2, ps[:, :], gb_sb[:]),
                        wait=[(s_mm, s_mm.n), (s_evh, gb_done), (s_st, s_st.n),
                              (s_evv, s_evv.n)],
                        inc=s_evv)
                    evv[bank] = s_evv.n
                    wr = s_evv.n
                    dst = AP(x4, (seq * D + m * 128) * T + tb * TB, [[T, 128], [1, TB]])
                    p.dma(GPS, dst, t2, wait=[(s_evv, wr), (s_st, s_st.n)],
                          inc=s_st, accum_op=ALU.add)
            if ei < NE - 1:
                load_w2(ei + 1, [(s_mm, s_mm.n)])
                w1_ready = w1_next
                w2_ready = s_w2.n
        if seq < NSEQ - 1:
            load_w1(0, [(s_mm, s_mm.n)])
            w1_ready = s_w1.n
            load_w2(0, [])
            w2_ready = s_w2.n
    p.do(SYN, lambda e: e.nop(), wait=[(s_st, s_st.n)])




GELU_C = 0.7978845608028654
GELU_A = 0.044715


def build_phase_b():
    nc = _mk_nc()
    x1 = nc.dram_tensor("x1", [NSEQ * D, T], F32, kind="ExternalInput")
    alpha = nc.dram_tensor("alpha", [1], F32, kind="ExternalInput")
    wgt = nc.dram_tensor("wgt", [D, NE], F32, kind="ExternalInput")
    bgt = nc.dram_tensor("bgt", [NE], F32, kind="ExternalInput")
    w1m = nc.dram_tensor("w1m", [D, HF], F32, kind="ExternalInput")
    b1m = nc.dram_tensor("b1m", [128, HC], F32, kind="ExternalInput")
    w2m = nc.dram_tensor("w2m", [HF, D], F32, kind="ExternalInput")
    b2m = nc.dram_tensor("b2m", [128, DC], F32, kind="ExternalInput")
    w1e = nc.dram_tensor("w1e", [NE * D, HF], F32, kind="ExternalInput")
    b1e = nc.dram_tensor("b1e", [NE * 128, HC], F32, kind="ExternalInput")
    w2e = nc.dram_tensor("w2e", [NE * HF, D], F32, kind="ExternalInput")
    b2e = nc.dram_tensor("b2e", [NE * 128, DC], F32, kind="ExternalInput")
    x2 = nc.dram_tensor("x2", [NSEQ * D, T], F32, kind="ExternalOutput")
    with ExitStack() as stack:
        p = Prog(nc, stack)
        emit_l0ffn(p, x1, alpha, wgt, bgt, w1m, b1m, w2m, b2m,
                   w1e, b1e, w2e, b2e, x2)
        p.build()
    return nc


def emit_l0ffn(p, x1, alpha, wgt, bgt, w1m, b1m, w2m, b2m,
               w1e, b1e, w2e, b2e, x2, entry=()):
    cm = Common(p, sel8=False, identf=True, pfx="cb", entry=entry)
    lnsems = make_ln_sems(p, "bln")
    s_ld = p.sem("b_ld")
    s_w1 = p.sem("b_w1s")
    s_w2 = p.sem("b_w2s")
    s_x1 = p.sem("b_x1s")
    s_mm = p.sem("b_mm")
    s_gl = [p.sem("b_gl0"), p.sem("b_gl1")]
    s_g = p.sem("b_g")
    s_st = p.sem("b_st")
    s_ix = p.sem("b_ix")

    lnb = p.sbuf("b_ln", [128, DC * T], BF16)
    w1sb = p.sbuf("b_w1", [128, DC * HF], BF16)
    w2sb = p.sbuf("b_w2", [128, HC * D], BF16)
    hsb = p.sbuf("b_h", [128, HC * TB], BF16)
    xin = hsb[:].bitcast(F32)
    garena = p.sbuf("b_garena", [128, T], BF16)
    xsq = garena[:, 0:2 * TB]
    lntmp = garena[:, 2 * TB:4 * TB].bitcast(F32)
    _gs = garena[:, 0:2 * TB].bitcast(F32)
    _gw = garena[:, 2 * TB:4 * TB].bitcast(F32)
    _gt = p.sbuf("b_gt", [128, TB], F32)
    gs = [_gs, _gs]
    gw = [_gw, _gw]
    gt = [_gt, _gt]
    arena = p.sbuf("b_arena", [128, 2 * TB], F32)
    rows = arena[0:1, 0:2 * TB]
    xacc = arena[:, 0:TB]
    tmp2 = arena[:, TB:2 * TB]
    wgsb = p.sbuf("b_wg", [128, DC * NE], F32)
    bgsb = p.sbuf("b_bg", [NE, 1], F32)
    b1msb = p.sbuf("b_b1m", [128, HC], F32)
    b1esb = p.sbuf("b_b1e", [128, HC], F32)
    mu = p.sbuf("b_mu", [128, DC], F32)
    grow = p.sbuf("b_grow", [1, 32], F32)
    lgcol = p.sbuf("b_lgcol", [NE, 1], F32)
    cols = p.sbuf("b_cols", [128, 8], F32)
    btot = p.sbuf("b_btot", [128, DC], F32)
    bxc = p.sbuf("b_bxc", [128, 2 * DC], F32)
    idxs = [p.sbuf("b_idxa", [128, DC + HC + 1], I32),
            p.sbuf("b_idxb", [128, DC + HC + 1], I32)]
    iotaw = p.sbuf("b_iotaw", [128, DC + HC + 1], F32)
    iota8 = p.sbuf("b_iota8", [1, NE], F32)
    idxf = p.sbuf("b_idxf", [128, HC], F32)
    ii1 = p.sbuf("b_ii1", [128, HC], I32)

    ps_a = p.psum("b_psa", [128, TB], F32)
    ps_b = p.psum("b_psb", [128, TB], F32)
    ps_c = p.psum("b_psc", [128, TB], F32)
    ps_d = p.psum("b_psd", [128, TB], F32)
    ps_m = [p.psum("b_pm0", [128, TB], F32), p.psum("b_pm1", [128, TB], F32)]
    ps_tf = p.psum("b_ptf", [128, TB], F32)

    s0 = cm.sem
    p.do(GPS, lambda e: e.iota(ii1[:, 0:DC], pattern=[[128, DC]], base=0,
                               channel_multiplier=1), wait=[(s0, s0.n)], inc=s0)
    p.do(VEC, lambda e: e.tensor_copy(out=iotaw[:, 0:DC], in_=ii1[:, 0:DC]),
         wait=[(s0, s0.n)], inc=s0)
    p.do(GPS, lambda e: e.iota(ii1[:, 0:HC], pattern=[[128, HC]], base=0,
                               channel_multiplier=1), wait=[(s0, s0.n)], inc=s0)
    p.do(VEC, lambda e: e.tensor_copy(out=iotaw[:, DC:DC + HC], in_=ii1[:, 0:HC]),
         wait=[(s0, s0.n)], inc=s0)
    p.do(GPS, lambda e: e.iota(ii1[:, 0:1], pattern=[[0, 1]], base=0,
                               channel_multiplier=1), wait=[(s0, s0.n)], inc=s0)
    p.do(VEC, lambda e: e.tensor_copy(out=iotaw[:, DC + HC:DC + HC + 1],
                                      in_=ii1[:, 0:1]), wait=[(s0, s0.n)], inc=s0)
    p.do(GPS, lambda e: e.iota(ii1[0:1, 0:NE], pattern=[[1, NE]], base=0,
                               channel_multiplier=0), wait=[(s0, s0.n)], inc=s0)
    p.do(VEC, lambda e: e.tensor_copy(out=iota8[:], in_=ii1[0:1, 0:NE]),
         wait=[(s0, s0.n)], inc=s0)
    cdone = s0.n

    for j in range(DC):
        p.dma(GPS, wgsb[:, j * NE:(j + 1) * NE],
              AP(wgt, j * 128 * NE, [[NE, 128], [1, NE]]),
              wait=([(s0, cdone)] if j == 0 else []), inc=s_ld)
    p.dma(GPS, bgsb[:], AP(bgt, 0, [[1, NE], [1, 1]]), inc=s_ld)
    p.dma(GPS, bxc[:, 0:DC], b2m.ap(), inc=s_ld)
    p.dma(GPS, cols[:, 0:1], AP(alpha, 0, [[0, 128], [1, 1]]), inc=s_ld)
    p.dma(GPS, b1msb[:], b1m.ap(), inc=s_ld)
    static_ld = s_ld.n
    p.do(VEC, lambda e: e.tensor_scalar(out=cols[:, 0:1], in0=cols[:, 0:1],
                                        scalar1=0.1, scalar2=1.0, op0=ALU.max,
                                        op1=ALU.min), wait=[(s_ld, static_ld)], inc=s_g)
    p.do(VEC, lambda e: e.tensor_scalar(out=cols[:, 1:2], in0=cols[:, 0:1],
                                        scalar1=-1.0, scalar2=1.0, op0=ALU.mult,
                                        op1=ALU.add), wait=[(s_g, s_g.n)], inc=s_g)

    def load_w1m(wait):
        for k in range(DC):
            p.dma(GPS, w1sb[:, k * HF:(k + 1) * HF],
                  AP(w1m, (k * 128) * HF, [[HF, 128], [1, HF]]),
                  wait=(wait if k == 0 else []), inc=s_w1)

    def load_w2m(wait):
        for k in range(HC):
            p.dma(GPS, w2sb[:, k * D:(k + 1) * D],
                  AP(w2m, (k * 128) * D, [[D, 128], [1, D]]),
                  wait=(wait if k == 0 else []), inc=s_w2)

    def gather_w1e(ix, wait):
        for k in range(DC):
            p.do(GPS, lambda e, k=k, ix=ix: e.indirect_dma_start(
                out=w1sb[:, k * HF:(k + 1) * HF], out_offset=None, in_=w1e.ap(),
                in_offset=bass.IndirectOffsetOnAxis(idxs[ix][:, k:k + 1], 0)),
                wait=(wait if k == 0 else []), inc=s_w1, inc_k=16)

    def gather_w2e(ix, wait):
        for k in range(HC):
            p.do(GPS, lambda e, k=k, ix=ix: e.indirect_dma_start(
                out=w2sb[:, k * D:(k + 1) * D], out_offset=None, in_=w2e.ap(),
                in_offset=bass.IndirectOffsetOnAxis(idxs[ix][:, DC + k:DC + k + 1], 0)),
                wait=(wait if k == 0 else []), inc=s_w2, inc_k=16)

    def emit_idx(i_colidx, ix):
        ic = cols[:, i_colidx:i_colidx + 1]
        sc = cols[:, 6:7]
        for (stride, lo, n) in [(float(D), 0, DC), (float(HF), DC, HC),
                                (128.0, DC + HC, 1)]:
            p.do(VEC, lambda e, stride=stride: e.tensor_scalar_mul(sc, ic, stride),
                 wait=[(s_g, s_g.n)], inc=s_g)
            p.do(VEC, lambda e, lo=lo, n=n: e.tensor_single_scalar(
                out=idxf[:, 0:n], in_=iotaw[:, lo:lo + n], scalar=sc, op=ALU.add),
                wait=[(s_g, s_g.n)], inc=s_g)
            p.do(VEC, lambda e, ix=ix, lo=lo, n=n: e.tensor_copy(
                out=idxs[ix][:, lo:lo + n], in_=idxf[:, 0:n]),
                wait=[(s_g, s_g.n)], inc=s_g)
        return s_g.n

    def ffn_pass(b1col_t, scale_col, first_wait, w1rdy, w2rdy, seq, after_h_last):
        evb = {0: 0, 1: 0}
        for tb in range(NTB):
            for m in range(HC):
                bank = m % 2
                ps = ps_m[bank]
                sg = s_gl[bank]
                sq, gww, gtt = gs[bank], gw[bank], gt[bank]
                bc = b1col_t[:, m:m + 1]
                for k in range(DC):
                    w = []
                    if k == 0:
                        w = [(sg, evb[bank])]
                        if m == 0 and tb == 0:
                            w = w + first_wait + [(s_w1, w1rdy)]
                    p.do(TEN, lambda e, m=m, k=k, tb=tb, ps=ps: e.matmul(
                        ps[:, :], w1sb[:, k * HF + m * 128: k * HF + (m + 1) * 128],
                        lnb[:, k * T + tb * TB: k * T + (tb + 1) * TB],
                        start=(k == 0), stop=(k == DC - 1)), wait=w,
                        inc=(s_mm if k == DC - 1 else None))
                bw = lambda: [(s_gl[0], s_gl[0].n), (s_gl[1], s_gl[1].n)]
                p.do(SCA, lambda e, ps=ps, sq=sq, bc=bc: e.activation(
                    out=sq[:, 0:TB], in_=ps[:, :], func=AF.Square, bias=bc),
                    wait=[(s_mm, s_mm.n)] + bw(), inc=sg)
                p.do(GPS, lambda e, sq=sq, gww=gww: e.tensor_scalar(
                    out=gww[:], in0=sq[:, 0:TB], scalar1=2 * GELU_C * GELU_A,
                    scalar2=2 * GELU_C, op0=ALU.mult, op1=ALU.add),
                    wait=bw(), inc=sg)
                p.do(VEC, lambda e, ps=ps, gww=gww, gtt=gtt, bc=bc:
                     e.scalar_tensor_tensor(
                         out=gtt[:], in0=ps[:, :], scalar=bc, in1=gww[:],
                         op0=ALU.add, op1=ALU.mult), wait=bw(), inc=sg)
                p.do(SCA, lambda e, gtt=gtt: e.activation(
                    out=gtt[:], in_=gtt[:], func=AF.Sigmoid),
                    wait=bw(), inc=sg)
                p.do(VEC, lambda e, ps=ps, gtt=gtt, m=m, bc=bc:
                     e.scalar_tensor_tensor(
                         out=hsb[:, m * TB:(m + 1) * TB], in0=ps[:, :], scalar=bc,
                         in1=gtt[:], op0=ALU.add, op1=ALU.mult),
                     wait=bw(), inc=sg)
                evb[bank] = sg.n
            h_done = [(s_gl[0], s_gl[0].n), (s_gl[1], s_gl[1].n)]
            if tb == NTB - 1 and after_h_last is not None:
                after_h_last()
            for m in range(DC):
                bank = m % 2
                ps = ps_m[bank]
                sg = s_gl[bank]
                for k in range(HC):
                    w = []
                    if k == 0:
                        w = [(sg, evb[bank])]
                        if m == 0:
                            w = w + h_done
                            if tb == 0:
                                w.append((s_w2, w2rdy))
                    p.do(TEN, lambda e, m=m, k=k, tb=tb, ps=ps: e.matmul(
                        ps[:, :], w2sb[:, k * D + m * 128: k * D + (m + 1) * 128],
                        hsb[:, k * TB:(k + 1) * TB],
                        start=(k == 0), stop=(k == HC - 1)), wait=w,
                        inc=(s_mm if k == DC - 1 and False else
                             (s_mm if k == HC - 1 else None)))
                p.do(VEC, lambda e, ps=ps: e.tensor_single_scalar(
                    out=tmp2, in_=ps[:, :], scalar=scale_col, op=ALU.mult),
                    wait=[(s_mm, s_mm.n), (s_st, s_st.n),
                          (s_gl[0], s_gl[0].n), (s_gl[1], s_gl[1].n)], inc=sg)
                evb[bank] = sg.n
                dst = AP(x2, (seq * D + m * 128) * T + tb * TB, [[T, 128], [1, TB]])
                p.dma(GPS, dst, tmp2, wait=[(sg, sg.n), (s_st, s_st.n)],
                      inc=s_st, accum_op=ALU.add)

    load_w1m([(s_ld, static_ld)])
    load_w2m([])
    w1_ready, w2_ready = s_w1.n, s_w2.n
    prev_ln = None
    for seq in range(NSEQ):
        iw = [(cm.sem, cm.done)]
        if seq > 0:
            iw = [(s_gl[0], s_gl[0].n), (s_gl[1], s_gl[1].n), (s_mm, s_mm.n),
                  (s_g, s_g.n), (s_st, s_st.n),
                  (prev_ln.s_ln, prev_ln.s_ln.n), (prev_ln.s_st, prev_ln.s_st.n)]
        ln = emit_ln(p, cm, f"bln{seq}", x1, seq * D * T, lnb,
                     (ps_a, ps_b, ps_c, ps_d), xin, xsq, rows, lntmp,
                     init_waits=iw, sems=lnsems)
        prev_ln = ln

        for j in range(DC):
            p.do(VEC, lambda e, j=j: e.reduce_sum(
                mu[:, j:j + 1], lnb[:, j * T:(j + 1) * T], axis=AX.X),
                wait=[(ln.s_ln, ln.s_ln.n), (s_g, s_g.n)], inc=s_g)
        mu_done = s_g.n
        for j in range(DC):
            p.do(TEN, lambda e, j=j: e.matmul(
                ps_tf[0:NE, 0:1], wgsb[:, j * NE:(j + 1) * NE], mu[:, j:j + 1],
                start=(j == 0), stop=(j == DC - 1)),
                wait=([(s_g, mu_done), (s_ld, static_ld)] if j == 0 else []),
                inc=(s_mm if j == DC - 1 else None))
        p.do(VEC, lambda e: e.tensor_scalar(
            out=lgcol[:], in0=ps_tf[0:NE, 0:1], scalar1=1.0 / T,
            scalar2=bgsb[:], op0=ALU.mult, op1=ALU.add),
            wait=[(s_mm, s_mm.n), (s_g, s_g.n)], inc=s_g)
        p.do(TEN, lambda e: e.matmul(
            ps_d[0:1, 0:NE], lgcol[:], cm.identf[0:NE, 0:NE],
            is_transpose=True, start=True, stop=True),
            wait=[(s_g, s_g.n)], inc=s_mm)
        lg = grow[0:1, 0:NE]
        kq = grow[0:1, 16:16 + NE]
        mq = grow[0:1, 24:24 + NE]
        c1 = grow[0:1, 8:9]
        c2 = grow[0:1, 9:10]
        p.do(VEC, lambda e: e.tensor_copy(out=lg, in_=ps_d[0:1, 0:NE]),
             wait=[(s_mm, s_mm.n), (s_g, s_g.n)], inc=s_g)
        p.do(VEC, lambda e: e.reduce_max(c1, lg, axis=AX.X, negate=True),
             wait=[(s_g, s_g.n)], inc=s_g)
        p.do(SCA, lambda e: e.activation(out=lg, in_=lg, func=AF.Exp, bias=c1),
             wait=[(s_g, s_g.n)], inc=s_g)
        p.do(VEC, lambda e: e.reduce_max(c1, lg, axis=AX.X), wait=[(s_g, s_g.n)],
             inc=s_g)
        p.do(VEC, lambda e: e.tensor_single_scalar(out=kq, in_=lg, scalar=c1,
                                                   op=ALU.is_ge),
             wait=[(s_g, s_g.n)], inc=s_g)
        p.do(VEC, lambda e: e.tensor_scalar(out=mq, in0=kq, scalar1=-1.0,
                                            scalar2=1.0, op0=ALU.mult, op1=ALU.add),
             wait=[(s_g, s_g.n)], inc=s_g)
        p.do(VEC, lambda e: e.tensor_mul(mq, mq, lg), wait=[(s_g, s_g.n)], inc=s_g)
        p.do(VEC, lambda e: e.reduce_max(c2, mq, axis=AX.X), wait=[(s_g, s_g.n)],
             inc=s_g)
        p.do(VEC, lambda e: e.tensor_single_scalar(out=mq, in_=mq, scalar=c2,
                                                   op=ALU.is_ge),
             wait=[(s_g, s_g.n)], inc=s_g)
        p.do(VEC, lambda e: e.tensor_add(grow[0:1, 10:11], c1, c2),
             wait=[(s_g, s_g.n)], inc=s_g)
        p.do(VEC, lambda e: e.reciprocal(grow[0:1, 10:11], grow[0:1, 10:11]),
             wait=[(s_g, s_g.n)], inc=s_g)
        p.do(VEC, lambda e: e.tensor_mul(grow[0:1, 11:12], c1, grow[0:1, 10:11]),
             wait=[(s_g, s_g.n)], inc=s_g)
        p.do(VEC, lambda e: e.tensor_mul(grow[0:1, 12:13], c2, grow[0:1, 10:11]),
             wait=[(s_g, s_g.n)], inc=s_g)
        p.do(VEC, lambda e: e.tensor_mul(kq, kq, iota8[0:1, :]),
             wait=[(s_g, s_g.n)], inc=s_g)
        p.do(VEC, lambda e: e.reduce_sum(grow[0:1, 13:14], kq, axis=AX.X),
             wait=[(s_g, s_g.n)], inc=s_g)
        p.do(VEC, lambda e: e.tensor_mul(mq, mq, iota8[0:1, :]),
             wait=[(s_g, s_g.n)], inc=s_g)
        p.do(VEC, lambda e: e.reduce_sum(grow[0:1, 14:15], mq, axis=AX.X),
             wait=[(s_g, s_g.n)], inc=s_g)
        top2_done = s_g.n
        for n, (src, dstc) in enumerate([(11, 2), (12, 3), (13, 4), (14, 5)]):
            p.do(TEN, lambda e, src=src, dstc=dstc, n=n: e.matmul(
                ps_tf[:, dstc:dstc + 1], cm.ones_rowf[:], grow[0:1, src:src + 1],
                start=(n == 0), stop=(n == 3)),
                wait=([(s_g, top2_done)] if n == 0 else []),
                inc=(s_mm if n == 3 else None))
        p.do(VEC, lambda e: e.tensor_copy(out=cols[:, 2:6], in_=ps_tf[:, 2:6]),
             wait=[(s_mm, s_mm.n), (s_g, s_g.n)], inc=s_g)
        p.do(VEC, lambda e: e.tensor_mul(cols[:, 2:3], cols[:, 2:3], cols[:, 1:2]),
             wait=[(s_g, s_g.n)], inc=s_g)
        p.do(VEC, lambda e: e.tensor_mul(cols[:, 3:4], cols[:, 3:4], cols[:, 1:2]),
             wait=[(s_g, s_g.n)], inc=s_g)
        cols_done = s_g.n

        idx1_done = emit_idx(4, 0)
        idx2_done = emit_idx(5, 1)

        p.do(GPS, lambda e: e.indirect_dma_start(
            out=bxc[:, DC:2 * DC], out_offset=None, in_=b2e.ap(),
            in_offset=bass.IndirectOffsetOnAxis(
                idxs[0][:, DC + HC:DC + HC + 1], 0)),
            wait=[(s_g, idx2_done)], inc=s_ix, inc_k=16)
        p.do(VEC, lambda e: e.tensor_single_scalar(
            out=btot[:], in_=bxc[:, 0:DC], scalar=cols[:, 0:1], op=ALU.mult),
            wait=[(s_ld, static_ld), (s_g, cols_done)], inc=s_g)
        p.do(VEC, lambda e: e.tensor_single_scalar(
            out=bxc[:, DC:2 * DC], in_=bxc[:, DC:2 * DC], scalar=cols[:, 2:3],
            op=ALU.mult), wait=[(s_ix, s_ix.n), (s_g, s_g.n)], inc=s_g)
        p.do(VEC, lambda e: e.tensor_add(btot[:], btot[:], bxc[:, DC:2 * DC]),
             wait=[(s_g, s_g.n)], inc=s_g)
        b2e1_used = s_g.n
        p.do(GPS, lambda e: e.indirect_dma_start(
            out=bxc[:, DC:2 * DC], out_offset=None, in_=b2e.ap(),
            in_offset=bass.IndirectOffsetOnAxis(
                idxs[1][:, DC + HC:DC + HC + 1], 0)),
            wait=[(s_g, b2e1_used)], inc=s_ix, inc_k=16)
        p.do(VEC, lambda e: e.tensor_single_scalar(
            out=bxc[:, DC:2 * DC], in_=bxc[:, DC:2 * DC], scalar=cols[:, 3:4],
            op=ALU.mult), wait=[(s_ix, s_ix.n), (s_g, s_g.n)], inc=s_g)
        p.do(VEC, lambda e: e.tensor_add(btot[:], btot[:], bxc[:, DC:2 * DC]),
             wait=[(s_g, s_g.n)], inc=s_g)
        btot_done = s_g.n

        for tb in range(NTB):
            for m in range(DC):
                p.dma(SYN, xacc,
                      AP(x1, (seq * D + m * 128) * T + tb * TB, [[T, 128], [1, TB]]),
                      wait=[(s_st, s_st.n), (s_g, btot_done)], inc=s_x1)
                p.do(VEC, lambda e, m=m: e.tensor_single_scalar(
                    out=xacc, in_=xacc, scalar=btot[:, m:m + 1], op=ALU.add),
                    wait=[(s_x1, s_x1.n), (s_g, s_g.n)], inc=s_g)
                p.dma(GPS, AP(x2, (seq * D + m * 128) * T + tb * TB,
                              [[T, 128], [1, TB]]), xacc,
                      wait=[(s_g, s_g.n)], inc=s_st)

        def after_main_h():
            gather_w1e(0, [(s_mm, s_mm.n), (s_g, idx1_done)])
        ffn_pass(b1msb, cols[:, 0:1], [(ln.s_ln, ln.s_ln.n), (s_g, btot_done)],
                 w1_ready, w2_ready, seq, after_main_h)
        w1_ready = s_w1.n
        gather_w2e(0, [(s_mm, s_mm.n)])
        w2_ready = s_w2.n
        p.do(GPS, lambda e: e.indirect_dma_start(
            out=b1esb[:], out_offset=None, in_=b1e.ap(),
            in_offset=bass.IndirectOffsetOnAxis(
                idxs[0][:, DC + HC:DC + HC + 1], 0)),
            wait=[(s_gl[0], s_gl[0].n), (s_gl[1], s_gl[1].n)], inc=s_ix, inc_k=16)
        b1e_rd = s_ix.n

        def after_e1_h():
            gather_w1e(1, [(s_mm, s_mm.n), (s_g, idx2_done)])
        ffn_pass(b1esb, cols[:, 2:3], [(s_ix, b1e_rd)],
                 w1_ready, w2_ready, seq, after_e1_h)
        w1_ready = s_w1.n
        gather_w2e(1, [(s_mm, s_mm.n)])
        w2_ready = s_w2.n
        p.do(GPS, lambda e: e.indirect_dma_start(
            out=b1esb[:], out_offset=None, in_=b1e.ap(),
            in_offset=bass.IndirectOffsetOnAxis(
                idxs[1][:, DC + HC:DC + HC + 1], 0)),
            wait=[(s_gl[0], s_gl[0].n), (s_gl[1], s_gl[1].n)], inc=s_ix, inc_k=16)
        b1e_rd = s_ix.n

        def after_e2_h():
            if seq < NSEQ - 1:
                load_w1m([(s_mm, s_mm.n)])
        ffn_pass(b1esb, cols[:, 3:4], [(s_ix, b1e_rd)],
                 w1_ready, w2_ready, seq, after_e2_h)
        w1_ready = s_w1.n
        if seq < NSEQ - 1:
            load_w2m([(s_mm, s_mm.n)])
            w2_ready = s_w2.n
    p.do(SYN, lambda e: e.nop(), wait=[(s_st, s_st.n)])




def build_phase_c():
    nc = _mk_nc()
    x2 = nc.dram_tensor("x2", [NSEQ * D, T], F32, kind="ExternalInput")
    wqt = nc.dram_tensor("wqt", [D, D], F32, kind="ExternalInput")
    wkt = nc.dram_tensor("wkt", [D, D], F32, kind="ExternalInput")
    wvt = nc.dram_tensor("wvt", [D, D], F32, kind="ExternalInput")
    wot = nc.dram_tensor("wot", [D, D], F32, kind="ExternalInput")
    bq = nc.dram_tensor("bq", [128, DC], F32, kind="ExternalInput")
    bk = nc.dram_tensor("bk", [128, DC], F32, kind="ExternalInput")
    bv = nc.dram_tensor("bv", [128, DC], F32, kind="ExternalInput")
    bo = nc.dram_tensor("bo", [128, DC], F32, kind="ExternalInput")
    srcb = nc.dram_tensor("srcb", [3], I32, kind="ExternalInput")
    tgtb = nc.dram_tensor("tgtb", [3], I32, kind="ExternalInput")
    x3 = nc.dram_tensor("x3", [NSEQ * D, T], F32, kind="ExternalOutput")
    mxd = nc.dram_tensor("mxd", [16 * 128, T], BF16,
                         kind="ExternalOutput" if DEBUG_C else "Internal")
    dbg = None
    if DEBUG_C:
        dbg = {n: nc.dram_tensor("dbg_" + n, [128, DC * T], BF16,
                                 kind="ExternalOutput") for n in
               ["qv", "kv", "vv", "yfm"]}
        dbg["sc"] = nc.dram_tensor("dbg_sc", [128, T], F32, kind="ExternalOutput")
        dbg["att"] = nc.dram_tensor("dbg_att", [128, T], BF16, kind="ExternalOutput")
        dbg["attT"] = nc.dram_tensor("dbg_attT", [128, T], BF16, kind="ExternalOutput")
        dbg["vT"] = nc.dram_tensor("dbg_vT", [128, 1024], BF16, kind="ExternalOutput")
        dbg["mx"] = nc.dram_tensor("dbg_mx", [128, 2 * T], BF16, kind="ExternalOutput")
    with ExitStack() as stack:
        p = Prog(nc, stack)
        emit_l1attn(p, x2, wqt, wkt, wvt, wot, bq, bk, bv, bo, srcb, tgtb,
                    x3, mxd, dbg)
        p.build()
    return nc


DEBUG_C = False
DBG_HI = (0, 0)


def emit_l1attn(p, x2, wqt, wkt, wvt, wot, bq, bk, bv, bo, srcb, tgtb, x3, mxd,
                dbg=None, entry=()):
    cm = Common(p, sel8=False, identf=True, pfx="cc", entry=entry)
    lnsems = make_ln_sems(p, "cln")
    s_ld = p.sem("c_ld")
    s_wb = [p.sem("c_ws0"), p.sem("c_ws1")]
    s_x2 = p.sem("c_x2s")
    s_mxp = [p.sem("c_mxs0"), p.sem("c_mxs1")]
    s_mm = p.sem("c_mm")
    s_ev = p.sem("c_ev")
    s_vv = p.sem("c_vv")
    s_g = p.sem("c_g")
    s_st = p.sem("c_st")

    lnb = p.sbuf("c_ln", [128, DC * T], BF16)
    y_fm = lnb
    qv = p.sbuf("c_qv", [128, DC * T], BF16)
    xin = qv[:].bitcast(F32)
    kv = p.sbuf("c_kv", [128, DC * T], BF16)
    vv = p.sbuf("c_vv", [128, DC * T], BF16)
    wful = [p.sbuf("c_w0", [128, DC * 1024], BF16),
            p.sbuf("c_w1", [128, DC * 1024], BF16)]
    mx_sb = p.sbuf("c_mx", [128, 2 * T], BF16)
    garena = p.sbuf("c_garena", [128, T], BF16)
    xsq = garena[:, 0:2 * TB]
    lntmp = garena[:, 2 * TB:4 * TB].bitcast(F32)
    arena = p.sbuf("c_arena", [128, 2 * TB], F32)
    rows = arena[0:1, 0:2 * TB]
    xacc = arena[:, 0:TB]
    sc_sb = p.sbuf("c_sc", [128, T], F32)
    ri = sc_sb[0:32, 0:1024].bitcast(BF16)
    emat = p.sbuf("c_emat", [32, T], BF16)
    att = p.sbuf("c_att", [128, T], BF16)
    attT = p.sbuf("c_attT", [128, T], BF16)
    vT = p.sbuf("c_vT", [128, 16 * 64], BF16)
    stage = p.sbuf("c_stage", [128, 2 * TB], BF16)
    bcols = p.sbuf("c_bcols", [128, 4 * DC], F32)
    zcol = p.sbuf("c_zcol", [128, 2], F32)
    swap64 = p.sbuf("c_swap", [128, 128], BF16)
    bmsb = p.sbuf("c_bm", [32, 32], F32)
    bmb = p.sbuf("c_bmb", [32, 32], BF16)
    srow = p.sbuf("c_srow", [1, 8], F32)
    iota32 = p.sbuf("c_iota32", [32, 1], F32)
    eqs = p.sbuf("c_eqs", [32, 2], F32)
    eqrow = p.sbuf("c_eqrow", [1, 64], F32)
    asb = p.sbuf("c_asb", [128, 32], BF16)
    atsb = p.sbuf("c_atsb", [32, 128], BF16)
    ii = p.sbuf("c_ii", [32, 1], I32)

    ps_a = p.psum("c_psa", [128, TB], F32)
    ps_b = p.psum("c_psb", [128, TB], F32)
    ps_c = p.psum("c_psc", [128, TB], F32)
    ps_d = p.psum("c_psd", [128, TB], F32)
    ps_t = [p.psum("c_pt0", [128, 1024], BF16), p.psum("c_pt1", [128, 1024], BF16)]
    ps_y = p.psum("c_py", [128, TB], F32)
    ps_sw = p.psum("c_psw", [128, TB], F32)

    s0 = cm.sem
    p.do(GPS, lambda e: e.iota(ii[:], pattern=[[0, 1]], base=0,
                               channel_multiplier=1), wait=[(s0, s0.n)], inc=s0)
    p.do(VEC, lambda e: e.tensor_copy(out=iota32[:], in_=ii[:]),
         wait=[(s0, s0.n)], inc=s0)
    p.do(GPS, lambda e: e.memset(swap64[:], 0.0), wait=[(s0, s0.n)], inc=s0)
    p.do(GPS, lambda e: e.affine_select(
        out=swap64[:], in_=swap64[:], compare_op=ALU.not_equal, fill=1.0,
        base=64, pattern=[[-1, 128]], channel_multiplier=1),
        wait=[(s0, s0.n)], inc=s0)
    p.do(GPS, lambda e: e.affine_select(
        out=swap64[:], in_=swap64[:], compare_op=ALU.not_equal, fill=1.0,
        base=-64, pattern=[[-1, 128]], channel_multiplier=1),
        wait=[(s0, s0.n)], inc=s0)
    p.do(GPS, lambda e: e.memset(sc_sb[0:32, :], 0.0), wait=[(s0, s0.n)], inc=s0)
    p.do(GPS, lambda e: e.memset(ri[:, :], 1.0), wait=[(s0, s0.n)], inc=s0)
    p.do(GPS, lambda e: e.affine_select(
        out=ri[:, :], in_=ri[:, :], compare_op=ALU.is_ge, fill=0.0,
        base=0, pattern=[[1, 2048]], channel_multiplier=-64),
        wait=[(s0, s0.n)], inc=s0)
    p.do(GPS, lambda e: e.affine_select(
        out=ri[:, :], in_=ri[:, :], compare_op=ALU.is_ge, fill=0.0,
        base=63, pattern=[[-1, 2048]], channel_multiplier=64),
        wait=[(s0, s0.n)], inc=s0)
    p.do(GPS, lambda e: e.memset(emat[:], 1.0), wait=[(s0, s0.n)], inc=s0)
    p.do(GPS, lambda e: e.affine_select(
        out=emat[:], in_=emat[:], compare_op=ALU.is_ge, fill=0.0,
        base=0, pattern=[[1, T]], channel_multiplier=-64),
        wait=[(s0, s0.n)], inc=s0)
    p.do(GPS, lambda e: e.affine_select(
        out=emat[:], in_=emat[:], compare_op=ALU.is_ge, fill=0.0,
        base=63, pattern=[[-1, T]], channel_multiplier=64),
        wait=[(s0, s0.n)], inc=s0)
    p.do(GPS, lambda e: e.memset(bmsb[:], 1.0), wait=[(s0, s0.n)], inc=s0)
    p.do(GPS, lambda e: e.affine_select(
        out=bmsb[:], in_=bmsb[:], compare_op=ALU.is_ge, fill=0.0, base=1,
        pattern=[[-1, 32]], channel_multiplier=1), wait=[(s0, s0.n)], inc=s0)
    p.do(GPS, lambda e: e.affine_select(
        out=bmsb[:], in_=bmsb[:], compare_op=ALU.is_ge, fill=0.0, base=1,
        pattern=[[1, 32]], channel_multiplier=-1), wait=[(s0, s0.n)], inc=s0)
    p.do(GPS, lambda e: e.affine_select(
        out=bmsb[:], in_=bmsb[:], compare_op=ALU.not_equal, fill=1.0, base=0,
        pattern=[[1, 32]], channel_multiplier=0), wait=[(s0, s0.n)], inc=s0)
    p.do(GPS, lambda e: e.affine_select(
        out=bmsb[:], in_=bmsb[:], compare_op=ALU.not_equal, fill=1.0, base=-31,
        pattern=[[1, 32]], channel_multiplier=0), wait=[(s0, s0.n)], inc=s0)
    cdone = s0.n
    p.do(GPS, lambda e: e.memset(srow[:].bitcast(I32), 0), wait=[(s0, cdone)],
         inc=s0)
    p.dma(GPS, srow[0:1, 0:3].bitcast(I32), AP(srcb, 0, [[3, 1], [1, 3]]),
          wait=[(s0, s0.n)], inc=s_ld)
    p.dma(GPS, srow[0:1, 4:7].bitcast(I32), AP(tgtb, 0, [[3, 1], [1, 3]]), inc=s_ld)
    sT_ld = s_ld.n
    p.do(VEC, lambda e: e.tensor_copy(out=srow[:], in_=srow[:].bitcast(I32)),
         wait=[(s_ld, sT_ld), (s0, cdone)], inc=s_g)
    for k in range(3):
        for n_, col in enumerate([k, 4 + k]):
            p.do(TEN, lambda e, col=col: e.matmul(
                ps_d[0:32, 0:1], cm.ones_rowf[0:1, 0:32], srow[0:1, col:col + 1],
                start=True, stop=True), wait=[(s_g, s_g.n)], inc=s_mm)
            p.do(VEC, lambda e, n_=n_: e.tensor_tensor(
                out=eqs[:, n_:n_ + 1], in0=iota32[:], in1=ps_d[0:32, 0:1],
                op=ALU.is_equal), wait=[(s_mm, s_mm.n), (s_g, s_g.n)], inc=s_g)
        for cc, doff in [(0, 0), (1, 32)]:
            p.do(TEN, lambda e, cc=cc: e.matmul(
                ps_d[0:1, 0:32], eqs[:, cc:cc + 1], cm.identf[0:32, 0:32],
                is_transpose=True, start=True, stop=True),
                wait=[(s_g, s_g.n)], inc=s_mm)
            p.do(VEC, lambda e, doff=doff: e.tensor_copy(
                out=eqrow[0:1, doff:doff + 32], in_=ps_d[0:1, 0:32]),
                wait=[(s_mm, s_mm.n), (s_g, s_g.n)], inc=s_g)
        p.do(TEN, lambda e, k=k: e.matmul(
            ps_c[0:32, 0:32], eqrow[0:1, 0:32], eqrow[0:1, 32:64],
            start=(k == 0), stop=(k == 2)), wait=[(s_g, s_g.n)],
            inc=(s_mm if k == 2 else None))
    p.do(VEC, lambda e: e.tensor_add(bmsb[:], bmsb[:], ps_c[0:32, 0:32]),
         wait=[(s_mm, s_mm.n), (s_g, s_g.n)], inc=s_g)
    p.do(VEC, lambda e: e.tensor_scalar(out=bmsb[:], in0=bmsb[:], scalar1=0.99,
                                        scalar2=None, op0=ALU.is_ge),
         wait=[(s_g, s_g.n)], inc=s_g)
    p.do(VEC, lambda e: e.tensor_scalar(out=bmb[:], in0=bmsb[:], scalar1=1e9,
                                        scalar2=-1e9, op0=ALU.mult, op1=ALU.add),
         wait=[(s_g, s_g.n)], inc=s_g)
    for i in range(16):
        p.do(TEN, lambda e, i=i: e.matmul(
            ps_d[:, 0:32], ri[:, i * 128:(i + 1) * 128], bmb[:],
            start=True, stop=True), wait=[(s_g, s_g.n)], inc=s_mm)
        p.do(VEC, lambda e: e.tensor_copy(out=asb[:], in_=ps_d[:, 0:32]),
             wait=[(s_mm, s_mm.n), (s_g, s_g.n), (s_st, s_st.n)], inc=s_g)
        p.do(TEN, lambda e: e.matmul(
            ps_t[0][0:32, 0:128], asb[:], cm.ident[:], is_transpose=True,
            start=True, stop=True), wait=[(s_g, s_g.n)], inc=s_mm)
        p.do(VEC, lambda e: e.tensor_copy(out=atsb[:], in_=ps_t[0][0:32, 0:128]),
             wait=[(s_mm, s_mm.n), (s_g, s_g.n)], inc=s_g)
        for q in range(4):
            ps = ps_a if q % 2 == 0 else ps_b
            p.do(TEN, lambda e, q=q, ps=ps: e.matmul(
                ps[:, :], atsb[:], emat[:, q * TB:(q + 1) * TB],
                start=True, stop=True), wait=[(s_g, s_g.n)], inc=s_mm)
            p.do(VEC, lambda e, q=q, ps=ps: e.tensor_copy(
                out=att[:, q * TB:(q + 1) * TB], in_=ps[:, :]),
                wait=[(s_mm, s_mm.n), (s_g, s_g.n)], inc=s_g)
        p.dma(GPS, AP(mxd, i * 128 * T, [[T, 128], [1, T]]), att[:, 0:T],
              wait=[(s_g, s_g.n)], inc=s_st)
    mx_built = s_st.n

    def load_w(buf, dram, wait):
        for k in range(DC):
            p.dma(GPS, wful[buf][:, k * 1024:(k + 1) * 1024],
                  AP(dram, (k * 128) * D, [[D, 128], [1, D]]),
                  wait=(wait if k == 0 else []), inc=s_wb[buf])

    prev_ln = None
    proj_ev = {0: 0, 1: 0}
    for seq in range(NSEQ):
        iw = [(cm.sem, cm.done), (s_st, s_st.n), (s_g, s_g.n)]
        if seq > 0:
            iw += [(s_ev, s_ev.n), (s_vv, s_vv.n), (s_mm, s_mm.n),
                   (prev_ln.s_ln, prev_ln.s_ln.n), (prev_ln.s_st, prev_ln.s_st.n)]
        ln = emit_ln(p, cm, f"cln{seq}", x2, seq * D * T, lnb,
                     (ps_a, ps_b, ps_c, ps_d), xin, xsq, rows, lntmp,
                     init_waits=iw, sems=lnsems)
        prev_ln = ln

        if seq == 0:
            p.dma(GPS, bcols[:, 0:DC], bq.ap(), inc=s_ld)
            p.dma(GPS, bcols[:, DC:2 * DC], bk.ap(), inc=s_ld)
            p.dma(GPS, bcols[:, 2 * DC:3 * DC], bv.ap(), inc=s_ld)
            p.dma(GPS, bcols[:, 3 * DC:4 * DC], bo.ap(), inc=s_ld)
        bcols_ld = s_ld.n
        load_w(0, wqt, [(s_mm, s_mm.n), (s_vv, s_vv.n), (s_st, mx_built)])
        w_ready = s_wb[0].n

        evq = {0: 0, 1: 0}
        _next_w = [wkt, wvt, wot]
        for wi, (dst, bofs, scale) in enumerate([(qv, 0, 1.0), (kv, DC, 0.125),
                                                 (vv, 2 * DC, 1.0)]):
            wbuf = wful[wi % 2]
            load_w((wi + 1) % 2, _next_w[wi], [(s_mm, s_mm.n)])
            w_next = s_wb[(wi + 1) % 2].n
            for tb in range(NTB):
                for m in range(DC):
                    bank = m % 2
                    ps = [ps_a, ps_b][bank]
                    for k in range(DC):
                        w = []
                        if k == 0:
                            w = [(s_vv, evq[bank]), (s_ev, s_ev.n)]
                            if m == 0 and tb == 0:
                                w += [(ln.s_ln, ln.s_ln.n),
                                      (s_wb[wi % 2], w_ready),
                                      (s_ld, bcols_ld)]
                        p.do(TEN, lambda e, m=m, k=k, tb=tb, ps=ps, wbuf=wbuf:
                             e.matmul(
                                 ps[:, :],
                                 wbuf[:, k * 1024 + m * 128: k * 1024 + (m + 1) * 128],
                                 lnb[:, k * T + tb * TB: k * T + (tb + 1) * TB],
                                 start=(k == 0), stop=(k == DC - 1)), wait=w,
                             inc=(s_mm if k == DC - 1 else None))
                    mmn = s_mm.n
                    stg = stage[:, bank * TB:(bank + 1) * TB]
                    p.do(SCA, lambda e, ps=ps, stg=stg, bofs=bofs, m=m:
                         e.activation(out=stg, in_=ps[:, :], func=AF.Identity,
                                      bias=bcols[:, bofs + m:bofs + m + 1]),
                         wait=[(s_mm, mmn), (s_ev, s_ev.n), (s_vv, evq[bank])],
                         inc=s_ev)
                    stg_done = s_ev.n
                    p.do(TEN, lambda e, stg=stg: e.matmul(
                        ps_sw[:, :], swap64[:], stg, start=True, stop=True),
                        wait=[(s_ev, stg_done), (s_vv, s_vv.n)], inc=s_mm)
                    swn = s_mm.n
                    for half in range(2):
                        for par in range(2):
                            b = 2 * m + half
                            shifted = (half != par)

                            def mkop(e, stg=stg, half=half, par=par, b=b, tb=tb,
                                     dst=dst, shifted=shifted, scale=scale):
                                if shifted:
                                    src = ps_sw[64 * par:64 * par + 64, :]
                                else:
                                    src = stg[64 * par:64 * par + 64, :]
                                src = src.rearrange("p (a l) -> p a l",
                                                    l=128)[:, par::2, :]
                                out = dst[64 * par:64 * par + 64, :].rearrange(
                                    "p (c a l s) -> p c a l s", c=4, a=2,
                                    l=128)[:, tb, :, :, b]
                                if scale == 1.0:
                                    return e.tensor_copy(out=out, in_=src)
                                return e.tensor_scalar_mul(out, src, scale)
                            p.do(VEC, mkop,
                                 wait=[(s_ev, stg_done), (s_mm, swn),
                                       (s_vv, s_vv.n)], inc=s_vv)
                    evq[bank] = s_vv.n
            w_ready = w_next
        qkv_done = [(s_vv, s_vv.n), (s_mm, s_mm.n)]

        mxld = {0: 0, 1: 0}
        for h in range(16):
            hb = 64 * (h % 2)
            hcol = (h // 2) * 2048
            for c in range(16):
                bank2 = (c // 4) % 2
                p.do(TEN, lambda e, c=c, hb=hb, hcol=hcol, bank2=bank2: e.matmul(
                    ps_t[bank2][0:128, (c % 4) * 64:(c % 4) * 64 + 64],
                    vv[hb:hb + 64, hcol + c * 128:hcol + (c + 1) * 128],
                    cm.ident[hb:hb + 64, hb:hb + 64], is_transpose=True,
                    start=(c % 4 == 0), stop=(c % 4 == 3)),
                    wait=(qkv_done + [(s_ev, s_ev.n)] if c == 0 else
                          ([(s_ev, s_ev.n)] if c % 4 == 0 else [])),
                    inc=(s_mm if c % 4 == 3 else None))
                if c % 4 == 3:
                    grp = c // 4
                    p.do(SCA, lambda e, grp=grp, bank2=bank2: e.copy(
                        out=vT[:, grp * 256:(grp + 1) * 256],
                        in_=ps_t[bank2][0:128, 0:256]),
                        wait=[(s_mm, s_mm.n), (s_ev, s_ev.n)], inc=s_ev)
            vt_done = s_ev.n
            for i in range(16):
                mxbuf = mx_sb[:, (i % 2) * T:(i % 2 + 1) * T]
                s_mx = s_mxp[i % 2]
                p.dma(SYN, mxbuf, AP(mxd, i * 128 * T, [[T, 128], [1, T]]),
                      wait=[(s_vv, mxld[i % 2])] +
                           ([(s_mm, s_mm.n)] if (h == 0 and i <= 1) else []),
                      inc=s_mx)
                mx_rd = s_mx.n
                madd_cnt = {}
                for q in range(4):
                    ps = [ps_a, ps_b][q % 2]
                    wq = []
                    if q <= 1:
                        wq = [(s_vv, s_vv.n), (s_ev, s_ev.n)]
                    else:
                        wq = [(s_vv, madd_cnt[q - 2])]
                    p.do(TEN, lambda e, q=q, ps=ps, hb=hb, hcol=hcol, i=i:
                         e.matmul(
                             ps[:, :],
                             qv[hb:hb + 64, hcol + i * 128:hcol + (i + 1) * 128],
                             kv[hb:hb + 64, hcol + q * TB:hcol + (q + 1) * TB],
                             start=True, stop=True),
                         wait=wq, inc=s_mm)
                    mmq = s_mm.n
                    p.do(VEC, lambda e, q=q, ps=ps, mxbuf=mxbuf: e.tensor_add(
                        sc_sb[:, q * TB:(q + 1) * TB], ps[:, :],
                        mxbuf[:, q * TB:(q + 1) * TB]),
                        wait=[(s_mm, mmq), (s_mx, mx_rd), (s_vv, s_vv.n)] +
                             ([(s_ev, s_ev.n)] if q == 0 else []), inc=s_vv)
                    madd_cnt[q] = s_vv.n
                sc_done = [(s_vv, s_vv.n), (s_ev, s_ev.n)]
                p.do(VEC, lambda e: e.reduce_max(zcol[:, 0:1], sc_sb[:],
                                                 axis=AX.X, negate=True),
                     wait=sc_done, inc=s_vv)
                p.do(SCA, lambda e: e.activation(out=att[:], in_=sc_sb[:],
                                                 func=AF.Exp, bias=zcol[:, 0:1]),
                     wait=[(s_vv, s_vv.n), (s_ev, s_ev.n), (s_mm, s_mm.n)],
                     inc=s_ev)
                p.do(VEC, lambda e: e.reduce_sum(zcol[:, 1:2], att[:], axis=AX.X),
                     wait=[(s_ev, s_ev.n), (s_vv, s_vv.n)], inc=s_vv)
                p.do(VEC, lambda e: e.reciprocal(zcol[:, 1:2], zcol[:, 1:2]),
                     wait=[(s_vv, s_vv.n)], inc=s_vv)
                p.do(VEC, lambda e: e.tensor_single_scalar(
                    out=att[:], in_=att[:], scalar=zcol[:, 1:2], op=ALU.mult),
                    wait=[(s_vv, s_vv.n)], inc=s_vv)
                att_done = s_vv.n
                if dbg is not None and seq == 0 and (h, i) == DBG_HI:
                    p.dma(GPS, dbg["sc"].ap(), sc_sb[:],
                          wait=[(s_vv, att_done)], inc=s_st)
                    p.dma(GPS, dbg["att"].ap(), att[:],
                          wait=[(s_vv, att_done)], inc=s_st)
                    p.dma(GPS, dbg["mx"].ap()[0:128, 0:2 * T].rearrange(
                        "p c -> p c") if False else dbg["mx"].ap(), mx_sb[:],
                        wait=[(s_vv, att_done)], inc=s_st)
                for c in range(16):
                    bk2 = c // 8
                    p.do(TEN, lambda e, c=c, bk2=bk2: e.matmul(
                        ps_t[bk2][:, (c % 8) * 128:(c % 8) * 128 + 128],
                        att[:, c * 128:(c + 1) * 128], cm.ident[:],
                        is_transpose=True, start=(c % 8 == 0), stop=(c % 8 == 7)),
                        wait=([(s_vv, att_done)] if c == 0 else []) +
                             ([(s_ev, s_ev.n), (s_vv, s_vv.n)] if c % 8 == 0 else []),
                        inc=(s_mm if c % 8 == 7 else None))
                    if c % 8 == 7:
                        p.do(SCA, lambda e, bk2=bk2: e.copy(
                            out=attT[:, bk2 * 1024:(bk2 + 1) * 1024],
                            in_=ps_t[bk2][:, :]),
                            wait=[(s_mm, s_mm.n), (s_ev, s_ev.n), (s_vv, s_vv.n)],
                            inc=s_ev)
                attT_done = s_ev.n
                for c in range(16):
                    p.do(TEN, lambda e, c=c, hb=hb: e.matmul(
                        ps_y[hb:hb + 64, 0:128], vT[:, c * 64:(c + 1) * 64],
                        attT[:, c * 128:(c + 1) * 128],
                        start=(c == 0), stop=(c == 15)),
                        wait=([(s_ev, attT_done), (s_vv, s_vv.n)] if c == 0 else []),
                        inc=(s_mm if c == 15 else None))
                p.do(SCA, lambda e, i=i, hb=hb, hcol=hcol: e.copy(
                    out=y_fm[hb:hb + 64, hcol + i * 128:hcol + (i + 1) * 128],
                    in_=ps_y[hb:hb + 64, 0:128]),
                    wait=[(s_mm, s_mm.n), (s_ev, s_ev.n)], inc=s_ev)
                mxld[i % 2] = s_vv.n
        ystrips_done = s_ev.n
        if dbg is not None and seq == 0:
            for nname, buf in [("qv", qv), ("kv", kv), ("vv", vv), ("yfm", y_fm),
                               ("attT", attT), ("vT", vT)]:
                p.dma(GPS, dbg[nname].ap(), buf[:],
                      wait=[(s_ev, ystrips_done), (s_vv, s_vv.n)], inc=s_st)

        wbuf = wful[1]
        for tb in range(NTB):
            for m in range(DC):
                bank = m % 2
                ps = [ps_a, ps_b][bank]
                for k in range(DC):
                    w = []
                    if k == 0:
                        w = [(s_vv, proj_ev[bank])]
                        if m == 0 and tb == 0:
                            w += [(s_ev, ystrips_done), (s_wb[1], s_wb[1].n)]
                    p.do(TEN, lambda e, m=m, k=k, tb=tb, ps=ps, wbuf=wbuf:
                         e.matmul(
                             ps[:, :],
                             wbuf[:, k * 1024 + m * 128: k * 1024 + (m + 1) * 128],
                             y_fm[:, k * T + tb * TB: k * T + (tb + 1) * TB],
                             start=(k == 0), stop=(k == DC - 1)), wait=w,
                         inc=(s_mm if k == DC - 1 else None))
                mmn = s_mm.n
                p.dma(SYN, xacc,
                      AP(x2, (seq * D + m * 128) * T + tb * TB, [[T, 128], [1, TB]]),
                      wait=[(s_vv, s_vv.n), (s_st, s_st.n)], inc=s_x2)
                p.do(VEC, lambda e, ps=ps, m=m: e.scalar_tensor_tensor(
                    out=xacc, in0=ps[:, :],
                    scalar=bcols[:, 3 * DC + m:3 * DC + m + 1],
                    in1=xacc, op0=ALU.add, op1=ALU.add),
                    wait=[(s_mm, mmn), (s_x2, s_x2.n), (s_vv, s_vv.n)], inc=s_vv)
                proj_ev[bank] = s_vv.n
                p.dma(GPS, AP(x3, (seq * D + m * 128) * T + tb * TB,
                              [[T, 128], [1, TB]]), xacc,
                      wait=[(s_vv, s_vv.n)], inc=s_st)
    p.do(SYN, lambda e: e.nop(), wait=[(s_st, s_st.n)])




EXT = 1536
PAD = 1536


def build_phase_a():
    nc = _mk_nc()
    xin = nc.dram_tensor("xin", [NSEQ * D, T], F32, kind="ExternalInput")
    watt = nc.dram_tensor("watt", [D, 3 * D], F32, kind="ExternalInput")
    wproj = nc.dram_tensor("wproj", [D, D], F32, kind="ExternalInput")
    ekt = nc.dram_tensor("ekt", [64, EXT], F32, kind="ExternalInput")
    evr = nc.dram_tensor("evr", [EXT, 64], F32, kind="ExternalInput")
    x1 = nc.dram_tensor("x1", [NSEQ * D, T], F32, kind="ExternalOutput")
    pd = [nc.dram_tensor(f"pd{b}", [128 * EXT + 2 * EXT], BF16, kind="Internal")
          for b in range(2)]
    ad = [nc.dram_tensor(f"ad{b}", [PAD + 128 * EXT + EXT], BF16, kind="Internal")
          for b in range(2)]
    with ExitStack() as stack:
        p = Prog(nc, stack)
        emit_l0attn(p, xin, watt, wproj, ekt, evr, x1, pd, ad)
        p.build()
    return nc


def emit_l0attn(p, xin_d, watt_d, wproj_d, ekt_d, evr_d, x1_d, pd, ad, entry=()):
    cm = Common(p, sel8=False, identf=True, pfx="ca", entry=entry)
    lnsems = make_ln_sems(p, "aln")
    s_ld = p.sem("a_ld")
    s_wp = [p.sem("a_wp0"), p.sem("a_wp1")]
    s_ps = [p.sem("a_ps0"), p.sem("a_ps1")]
    s_bd = [p.sem("a_bd0"), p.sem("a_bd1")]
    s_as = [p.sem("a_as0"), p.sem("a_as1")]
    s_ar = [p.sem("a_ar0"), p.sem("a_ar1")]
    s_x = p.sem("a_xs")
    s_mm = p.sem("a_mm")
    s_ev = p.sem("a_ev")
    s_vv = p.sem("a_vv")
    s_st = p.sem("a_st")

    lnb = p.sbuf("a_ln", [128, DC * T], BF16)
    sc_n = lnb[:, 0:4096].bitcast(F32)
    att = lnb[:, 4096:5632]
    attT = lnb[:, 5632:7168]
    attR = lnb[:, 7168:8704]
    attRT = lnb[:, 8704:10240]
    sc_f = lnb[:, 10240:11264].bitcast(F32)
    attf = lnb[:, 11264:11776]
    attfT = lnb[:, 11776:12288]
    vT = lnb[:, 12288:13312]
    p_sb = lnb[:, 13312:14848]
    band = lnb[:, 14848:16384]
    wasb = p.sbuf("a_watt", [128, DC * 3 * D], BF16)
    yv = wasb[:, 0:DC * T]
    qv = p.sbuf("a_qv", [128, DC * T], BF16)
    kv = p.sbuf("a_kv", [128, DC * T], BF16)
    vv = p.sbuf("a_vv", [128, DC * T], BF16)
    xls = qv[:].bitcast(F32)
    garena = p.sbuf("a_garena", [128, T], BF16)
    xsq = garena[:, 0:2 * TB]
    lntmp = garena[:, 2 * TB:4 * TB].bitcast(F32)
    arena = p.sbuf("a_arena", [128, 2 * TB], F32)
    rows = arena[0:1, 0:2 * TB]
    xacc = arena[:, 0:TB]
    ekb = p.sbuf("a_ekb", [128, EXT], BF16)
    evb = p.sbuf("a_evb", [128, 12 * 64], BF16)
    masks = p.sbuf("a_masks", [128, 4 * TB + TB], BF16)
    zc = p.sbuf("a_zc", [128, 6], F32)
    fmr = p.sbuf("a_fmr", [1, 128], BF16)

    ps_a = p.psum("a_psa", [128, TB], F32)
    ps_b = p.psum("a_psb", [128, TB], F32)
    ps_c = p.psum("a_psc", [128, TB], F32)
    ps_d = p.psum("a_psd", [128, TB], F32)
    ps_t = [p.psum("a_pt0", [128, 1024], BF16), p.psum("a_pt1", [128, 1024], BF16)]
    ps_y = p.psum("a_py", [128, TB], F32)
    ps_tf = p.psum("a_ptf", [128, TB], F32)

    s0 = cm.sem
    p.do(GPS, lambda e: e.memset(masks[:], 0.0), wait=[(s0, s0.n)], inc=s0)
    for v in range(4):
        p.do(GPS, lambda e, v=v: e.affine_select(
            out=masks[:, v * TB:(v + 1) * TB], in_=masks[:, v * TB:(v + 1) * TB],
            compare_op=ALU.is_ge, fill=-1e9, base=128 * v,
            pattern=[[-1, TB]], channel_multiplier=1), wait=[(s0, s0.n)], inc=s0)
    cdone = s0.n
    p.dma(GPS, ekb[0:64, :], ekt_d.ap(), wait=[(s0, cdone)], inc=s_ld)
    p.dma(GPS, ekb[64:128, :], ekt_d.ap(), inc=s_ld)
    for c in range(12):
        p.dma(GPS, evb[:, c * 64:(c + 1) * 64],
              AP(evr_d, c * 128 * 64, [[64, 128], [1, 64]]), inc=s_ld)
    p.do(GPS, lambda e: e.memset(garena[:], 0.0), wait=[(s0, cdone)], inc=s0)
    zdone = s0.n
    for b in range(2):
        p.dma(SYN, AP(pd[b], 128 * EXT, [[EXT, 2], [1, EXT]]), garena[:, 0:EXT * 2]
              if False else garena[0:2, 0:EXT], wait=[(s0, zdone)], inc=s_ps[b])
        p.dma(SYN, AP(ad[b], 0, [[EXT, 1], [1, PAD]]), garena[0:1, 0:PAD],
              wait=[(s0, zdone)], inc=s_as[b])
        p.dma(SYN, AP(ad[b], PAD + 128 * EXT, [[EXT, 1], [1, EXT]]),
              garena[0:1, 0:EXT], wait=[(s0, zdone)], inc=s_as[b])
    z_ps = [s_ps[0].n, s_ps[1].n]
    z_as = [s_as[0].n, s_as[1].n]

    def load_watt(wait):
        for k in range(DC):
            p.dma(GPS, wasb[:, k * 3 * D:(k + 1) * 3 * D],
                  AP(watt_d, (k * 128) * 3 * D, [[3 * D, 128], [1, 3 * D]]),
                  wait=(wait if k == 0 else []), inc=s_wp[0])

    load_watt([(s_ld, s_ld.n)])
    watt_ready = s_wp[0].n
    prev_ln = None
    prev_attn = []
    for seq in range(NSEQ):
        iw = [(cm.sem, cm.done), (s_ld, s_ld.n),
              (s_ps[0], z_ps[0]), (s_ps[1], z_ps[1]),
              (s_as[0], z_as[0]), (s_as[1], z_as[1])]
        if seq > 0:
            iw = [(s_ev, s_ev.n), (s_vv, s_vv.n), (s_mm, s_mm.n), (s_st, s_st.n),
                  (s_ps[0], s_ps[0].n), (s_ps[1], s_ps[1].n),
                  (s_bd[0], s_bd[0].n), (s_bd[1], s_bd[1].n),
                  (s_as[0], s_as[0].n), (s_as[1], s_as[1].n),
                  (s_ar[0], s_ar[0].n), (s_ar[1], s_ar[1].n),
                  (prev_ln.s_ln, prev_ln.s_ln.n), (prev_ln.s_st, prev_ln.s_st.n)]
            load_watt([(s_mm, s_mm.n), (s_x, s_x.n)])
            watt_ready = s_wp[0].n
        ln = emit_ln(p, cm, f"aln{seq}", xin_d, seq * D * T, lnb,
                     (ps_a, ps_b, ps_c, ps_d), xls, xsq, rows, lntmp,
                     init_waits=iw, sems=lnsems)
        prev_ln = ln

        evq = {0: 0, 1: 0}
        for tb in range(NTB):
            for m in range(3 * DC):
                bank = m % 2
                ps = [ps_a, ps_b][bank]
                for k in range(DC):
                    w = []
                    if k == 0:
                        w = [(s_ev, evq[bank])]
                        if m == 0 and tb == 0:
                            w += [(ln.s_ln, ln.s_ln.n), (s_wp[0], watt_ready)]
                    p.do(TEN, lambda e, m=m, k=k, tb=tb, ps=ps: e.matmul(
                        ps[:, :],
                        wasb[:, k * 3 * D + m * 128: k * 3 * D + (m + 1) * 128],
                        lnb[:, k * T + tb * TB: k * T + (tb + 1) * TB],
                        start=(k == 0), stop=(k == DC - 1)), wait=w,
                        inc=(s_mm if k == DC - 1 else None))
                dst = [qv, kv, vv][m // DC]
                mc = m % DC
                scl = 0.125 if (m // DC) == 1 else None
                p.do(SCA, lambda e, ps=ps, dst=dst, mc=mc, tb=tb, scl=scl:
                     (e.mul(out=dst[:, mc * T + tb * TB: mc * T + (tb + 1) * TB],
                            in_=ps[:, :], mul=0.125) if scl else
                      e.copy(out=dst[:, mc * T + tb * TB: mc * T + (tb + 1) * TB],
                             in_=ps[:, :])),
                     wait=[(s_mm, s_mm.n), (s_ev, s_ev.n)], inc=s_ev)
                evq[bank] = s_ev.n
        qkv_done = [(s_ev, s_ev.n), (s_mm, s_mm.n)]
        for kc in range(DC):
            p.dma(GPS, wasb[:, DC * T + kc * 1024:DC * T + (kc + 1) * 1024],
                  AP(wproj_d, (kc * 128) * D, [[D, 128], [1, D]]),
                  wait=([(s_mm, s_mm.n)] if kc == 0 else []), inc=s_wp[1])
        wp_pre = s_wp[1].n

        for h in range(16):
            hb = 64 * (h % 2)
            hcol = (h // 2) * 2048
            for c in range(16):
                bank2 = (c // 4) % 2
                p.do(TEN, lambda e, c=c, hb=hb, hcol=hcol, bank2=bank2: e.matmul(
                    ps_t[bank2][0:128, (c % 4) * 64:(c % 4) * 64 + 64],
                    vv[hb:hb + 64, hcol + c * 128:hcol + (c + 1) * 128],
                    cm.ident[hb:hb + 64, hb:hb + 64], is_transpose=True,
                    start=(c % 4 == 0), stop=(c % 4 == 3)),
                    wait=(qkv_done + [(s_vv, s_vv.n)] if c == 0 else
                          ([(s_vv, s_vv.n)] if c % 4 == 0 else [])),
                    inc=(s_mm if c % 4 == 3 else None))
                if c % 4 == 3:
                    grp = c // 4
                    p.do(VEC, lambda e, grp=grp, bank2=bank2: e.tensor_copy(
                        out=vT[:, grp * 256:(grp + 1) * 256],
                        in_=ps_t[bank2][0:128, 0:256]),
                        wait=[(s_mm, s_mm.n), (s_vv, s_vv.n)], inc=s_vv)
            vt_done = s_vv.n
            for i in range(16):
                t0 = i * 128
                B0 = 1 if t0 >= 1536 else 0
                smin = B0 * 512
                live = (t0 - smin) // 512 + 1
                has_far = (B0 == 1)
                par = i % 2
                q_sl = qv[hb:hb + 64, hcol + i * 128:hcol + (i + 1) * 128]
                for b3 in range(3):
                    ps = [ps_a, ps_b][b3 % 2]
                    p.do(TEN, lambda e, b3=b3, ps=ps, q_sl=q_sl, hb=hb: e.matmul(
                        ps[:, :], q_sl, ekb[hb:hb + 64, b3 * TB:(b3 + 1) * TB],
                        start=True, stop=True),
                        wait=[(s_vv, s_vv.n), (s_ev, s_ev.n)] if b3 <= 1 else
                             [(s_ev, s_ev.n)],
                        inc=s_mm)
                    p.do(SCA, lambda e, b3=b3, ps=ps: e.copy(
                        out=p_sb[:, b3 * TB:(b3 + 1) * TB], in_=ps[:, :]),
                        wait=[(s_mm, s_mm.n), (s_ev, s_ev.n)], inc=s_ev)
                p_done = s_ev.n
                p.dma(SYN, AP(pd[par], 0, [[EXT, 128], [1, EXT]]), p_sb[:],
                      wait=[(s_ev, p_done), (s_bd[par], s_bd[par].n)], inc=s_ps[par])
                p.dma(GPS, band[:],
                      AP(pd[par], EXT - 1 - t0 + smin, [[EXT - 1, 128], [1, EXT]]),
                      wait=[(s_ps[par], s_ps[par].n)], inc=s_bd[par])
                band_rd = s_bd[par].n
                for b3 in range(3):
                    ps = [ps_a, ps_b][b3 % 2]
                    s_abs = smin + b3 * 512
                    p.do(TEN, lambda e, b3=b3, ps=ps, q_sl=q_sl, hb=hb,
                         hcol=hcol, s_abs=s_abs: e.matmul(
                             ps[:, :], q_sl,
                             kv[hb:hb + 64, hcol + s_abs:hcol + s_abs + TB],
                             start=True, stop=True),
                         wait=[(s_vv, s_vv.n), (s_ev, s_ev.n)] if b3 <= 1 else
                              [(s_ev, s_ev.n)], inc=s_mm)
                    mmb = s_mm.n
                    if b3 == live - 1:
                        mv = (t0 - smin - b3 * 512) // 128
                        p.do(VEC, lambda e, b3=b3, ps=ps, mv=mv: e.tensor_add(
                            sc_n[:, b3 * TB:(b3 + 1) * TB], ps[:, :],
                            band[:, b3 * TB:(b3 + 1) * TB]),
                            wait=[(s_mm, mmb), (s_bd[par], band_rd),
                                  (s_vv, s_vv.n)], inc=s_vv)
                        p.do(VEC, lambda e, b3=b3, mv=mv: e.tensor_add(
                            sc_n[:, b3 * TB:(b3 + 1) * TB],
                            sc_n[:, b3 * TB:(b3 + 1) * TB],
                            masks[:, mv * TB:(mv + 1) * TB]),
                            wait=[(s_vv, s_vv.n)], inc=s_vv)
                    elif b3 < live:
                        p.do(VEC, lambda e, b3=b3, ps=ps: e.tensor_add(
                            sc_n[:, b3 * TB:(b3 + 1) * TB], ps[:, :],
                            band[:, b3 * TB:(b3 + 1) * TB]),
                            wait=[(s_mm, mmb), (s_bd[par], band_rd),
                                  (s_vv, s_vv.n)], inc=s_vv)
                    else:
                        p.do(VEC, lambda e, b3=b3, ps=ps: e.tensor_scalar(
                            out=sc_n[:, b3 * TB:(b3 + 1) * TB], in0=ps[:, :],
                            scalar1=0.0, scalar2=-1e9, op0=ALU.mult, op1=ALU.add),
                            wait=[(s_mm, mmb), (s_vv, s_vv.n)], inc=s_vv)
                if has_far:
                    p.do(TEN, lambda e, q_sl=q_sl, hb=hb, hcol=hcol: e.matmul(
                        ps_c[:, :], q_sl, kv[hb:hb + 64, hcol:hcol + TB],
                        start=True, stop=True), wait=[(s_vv, s_vv.n)], inc=s_mm)
                    p.do(VEC, lambda e: e.tensor_copy(out=zc[:, 4:5],
                                                      in_=p_sb[:, 512:513]),
                         wait=[(s_ev, p_done), (s_vv, s_vv.n)], inc=s_vv)
                    p.do(VEC, lambda e: e.tensor_single_scalar(
                        out=sc_f[:, 0:TB], in_=ps_c[:, :],
                        scalar=zc[:, 4:5], op=ALU.add),
                        wait=[(s_mm, s_mm.n), (s_vv, s_vv.n)],
                        inc=s_vv)
                p.do(VEC, lambda e: e.reduce_max(zc[:, 0:1], sc_n[:, 0:1536],
                                                 axis=AX.X, negate=True),
                     wait=[(s_vv, s_vv.n)], inc=s_vv)
                if has_far:
                    p.do(VEC, lambda e: e.reduce_max(zc[:, 1:2], sc_f[:, 0:TB],
                                                     axis=AX.X, negate=True),
                         wait=[(s_vv, s_vv.n)], inc=s_vv)
                    p.do(VEC, lambda e: e.tensor_tensor(
                        out=zc[:, 0:1], in0=zc[:, 0:1], in1=zc[:, 1:2],
                        op=ALU.min), wait=[(s_vv, s_vv.n)], inc=s_vv)
                nm_done = s_vv.n
                p.do(SCA, lambda e: e.activation(out=att[:], in_=sc_n[:, 0:1536],
                                                 func=AF.Exp, bias=zc[:, 0:1]),
                     wait=[(s_vv, nm_done), (s_ev, s_ev.n), (s_mm, s_mm.n),
                           (s_as[par], s_as[par].n)], inc=s_ev)
                if has_far:
                    p.do(SCA, lambda e: e.activation(out=attf[:], in_=sc_f[:, 0:TB],
                                                     func=AF.Exp, bias=zc[:, 0:1]),
                         wait=[(s_ev, s_ev.n)], inc=s_ev)
                exp_done = s_ev.n
                p.do(VEC, lambda e: e.reduce_sum(zc[:, 2:3], att[:], axis=AX.X),
                     wait=[(s_ev, exp_done), (s_vv, s_vv.n)], inc=s_vv)
                if has_far:
                    p.do(VEC, lambda e: e.reduce_sum(zc[:, 3:4], attf[:], axis=AX.X),
                         wait=[(s_vv, s_vv.n)], inc=s_vv)
                    p.do(VEC, lambda e: e.tensor_add(zc[:, 2:3], zc[:, 2:3],
                                                     zc[:, 3:4]),
                         wait=[(s_vv, s_vv.n)], inc=s_vv)
                p.do(VEC, lambda e: e.reciprocal(zc[:, 2:3], zc[:, 2:3]),
                     wait=[(s_vv, s_vv.n)], inc=s_vv)
                p.do(VEC, lambda e: e.tensor_single_scalar(
                    out=att[:], in_=att[:], scalar=zc[:, 2:3], op=ALU.mult),
                    wait=[(s_vv, s_vv.n)], inc=s_vv)
                if has_far:
                    p.do(VEC, lambda e: e.tensor_single_scalar(
                        out=attf[:], in_=attf[:], scalar=zc[:, 2:3], op=ALU.mult),
                        wait=[(s_vv, s_vv.n)], inc=s_vv)
                att_done = s_vv.n
                p.dma(SYN, AP(ad[par], PAD, [[EXT, 128], [1, EXT]]), att[:],
                      wait=[(s_vv, att_done), (s_ar[par], s_ar[par].n)],
                      inc=s_as[par])
                p.dma(GPS, attR[:],
                      AP(ad[par], PAD + 1 + t0 - smin - 1536,
                         [[EXT + 1, 128], [1, EXT]]),
                      wait=[(s_as[par], s_as[par].n)], inc=s_ar[par])
                ar_rd = s_ar[par].n
                for c in range(12):
                    bk2 = c // 8
                    p.do(TEN, lambda e, c=c, bk2=bk2: e.matmul(
                        ps_t[bk2][:, (c % 8) * 128:(c % 8) * 128 + 128],
                        att[:, c * 128:(c + 1) * 128], cm.ident[:],
                        is_transpose=True, start=(c % 8 == 0),
                        stop=(c % 8 == 7 or c == 11)),
                        wait=([(s_vv, att_done)] if c == 0 else []) +
                             ([(s_vv, s_vv.n)] if c % 8 == 0 else []),
                        inc=(s_mm if (c % 8 == 7 or c == 11) else None))
                    if c % 8 == 7 or c == 11:
                        n_c = 1024 if c == 7 else 512
                        p.do(VEC, lambda e, bk2=bk2, c=c, n_c=n_c: e.tensor_copy(
                            out=attT[:, bk2 * 1024:bk2 * 1024 + n_c],
                            in_=ps_t[bk2][:, 0:n_c]),
                            wait=[(s_mm, s_mm.n), (s_vv, s_vv.n)], inc=s_vv)
                attT_done = s_vv.n
                if has_far:
                    for c in range(4):
                        p.do(TEN, lambda e, c=c: e.matmul(
                            ps_t[0][:, c * 128:(c + 1) * 128],
                            attf[:, c * 128:(c + 1) * 128], cm.ident[:],
                            is_transpose=True, start=(c == 0), stop=(c == 3)),
                            wait=[(s_vv, attT_done)] if c == 0 else [],
                            inc=(s_mm if c == 3 else None))
                    p.do(VEC, lambda e: e.tensor_copy(out=attfT[:],
                                                      in_=ps_t[0][:, 0:512]),
                         wait=[(s_mm, s_mm.n), (s_vv, s_vv.n)], inc=s_vv)
                    p.do(VEC, lambda e: e.reduce_sum(zc[:, 3:4], attf[:],
                                                     axis=AX.X),
                         wait=[(s_vv, s_vv.n)], inc=s_vv)
                    p.do(TEN, lambda e: e.matmul(
                        ps_tf[0:1, 0:128], zc[:, 3:4], cm.identf[:],
                        is_transpose=True, start=True, stop=True),
                        wait=[(s_vv, s_vv.n)], inc=s_mm)
                    p.do(VEC, lambda e: e.tensor_copy(out=fmr[:],
                                                      in_=ps_tf[0:1, 0:128]),
                         wait=[(s_mm, s_mm.n), (s_vv, s_vv.n)], inc=s_vv)
                aux_done = s_vv.n
                for c in range(12):
                    bk2 = c // 8
                    p.do(TEN, lambda e, c=c, bk2=bk2: e.matmul(
                        ps_t[bk2][:, (c % 8) * 128:(c % 8) * 128 + 128],
                        attR[:, c * 128:(c + 1) * 128], cm.ident[:],
                        is_transpose=True, start=(c % 8 == 0),
                        stop=(c % 8 == 7 or c == 11)),
                        wait=([(s_ar[par], ar_rd), (s_vv, aux_done)] if c == 0
                              else []) +
                             ([(s_vv, s_vv.n)] if c % 8 == 0 and c > 0 else []),
                        inc=(s_mm if (c % 8 == 7 or c == 11) else None))
                    if c % 8 == 7 or c == 11:
                        n_c = 1024 if c == 7 else 512
                        p.do(VEC, lambda e, bk2=bk2, n_c=n_c: e.tensor_copy(
                            out=attRT[:, bk2 * 1024:bk2 * 1024 + n_c],
                            in_=ps_t[bk2][:, 0:n_c]),
                            wait=[(s_mm, s_mm.n), (s_vv, s_vv.n)], inc=s_vv)
                attRT_done = s_vv.n
                nmm = []
                for c in range(12):
                    cab = B0 * 4 + c // 1 if False else B0 * 4 + c
                    nmm.append(("near", c, cab))
                if has_far:
                    for c in range(4):
                        nmm.append(("far", c, c))
                for c in range(12):
                    nmm.append(("rel", c, c))
                if has_far:
                    nmm.append(("fout", 0, 0))
                nt = len(nmm)
                for idx, (kind, c, cab) in enumerate(nmm):
                    def mk(e, kind=kind, c=c, cab=cab, idx=idx, hb=hb, nt=nt):
                        st = (idx == 0)
                        sp = (idx == nt - 1)
                        if kind == "near":
                            return e.matmul(ps_y[hb:hb + 64, 0:128],
                                            vT[:, cab * 64:(cab + 1) * 64],
                                            attT[:, c * 128:(c + 1) * 128],
                                            start=st, stop=sp)
                        if kind == "far":
                            return e.matmul(ps_y[hb:hb + 64, 0:128],
                                            vT[:, cab * 64:(cab + 1) * 64],
                                            attfT[:, c * 128:(c + 1) * 128],
                                            start=st, stop=sp)
                        if kind == "rel":
                            return e.matmul(ps_y[hb:hb + 64, 0:128],
                                            evb[:, c * 64:(c + 1) * 64],
                                            attRT[:, c * 128:(c + 1) * 128],
                                            start=st, stop=sp)
                        return e.matmul(ps_y[hb:hb + 64, 0:128],
                                        evb[0:1, 256:320], fmr[:],
                                        start=st, stop=sp)
                    p.do(TEN, mk,
                         wait=([(s_vv, attRT_done), (s_ev, s_ev.n)]
                               if idx == 0 else []),
                         inc=(s_mm if idx == nt - 1 else None))
                p.do(SCA, lambda e, i=i, hb=hb, hcol=hcol: e.copy(
                    out=yv[hb:hb + 64, hcol + i * 128:hcol + (i + 1) * 128],
                    in_=ps_y[hb:hb + 64, 0:128]),
                    wait=[(s_mm, s_mm.n), (s_ev, s_ev.n)], inc=s_ev)
        y_done = s_ev.n

        pj_ev = {0: 0, 1: 0}
        for tb in range(NTB):
            for m in range(DC):
                bank = m % 2
                ps = [ps_a, ps_b][bank]
                for k in range(DC):
                    w = []
                    if k == 0:
                        w = [(s_vv, pj_ev[bank])]
                        if m == 0 and tb == 0:
                            w += [(s_ev, y_done), (s_wp[1], s_wp[1].n)]
                    p.do(TEN, lambda e, m=m, k=k, tb=tb, ps=ps: e.matmul(
                        ps[:, :],
                        wasb[:, DC * T + k * 1024 + m * 128:
                             DC * T + k * 1024 + (m + 1) * 128],
                        yv[:, k * T + tb * TB: k * T + (tb + 1) * TB],
                        start=(k == 0), stop=(k == DC - 1)), wait=w,
                        inc=(s_mm if k == DC - 1 else None))
                mmn = s_mm.n
                p.dma(SYN, xacc,
                      AP(xin_d, (seq * D + m * 128) * T + tb * TB,
                         [[T, 128], [1, TB]]),
                      wait=[(s_vv, s_vv.n), (s_st, s_st.n)], inc=s_x)
                p.do(VEC, lambda e, ps=ps: e.tensor_add(xacc, ps[:, :], xacc),
                     wait=[(s_mm, mmn), (s_x, s_x.n), (s_vv, s_vv.n)], inc=s_vv)
                pj_ev[bank] = s_vv.n
                p.dma(GPS, AP(x1_d, (seq * D + m * 128) * T + tb * TB,
                              [[T, 128], [1, TB]]), xacc,
                      wait=[(s_vv, s_vv.n), (s_st, s_st.n)], inc=s_st)
    p.do(SYN, lambda e: e.nop(), wait=[(s_st, s_st.n)])


def _feat_major(x):
    return np.ascontiguousarray(x.transpose(0, 2, 1)).reshape(-1, x.shape[1])


def _b1_cols(b):
    ne, hf = b.shape
    return np.ascontiguousarray(
        b.reshape(ne, hf // 128, 128).transpose(0, 2, 1)).reshape(ne * 128, hf // 128)




def build_fused():
    nc = _mk_nc()
    xin = nc.dram_tensor("xin", [NSEQ * D, T], F32, kind="ExternalInput")
    watt = nc.dram_tensor("watt", [D, 3 * D], F32, kind="ExternalInput")
    wproj = nc.dram_tensor("wproj", [D, D], F32, kind="ExternalInput")
    ekt = nc.dram_tensor("ekt", [64, EXT], F32, kind="ExternalInput")
    evr = nc.dram_tensor("evr", [EXT, 64], F32, kind="ExternalInput")
    x1 = nc.dram_tensor("x1", [NSEQ * D, T], F32, kind="Internal")
    pdn = [nc.dram_tensor(f"pd{b}", [128 * EXT + 2 * EXT], BF16, kind="Internal")
           for b in range(2)]
    adn = [nc.dram_tensor(f"ad{b}", [PAD + 128 * EXT + EXT], BF16, kind="Internal")
           for b in range(2)]
    alpha = nc.dram_tensor("alpha", [1], F32, kind="ExternalInput")
    wgt = nc.dram_tensor("wgt", [D, NE], F32, kind="ExternalInput")
    bgt = nc.dram_tensor("bgt", [NE], F32, kind="ExternalInput")
    w1m = nc.dram_tensor("w1m", [D, HF], F32, kind="ExternalInput")
    b1m = nc.dram_tensor("b1m", [128, HC], F32, kind="ExternalInput")
    w2m = nc.dram_tensor("w2m", [HF, D], F32, kind="ExternalInput")
    b2m = nc.dram_tensor("b2m", [128, DC], F32, kind="ExternalInput")
    w1e = nc.dram_tensor("w1e", [NE * D, HF], F32, kind="ExternalInput")
    b1e = nc.dram_tensor("b1e", [NE * 128, HC], F32, kind="ExternalInput")
    w2e = nc.dram_tensor("w2e", [NE * HF, D], F32, kind="ExternalInput")
    b2e = nc.dram_tensor("b2e", [NE * 128, DC], F32, kind="ExternalInput")
    x2 = nc.dram_tensor("x2", [NSEQ * D, T], F32, kind="Internal")
    wqt = nc.dram_tensor("wqt", [D, D], F32, kind="ExternalInput")
    wkt = nc.dram_tensor("wkt", [D, D], F32, kind="ExternalInput")
    wvt = nc.dram_tensor("wvt", [D, D], F32, kind="ExternalInput")
    wot = nc.dram_tensor("wot", [D, D], F32, kind="ExternalInput")
    bq = nc.dram_tensor("bq", [128, DC], F32, kind="ExternalInput")
    bk = nc.dram_tensor("bk", [128, DC], F32, kind="ExternalInput")
    bv = nc.dram_tensor("bv", [128, DC], F32, kind="ExternalInput")
    bo = nc.dram_tensor("bo", [128, DC], F32, kind="ExternalInput")
    srcb = nc.dram_tensor("srcb", [3], I32, kind="ExternalInput")
    tgtb = nc.dram_tensor("tgtb", [3], I32, kind="ExternalInput")
    x3 = nc.dram_tensor("x3", [NSEQ * D, T], F32, kind="Internal")
    mxd = nc.dram_tensor("mxd", [16 * 128, T], BF16, kind="Internal")
    wg = nc.dram_tensor("wg", [D, NE], F32, kind="ExternalInput")
    bg = nc.dram_tensor("bg", [NE], F32, kind="ExternalInput")
    w1 = nc.dram_tensor("w1", [NE * D, HF], F32, kind="ExternalInput")
    b1 = nc.dram_tensor("b1", [NE * 128, HC], F32, kind="ExternalInput")
    w2 = nc.dram_tensor("w2", [NE * HF, D], F32, kind="ExternalInput")
    b2 = nc.dram_tensor("b2", [NE, D], F32, kind="ExternalInput")
    x4 = nc.dram_tensor("x4", [NSEQ * D, T], F32, kind="ExternalOutput")

    with ExitStack() as sems_stack:
        p = Prog(nc, None, sem_stack=sems_stack)
        p._all_sems = []
        _orig_sem = p.sem

        def sem_track(name):
            s = _orig_sem(name)
            p._all_sems.append(s)
            return s
        p.sem = sem_track

        def all_sem_totals():
            return [(s, s.n) for s in p._all_sems]

        p.stack = sems_stack
        cv_f = p.sbuf("cv_f", [128, 4], F32)
        cv_b = p.sbuf("cv_b", [128, 4], BF16)
        s_cw = p.sem("cv_sem")
        for i, val in enumerate([1.0, 0.0, -1e9]):
            p.do(GPS, lambda e, i=i, val=val: e.memset(cv_f[:, i:i + 1], val),
                 wait=[(s_cw, s_cw.n)], inc=s_cw)
            p.do(GPS, lambda e, i=i, val=val: e.memset(cv_b[:, i:i + 1], val),
                 wait=[(s_cw, s_cw.n)], inc=s_cw)
            nc.const_aps.register(F32, val, cv_f[:, i:i + 1])
            nc.const_aps.register(BF16, val, cv_b[:, i:i + 1])
        entry0 = [(s_cw, s_cw.n)]

        with nc.Block() as block:
            p.flush(block)

            def phase(emit, *args, entry=()):
                with ExitStack() as st:
                    p.stack = st
                    emit(p, *args, entry=entry)
                    for eng in ENGINES:
                        if p.ops[eng]:
                            waits, fn, inc_h, inc_k = p.ops[eng][0]
                            p.ops[eng][0] = (tuple(entry) + waits, fn,
                                             inc_h, inc_k)
                    p.flush(block)

            phase(emit_l0attn, xin, watt, wproj, ekt, evr, x1, pdn, adn,
                  entry=entry0)
            e1 = all_sem_totals()
            phase(emit_l0ffn, x1, alpha, wgt, bgt, w1m, b1m, w2m, b2m,
                  w1e, b1e, w2e, b2e, x2, entry=e1)
            e2 = all_sem_totals()
            phase(emit_l1attn, x2, wqt, wkt, wvt, wot, bq, bk, bv, bo, srcb,
                  tgtb, x3, mxd, None, entry=e2)
            e3 = all_sem_totals()
            phase(emit_l1ffn, x3, wg, bg, w1, b1, w2, b2, x4, entry=e3)
    return nc


def _bias_cols(b):
    return np.ascontiguousarray(b.reshape(-1, 128).T)


def kernel(L0_attn_x, L0_attn_w_attn, L0_attn_w_proj, L0_attn_embk, L0_attn_embv,
           L0_ffn_alpha, L0_ffn_w_gate, L0_ffn_b_gate, L0_ffn_W1m, L0_ffn_b1m,
           L0_ffn_W2m, L0_ffn_b2m, L0_ffn_W1e, L0_ffn_b1e, L0_ffn_W2e, L0_ffn_b2e,
           L1_attn_Wq, L1_attn_bq, L1_attn_Wk, L1_attn_bk, L1_attn_Wv, L1_attn_bv,
           L1_attn_Wo, L1_attn_bo, L1_attn_src_blocks, L1_attn_tgt_blocks,
           L1_ffn_Wg, L1_ffn_bg, L1_ffn_W1, L1_ffn_b1, L1_ffn_W2, L1_ffn_b2):
    ncore = 8
    x = np.asarray(L0_attn_x, dtype=np.float32)

    rho = np.arange(EXT)
    rr = np.minimum(EXT - 1 - rho, 1023)
    ekt = np.ascontiguousarray(np.asarray(L0_attn_embk, np.float32)[rr].T)
    evr = np.ascontiguousarray(np.asarray(L0_attn_embv, np.float32)[rr])
    watt = np.ascontiguousarray(np.asarray(L0_attn_w_attn, np.float32))
    wproj = np.ascontiguousarray(np.asarray(L0_attn_w_proj, np.float32))
    alpha = np.asarray(L0_ffn_alpha, np.float32).reshape(1)
    wgt = np.ascontiguousarray(np.asarray(L0_ffn_w_gate, np.float32))
    bgt = np.ascontiguousarray(np.asarray(L0_ffn_b_gate, np.float32))
    w1m = np.ascontiguousarray(np.asarray(L0_ffn_W1m, np.float32))
    b1m = _bias_cols(np.asarray(L0_ffn_b1m, np.float32))
    w2m = np.ascontiguousarray(np.asarray(L0_ffn_W2m, np.float32))
    b2m = _bias_cols(np.asarray(L0_ffn_b2m, np.float32))
    w1e = np.ascontiguousarray(np.asarray(L0_ffn_W1e, np.float32).reshape(NE * D, HF))
    b1e = _b1_cols(np.asarray(L0_ffn_b1e, np.float32))
    w2e = np.ascontiguousarray(np.asarray(L0_ffn_W2e, np.float32).reshape(NE * HF, D))
    b2e = np.ascontiguousarray(
        np.asarray(L0_ffn_b2e, np.float32).reshape(NE, DC, 128)
        .transpose(0, 2, 1)).reshape(NE * 128, DC)
    wqt = np.ascontiguousarray(np.asarray(L1_attn_Wq, np.float32).T)
    wkt = np.ascontiguousarray(np.asarray(L1_attn_Wk, np.float32).T)
    wvt = np.ascontiguousarray(np.asarray(L1_attn_Wv, np.float32).T)
    wot = np.ascontiguousarray(np.asarray(L1_attn_Wo, np.float32).T)
    bq = _bias_cols(np.asarray(L1_attn_bq, np.float32))
    bk = _bias_cols(np.asarray(L1_attn_bk, np.float32))
    bv = _bias_cols(np.asarray(L1_attn_bv, np.float32))
    bo = _bias_cols(np.asarray(L1_attn_bo, np.float32))
    srcb = np.asarray(L1_attn_src_blocks, np.int32)
    tgtb = np.asarray(L1_attn_tgt_blocks, np.int32)
    wg = np.ascontiguousarray(np.asarray(L1_ffn_Wg, np.float32))
    bg = np.ascontiguousarray(np.asarray(L1_ffn_bg, np.float32))
    w1 = np.ascontiguousarray(np.asarray(L1_ffn_W1, np.float32).reshape(NE * D, HF))
    b1 = _b1_cols(np.asarray(L1_ffn_b1, np.float32))
    w2 = np.ascontiguousarray(np.asarray(L1_ffn_W2, np.float32).reshape(NE * HF, D))
    b2 = np.ascontiguousarray(np.asarray(L1_ffn_b2, np.float32))

    xs = [_feat_major(x[2 * c:2 * c + 2]) for c in range(ncore)]
    cores = list(range(ncore))

    nc_f = build_fused()
    res = run_bass_kernel_spmd(nc_f, [
        {"xin": xs[c], "watt": watt, "wproj": wproj, "ekt": ekt, "evr": evr,
         "alpha": alpha, "wgt": wgt, "bgt": bgt,
         "w1m": w1m, "b1m": b1m, "w2m": w2m, "b2m": b2m,
         "w1e": w1e, "b1e": b1e, "w2e": w2e, "b2e": b2e,
         "wqt": wqt, "wkt": wkt, "wvt": wvt, "wot": wot,
         "bq": bq, "bk": bk, "bv": bv, "bo": bo, "srcb": srcb, "tgtb": tgtb,
         "wg": wg, "bg": bg, "w1": w1, "b1": b1, "w2": w2, "b2": b2}
        for c in cores], core_ids=cores)
    x4s = [res.results[c]["x4"] for c in cores]

    out = np.concatenate(
        [x4s[c].reshape(NSEQ, D, T).transpose(0, 2, 1) for c in range(ncore)],
        axis=0)
    return np.ascontiguousarray(out).astype(np.float32)
